# version 03
# baseline (speedup 1.0000x reference)
import numpy as np
import ml_dtypes
from contextlib import ExitStack

import concourse.bass as bass
import concourse.mybir as mybir
from concourse.bass_utils import run_bass_kernel_spmd

F32 = mybir.dt.float32
BF16 = mybir.dt.bfloat16
I32 = mybir.dt.int32
AF = mybir.ActivationFunctionType
OP = mybir.AluOpType
AX = mybir.AxisListType
BF = ml_dtypes.bfloat16

T = 4096
D = 1024
HID = 1536
NCORES = 8
EPS = 1e-5
C_CONST = 8.0

SA = 512
NSA = T // SA
SB = 256
NSB = T // SB


class EM:
    ENGS = ("sync", "gpsimd", "tensor", "scalar", "vector")
    POOLS = {"sync": 4, "gpsimd": 4, "scalar": 2}

    def __init__(self, nc, es):
        self.nc = nc
        self.prog = {e: [] for e in self.ENGS}
        self.sem = {}
        self.cnt = {}
        self.max_wait = {}
        for name in ("t", "v", "s"):
            self._mk(es, name)
        self.pool = {}
        self.pool_idx = {}
        for q, n in self.POOLS.items():
            self.pool[q] = []
            self.pool_idx[q] = 0
            for i in range(n):
                nm = f"d_{q}_{i}"
                self._mk(es, nm)
                self.pool[q].append(nm)

    def _mk(self, es, name):
        self.sem[name] = es.enter_context(self.nc.semaphore(name))
        self.cnt[name] = 0
        self.max_wait[name] = 0

    def op(self, eng, fn, inc=None):
        if inc is not None:
            name, n = inc
            self.cnt[name] += n
            sem = self.sem[name]
            if eng in ("tensor", "scalar", "vector"):
                def thunk(e, fn=fn, sem=sem, n=n):
                    fn(e)
                    e.drain().then_inc(sem, n)
            else:
                def thunk(e, fn=fn, sem=sem, n=n):
                    fn(e).then_inc(sem, n)
        else:
            def thunk(e, fn=fn):
                fn(e)
        self.prog[eng].append(thunk)
        return (inc[0], self.cnt[inc[0]]) if inc else None

    def wait(self, eng, tgt):
        if tgt is None:
            return
        name, val = tgt
        if val <= 0:
            return
        self.max_wait[name] = max(self.max_wait[name], val)
        sem = self.sem[name]
        self.prog[eng].append(lambda e, sem=sem, v=val: e.wait_ge(sem, v))

    def dma(self, queue, fns):
        nm = self.pool[queue][self.pool_idx[queue]]
        self.pool_idx[queue] = (self.pool_idx[queue] + 1) % len(self.pool[queue])
        self.wait(queue, (nm, self.cnt[nm]))
        tgt = None
        for fn in fns:
            tgt = self.op(queue, fn, inc=(nm, 16))
        return tgt

    def seq(self, eng, fn):
        name = {"tensor": "t", "scalar": "s", "vector": "v"}[eng]
        tgt = self.op(eng, fn, inc=(name, 1))
        self.wait(eng, tgt)
        return tgt

    def mark(self, eng, name):
        self.cnt[name] += 1
        sem = self.sem[name]
        self.prog[eng].append(lambda e, sem=sem: e.drain().then_inc(sem, 1))
        return (name, self.cnt[name])

    def barrier(self):
        targets = [self.mark(eng, n) for eng, n in
                   (("tensor", "t"), ("scalar", "s"), ("vector", "v"))]
        for eng in self.ENGS:
            for tgt in targets:
                self.wait(eng, tgt)

    def drain_dma(self, eng="sync"):
        for q, names in self.pool.items():
            for nm in names:
                self.wait(eng, (nm, self.cnt[nm]))

    def check(self):
        for name, v in self.max_wait.items():
            assert v <= self.cnt[name], \
                f"wait on {name} for {v} but only {self.cnt[name]} emitted"


def _host_prep(inputs):
    g = {}
    x = np.ascontiguousarray(np.asarray(inputs["L0_attn_x"], dtype=np.float32))
    g["x0"] = [np.ascontiguousarray(x[i]) for i in range(NCORES)]

    tr = lambda a: np.ascontiguousarray(np.asarray(a, dtype=np.float32).T)
    fm = lambda a, n: np.ascontiguousarray(
        np.asarray(a, dtype=np.float32).reshape(n, 128).T)

    g["proj_wT"] = tr(inputs["L0_attn_proj_w"])
    cw = np.asarray(inputs["L0_attn_conv_w"], dtype=np.float32).reshape(HID, 4)
    g["conv_w"] = np.ascontiguousarray(
        cw.reshape(12, 128, 4).transpose(1, 0, 2).reshape(128, 48))
    g["conv_b"] = fm(inputs["L0_attn_conv_b"], 12)
    g["ip_wT"] = tr(inputs["L0_attn_ip_w"])
    g["gp_wT"] = tr(inputs["L0_attn_gp_w"])
    g["ip_b"] = fm(inputs["L0_attn_ip_b"], 12)
    g["gp_b"] = fm(inputs["L0_attn_gp_b"], 12)
    g["lam"] = fm(inputs["L0_attn_forget_lambda"], 12)
    g["out_wT"] = tr(inputs["L0_attn_out_w"])

    g["ffn_w"] = np.ascontiguousarray(np.asarray(inputs["L0_ffn_weight"], dtype=np.float32))
    g["ffn_b"] = np.ascontiguousarray(
        np.asarray(inputs["L0_ffn_bias"], dtype=np.float32).reshape(1, D))

    g["w_qkv"] = np.ascontiguousarray(np.asarray(inputs["L1_attn_w_qkv"], dtype=np.float32))
    g["w_proj"] = np.ascontiguousarray(np.asarray(inputs["L1_attn_w_proj"], dtype=np.float32))
    g["b_proj"] = np.ascontiguousarray(
        np.asarray(inputs["L1_attn_b_proj"], dtype=np.float32).reshape(1, D))

    g["router_wT"] = tr(inputs["L1_ffn_router_w"])
    g["router_bb"] = np.ascontiguousarray(np.broadcast_to(
        np.asarray(inputs["L1_ffn_bias"], dtype=np.float32).reshape(1, 8), (128, 8)))
    g["router_b"] = np.ascontiguousarray(
        np.asarray(inputs["L1_ffn_bias"], dtype=np.float32).reshape(1, 8))
    wg = np.asarray(inputs["L1_ffn_wg"], dtype=np.float32)
    wu = np.asarray(inputs["L1_ffn_wu"], dtype=np.float32)
    wd = np.asarray(inputs["L1_ffn_wd"], dtype=np.float32)
    g["wgT"] = np.ascontiguousarray(wg.transpose(0, 2, 1))
    g["wuT"] = np.ascontiguousarray(wu.transpose(0, 2, 1))
    g["wdT"] = np.ascontiguousarray(wd.transpose(0, 2, 1))
    g["sgT"] = tr(inputs["L1_ffn_sg"])
    g["suT"] = tr(inputs["L1_ffn_su"])
    g["sdT"] = tr(inputs["L1_ffn_sd"])

    g["ident"] = np.eye(128, dtype=BF)
    g["ones2"] = np.ones((2, 128), dtype=BF)
    k_idx = np.arange(128)[:, None]
    q_idx = np.arange(128)[None, :]
    g["trimask"] = np.ascontiguousarray(np.concatenate(
        [(k_idx >= q_idx).astype(BF), (k_idx <= q_idx).astype(BF)], axis=1))
    return g


INPUT_NAMES = ["proj_wT", "conv_w", "conv_b", "ip_wT", "gp_wT", "ip_b", "gp_b",
               "lam", "out_wT", "ffn_w", "ffn_b", "w_qkv", "w_proj", "b_proj",
               "router_wT", "router_b", "router_bb", "wgT", "wuT", "wdT", "sgT", "suT", "sdT",
               "ident", "ones2", "trimask"]


def build_program(stop_after="E", dbg=()):
    es = ExitStack()
    nc = bass.Bass("TRN2", target_bir_lowering=False, detect_race_conditions=False)
    em = EM(nc, es)

    def din(name, shape, dtype=F32):
        return nc.dram_tensor(name, list(shape), dtype, kind="ExternalInput").ap()

    def dscratch(name, shape, dtype=F32):
        kind = "ExternalOutput" if name in dbg else "Internal"
        return nc.dram_tensor(name, list(shape), dtype, kind=kind).ap()

    x0_h = din("x0", [T, D])
    proj_wT_h = din("proj_wT", [D, 3072])
    conv_w_h = din("conv_w", [128, 48])
    conv_b_h = din("conv_b", [128, 12])
    ip_wT_h = din("ip_wT", [HID, HID])
    gp_wT_h = din("gp_wT", [HID, HID])
    ip_b_h = din("ip_b", [128, 12])
    gp_b_h = din("gp_b", [128, 12])
    lam_h = din("lam", [128, 12])
    out_wT_h = din("out_wT", [HID, D])
    ffn_w_h = din("ffn_w", [D, D])
    ffn_b_h = din("ffn_b", [1, D])
    w_qkv_h = din("w_qkv", [D, 3072])
    w_proj_h = din("w_proj", [D, D])
    b_proj_h = din("b_proj", [1, D])
    router_wT_h = din("router_wT", [D, 8])
    router_b_h = din("router_b", [1, 8])
    router_bb_h = din("router_bb", [128, 8])
    wgT_h = din("wgT", [8, D, 512])
    wuT_h = din("wuT", [8, D, 512])
    wdT_h = din("wdT", [8, 512, D])
    sgT_h = din("sgT", [D, D])
    suT_h = din("suT", [D, D])
    sdT_h = din("sdT", [D, D])
    ident_h = din("ident", [128, 128], BF16)
    ones2_h = din("ones2", [2, 128], BF16)
    trimask_h = din("trimask", [128, 256], BF16)

    dbgZ_h = dscratch("dbgZ", [12, 128, T], BF16)
    dbgXD_h = dscratch("dbgXD", [T, D], BF16)
    gate_h = dscratch("gateS", [12, 128, T])
    xc_h = dscratch("xcS", [12, 128, T])
    x1_h = dscratch("x1S", [T, D])
    x2_h = dscratch("x2S", [T, D])
    x3_h = dscratch("x3S", [T, D])
    y_h = nc.dram_tensor("y", [T, D], F32, kind="ExternalOutput").ap()

    def sb(stack, name, shape, dt=F32):
        return stack.enter_context(nc.sbuf_tensor(name, shape, dt))

    def pst(stack, name, shape, dt=F32):
        return stack.enter_context(nc.psum_tensor(name, shape, dt))

    ident = sb(es, "identB", [128, 128], BF16)
    ones2 = sb(es, "ones2B", [2, 128], BF16)
    trimask = sb(es, "trimaskB", [128, 256], BF16)
    cst = sb(es, "cst", [128, 80])
    gpb = sb(es, "gpb", [128, 24])
    stats = sb(es, "stats", [128, 16])
    dump = sb(es, "dump", [128, D])
    kc = sb(es, "kc", [128, 4])

    d_init = em.dma("sync", [
        lambda e: e.dma_start(out=ident[:], in_=ident_h[:]),
        lambda e: e.dma_start(out=ones2[:], in_=ones2_h[:]),
        lambda e: e.dma_start(out=trimask[:], in_=trimask_h[:]),
        lambda e: e.dma_start(out=cst[:, 0:48], in_=conv_w_h[:]),
        lambda e: e.dma_start(out=cst[:, 48:60], in_=conv_b_h[:]),
        lambda e: e.dma_start(out=gpb[:, 0:12], in_=gp_b_h[:]),
        lambda e: e.dma_start(out=gpb[:, 12:24], in_=ip_b_h[:]),
    ])
    lamt = sb(es, "lamt", [128, 12])
    d_lam = em.dma("sync", [lambda e: e.dma_start(out=lamt[:], in_=lam_h[:])])

    em.op("vector", lambda e: nc.vector.memset(kc[:, 0:1], EPS))
    em.op("vector", lambda e: nc.vector.memset(kc[:, 1:2], 1.0 + 1e-6))
    em.op("vector", lambda e: nc.vector.memset(kc[:, 2:3], 1e-6), inc=("v", 1))

    em.wait("scalar", d_lam)
    ctmp = sb(es, "ctmp", [128, 12])
    s_c1 = em.seq("scalar", lambda e: nc.scalar.activation(ctmp[:], lamt[:], AF.Exp))
    em.wait("vector", s_c1)
    v_c1 = em.seq("vector", lambda e: nc.vector.tensor_scalar(
        ctmp[:], ctmp[:], 1.0, None, OP.add))
    em.wait("scalar", v_c1)
    s_c2 = em.op("scalar", lambda e: nc.scalar.activation(lamt[:], ctmp[:], AF.Ln),
                 inc=("s", 1))
    em.wait("vector", s_c2)
    v_c2 = em.op("vector", lambda e: nc.vector.tensor_scalar(
        cst[:, 60:72], lamt[:], -C_CONST, None, OP.mult), inc=("v", 1))
    em.wait("scalar", v_c2)
    em.wait("tensor", v_c2)
    em.wait("scalar", d_init)
    em.wait("vector", d_init)
    em.wait("tensor", d_init)

    def emit_ln(x_ap, out_bf_ap):
        sumx = stats[:, 0:1]; sumsq = stats[:, 1:2]; m = stats[:, 2:3]
        var = stats[:, 3:4]; r = stats[:, 4:5]; nmr = stats[:, 5:6]
        u = stats[:, 6:7]; mm = stats[:, 7:8]
        em.op("vector", lambda e: nc.vector.tensor_reduce(sumx, x_ap, AX.X, OP.add),
              inc=("v", 1))
        sv = em.op("scalar", lambda e: nc.scalar.activation(
            dump[:], x_ap, AF.Square, accum_out=sumsq), inc=("s", 1))
        em.wait("vector", sv)
        em.seq("vector", lambda e: nc.vector.tensor_scalar(m, sumx, 1.0 / D, None, OP.mult))
        em.seq("vector", lambda e: nc.vector.tensor_scalar(var, sumsq, 1.0 / D, None, OP.mult))
        em.seq("vector", lambda e: nc.vector.tensor_tensor(mm, m, m, OP.mult))
        vv = em.op("vector", lambda e: nc.vector.tensor_tensor(var, var, mm, OP.subtract),
                   inc=("v", 1))
        em.wait("scalar", vv)
        em.seq("scalar", lambda e: nc.scalar.activation(u, var, AF.Ln, bias=kc[:, 0:1]))
        sv2 = em.op("scalar", lambda e: nc.scalar.activation(r, u, AF.Exp, scale=-0.5),
                    inc=("s", 1))
        em.wait("vector", sv2)
        em.seq("vector", lambda e: nc.vector.scalar_tensor_tensor(
            nmr, m, -1.0, r, OP.mult, OP.mult))
        vap = em.op("vector", lambda e: nc.vector.tensor_scalar(
            dump[:], x_ap, r, nmr, OP.mult, OP.add), inc=("v", 1))
        em.wait("scalar", vap)
        return em.op("scalar", lambda e: nc.scalar.copy(out_bf_ap, dump[:]),
                     inc=("s", 1))

    esA = ExitStack()
    pw = sb(esA, "pwA", [128, 8 * 3072], BF16)
    dGw = em.dma("gpsimd", [
        (lambda dt: lambda e: e.dma_start(
            out=pw[:, dt * 3072:(dt + 1) * 3072],
            in_=proj_wT_h[dt * 128:(dt + 1) * 128, :]))(dt)
        for dt in range(8)])

    x0b = sb(esA, "x0bA", [128, 2 * 4 * D])
    x0v = x0b[:].rearrange("p (b j d) -> p b j d", b=2, j=4)
    xlnb = sb(esA, "xlnbA", [128, D], BF16)
    xlnT = sb(esA, "xlnTA", [128, 8 * SA], BF16)
    xlnTv = xlnT[:].rearrange("p (dt t) -> p dt t", dt=8)
    gateb = sb(esA, "gatebA", [128, 12 * SA])
    gatev = gateb[:].rearrange("p (h t) -> p h t", h=12)
    xhb = sb(esA, "xhbA", [128, 12 * (SA + 3)])
    xhv = xhb[:].rearrange("p (h t) -> p h t", h=12)
    xcb = sb(esA, "xcbA", [128, 12 * SA])
    xcv = xcb[:].rearrange("p (h t) -> p h t", h=12)


    esAp = ExitStack()
    ptr = pst(esAp, "ptrA", [128, 1024], BF16)
    pgx = [pst(esAp, f"pgxA{i}", [128, SA]) for i in range(4)]

    em.wait("tensor", dGw)

    v_ln_done = [None] * NSA
    d_out_done = [None] * NSA

    for i in range(NSA):
        buf = i % 2
        if i >= 2:
            em.wait("sync", v_ln_done[i - 2])
        d_in = em.dma("sync", [lambda e, i=i, buf=buf: e.dma_start(
            out=x0v[:, buf], in_=x0_h[i * SA:(i + 1) * SA, :].rearrange(
                "(j p) d -> p j d", p=128))])

        em.wait("vector", d_in)
        em.wait("scalar", d_in)
        s_evac = None
        for j in range(4):
            if not (i == 0 and j == 0):
                em.wait("scalar", ("t", em.cnt["t"]))
            s_apply = emit_ln(x0v[:, buf, j], xlnb[:])
            em.wait("tensor", s_apply)
            em.wait("tensor", s_evac)
            for dt in range(8):
                em.op("tensor", lambda e, dt=dt: nc.tensor.transpose(
                    ptr[:, dt * 128:(dt + 1) * 128],
                    xlnb[:, dt * 128:(dt + 1) * 128], ident[:]))
            tt = em.mark("tensor", "t")
            em.wait("scalar", tt)
            s_evac = em.op("scalar", lambda e, j=j: nc.scalar.copy(
                xlnTv[:, :, j * 128:(j + 1) * 128],
                ptr[:].rearrange("p (dt t) -> p dt t", dt=8)), inc=("s", 1))
        v_ln_done[i] = ("v", em.cnt["v"])
        s_xlnT_done = s_evac

        if i == 0:
            em.op("vector", lambda e: nc.vector.memset(xhv[:, :, 0:3], 0.0), inc=("v", 1))
        else:
            em.op("vector", lambda e: nc.vector.tensor_copy(
                xhv[:, :, 0:3], xhv[:, :, SA:SA + 3]), inc=("v", 1))

        em.wait("tensor", ("v", em.cnt["v"]))
        em.wait("tensor", s_xlnT_done)
        if i > 0:
            em.wait("scalar", d_out_done[i - 1])
            em.wait("vector", d_out_done[i - 1])
        s_base = em.cnt["s"]
        v_base = em.cnt["v"]
        for f in range(24):
            bank = pgx[f % 4]
            if f >= 4:
                fp = f - 4
                if fp < 12:
                    em.wait("tensor", ("s", s_base + fp + 1))
                else:
                    em.wait("tensor", ("v", v_base + (fp - 12) + 1))
            for dt in range(8):
                em.op("tensor", lambda e, f=f, dt=dt, bank=bank: nc.tensor.matmul(
                    bank[:], pw[:, dt * 3072 + f * 128: dt * 3072 + (f + 1) * 128],
                    xlnTv[:, dt], start=(dt == 0), stop=(dt == 7)))
            tt = em.mark("tensor", "t")
            if f < 12:
                em.wait("scalar", tt)
                em.op("scalar", lambda e, f=f, bank=bank: nc.scalar.activation(
                    gatev[:, f], bank[:], AF.Gelu), inc=("s", 1))
            else:
                em.wait("vector", tt)
                em.op("vector", lambda e, f=f, bank=bank: nc.vector.tensor_copy(
                    xhv[:, f - 12, 3:3 + SA], bank[:]), inc=("v", 1))
        s_gate_done = ("s", em.cnt["s"])

        for h in range(12):
            em.seq("vector", lambda e, h=h: nc.vector.tensor_scalar(
                xcv[:, h], xhv[:, h, 0:SA], cst[:, 4 * h:4 * h + 1],
                cst[:, 48 + h:49 + h], OP.mult, OP.add))
            for k in range(1, 4):
                em.seq("vector", lambda e, h=h, k=k: nc.vector.scalar_tensor_tensor(
                    xcv[:, h], xhv[:, h, k:k + SA], cst[:, 4 * h + k:4 * h + k + 1],
                    xcv[:, h], OP.mult, OP.add))
        v_conv_done = em.mark("vector", "v")

        em.wait("sync", s_gate_done)
        em.wait("sync", v_conv_done)
        fns = []
        for h in range(12):
            fns.append(lambda e, i=i, h=h: e.dma_start(
                out=gate_h[h, :, i * SA:(i + 1) * SA], in_=gatev[:, h]))
            fns.append(lambda e, i=i, h=h: e.dma_start(
                out=xc_h[h, :, i * SA:(i + 1) * SA], in_=xcv[:, h]))
        d_out_done[i] = em.dma("sync", fns)

    em.barrier()
    em.drain_dma()
    esAp.close()
    esA.close()

    if stop_after == "A":
        em.check()
        return nc, em, es

    esB = ExitStack()
    gw = sb(esB, "gwB", [128, 12 * HID], BF16)
    iw = sb(esB, "iwB", [128, 12 * HID], BF16)
    ow = sb(esB, "owB", [128, 12 * D], BF16)
    fns = []
    for ht in range(12):
        fns.append(lambda e, ht=ht: e.dma_start(
            out=gw[:, ht * HID:(ht + 1) * HID],
            in_=gp_wT_h[ht * 128:(ht + 1) * 128, :]))
        fns.append(lambda e, ht=ht: e.dma_start(
            out=iw[:, ht * HID:(ht + 1) * HID],
            in_=ip_wT_h[ht * 128:(ht + 1) * 128, :]))
        fns.append(lambda e, ht=ht: e.dma_start(
            out=ow[:, ht * D:(ht + 1) * D],
            in_=out_wT_h[ht * 128:(ht + 1) * 128, :]))
    dGw = em.dma("gpsimd", fns)

    xcf = sb(esB, "xcfB", [128, 12 * SB])
    xcfv = xcf[:].rearrange("p (h t) -> p h t", h=12)
    xcB = sb(esB, "xcBB", [128, 12 * SB], BF16)
    xcBv = xcB[:].rearrange("p (h t) -> p h t", h=12)
    gteb = sb(esB, "gtebB", [128, 12 * SB])
    gtev = gteb[:].rearrange("p (h t) -> p h t", h=12)
    x0c = sb(esB, "x0cB", [128, 2 * D])
    x0cv = x0c[:].rearrange("p (j d) -> p j d", j=2)
    recF = sb(esB, "recFB", [128, 12 * SB])
    recFv = recF[:].rearrange("p (g t) -> p g t", g=12)
    inpF = sb(esB, "inpFB", [128, 12 * SB])
    inpFv = inpF[:].rearrange("p (g t) -> p g t", g=12)
    alfb = sb(esB, "alfbB", [128, SB])
    t3b = sb(esB, "t3bB", [128, SB])
    t6b = sb(esB, "t6bB", [128, SB])
    zBb = sb(esB, "zBbB", [128, 12 * SB], BF16)
    zBv = zBb[:].rearrange("p (h t) -> p h t", h=12)
    x1b = sb(esB, "x1bB", [128, 2 * D])
    x1bv = x1b[:].rearrange("p (j d) -> p j d", j=2)
    carry = sb(esB, "carryB", [128, 12])
    em.op("vector", lambda e: nc.vector.memset(carry[:], 0.0), inc=("v", 1))

    esBp = ExitStack()
    prec = [pst(esBp, f"precB{i}", [128, 512]) for i in range(4)]
    pout = [pst(esBp, f"poutB{i}", [128, 512]) for i in range(4)]

    em.wait("tensor", dGw)

    v_stripe_done = None
    d_x1_out = None
    for i in range(NSB):
        if i >= 1:
            em.wait("sync", v_stripe_done)
        fns = []
        for h in range(12):
            fns.append(lambda e, i=i, h=h: e.dma_start(
                out=xcfv[:, h], in_=xc_h[h, :, i * SB:(i + 1) * SB]))
            fns.append(lambda e, i=i, h=h: e.dma_start(
                out=gtev[:, h], in_=gate_h[h, :, i * SB:(i + 1) * SB]))
        fns.append(lambda e, i=i: e.dma_start(
            out=x0cv[:],
            in_=x0_h[i * SB:(i + 1) * SB, :].rearrange("(j p) d -> p j d", p=128)))
        d_in = em.dma("sync", fns)

        em.wait("vector", d_in)
        v_xcB = em.op("vector", lambda e: nc.vector.tensor_copy(xcB[:], xcf[:]),
                      inc=("v", 1))

        em.wait("tensor", v_xcB)
        s_base = em.cnt["s"]
        for gt in range(12):
            for proj in range(2):
                bank = prec[(2 * gt + proj) % 4]
                if 2 * gt + proj >= 4:
                    em.wait("tensor", ("s", s_base + (2 * gt + proj) - 4 + 1))
                w = gw if proj == 0 else iw
                for ht in range(12):
                    em.op("tensor", lambda e, w=w, ht=ht, gt=gt, bank=bank:
                          nc.tensor.matmul(
                              bank[:, 0:SB],
                              w[:, ht * HID + gt * 128: ht * HID + (gt + 1) * 128],
                              xcBv[:, ht], start=(ht == 0), stop=(ht == 11)))
                tt = em.mark("tensor", "t")
                em.wait("scalar", tt)
                dst = recFv[:, gt] if proj == 0 else inpFv[:, gt]
                bias = gpb[:, gt:gt + 1] if proj == 0 else gpb[:, 12 + gt:13 + gt]
                em.op("scalar", lambda e, dst=dst, bank=bank, bias=bias:
                      nc.scalar.activation(dst, bank[:, 0:SB], AF.Sigmoid, bias=bias),
                      inc=("s", 1))
        s_sig_done = ("s", em.cnt["s"])

        em.wait("vector", s_sig_done)
        em.wait("scalar", s_sig_done)
        for gt in range(12):
            em.wait("scalar", ("v", em.cnt["v"]))
            sv = em.op("scalar", lambda e, gt=gt: nc.scalar.activation(
                alfb[:], recFv[:, gt], AF.Exp, scale=cst[:, 60 + gt:61 + gt]),
                inc=("s", 1))
            em.wait("vector", sv)
            vv = em.op("vector", lambda e: nc.vector.tensor_tensor(
                t3b[:], alfb[:], alfb[:], OP.mult), inc=("v", 1))
            em.seq("vector", lambda e: nc.vector.tensor_scalar(
                t3b[:], t3b[:], -1.0, 1.0, OP.mult, OP.add))
            em.wait("scalar", vv)
            em.wait("scalar", ("v", em.cnt["v"]))
            em.seq("scalar", lambda e: nc.scalar.activation(
                t6b[:], t3b[:], AF.Ln, scale=1.0, bias=kc[:, 2:3]))
            sv2 = em.op("scalar", lambda e: nc.scalar.activation(
                t3b[:], t6b[:], AF.Exp, scale=0.5), inc=("s", 1))
            em.wait("vector", sv2)
            em.seq("vector", lambda e, gt=gt: nc.vector.tensor_tensor(
                t6b[:], t3b[:], inpFv[:, gt], OP.mult))
            em.seq("vector", lambda e, gt=gt: nc.vector.tensor_tensor(
                t6b[:], t6b[:], xcfv[:, gt], OP.mult))
            em.seq("vector", lambda e, gt=gt: nc.vector.tensor_tensor_scan(
                t3b[:], alfb[:], t6b[:], carry[:, gt:gt + 1], OP.mult, OP.add))
            em.seq("vector", lambda e, gt=gt: nc.vector.tensor_copy(
                carry[:, gt:gt + 1], t3b[:, SB - 1:SB]))
            em.seq("vector", lambda e, gt=gt: nc.vector.tensor_tensor(
                t3b[:], t3b[:], gtev[:, gt], OP.mult))
            em.op("vector", lambda e, gt=gt: nc.vector.tensor_copy(
                zBv[:, gt], t3b[:]), inc=("v", 1))
        v_zB = ("v", em.cnt["v"])

        if "dbgZ" in dbg:
            em.wait("sync", ("v", em.cnt["v"]))
            em.dma("sync", [(lambda h: lambda e, i=i: e.dma_start(
                out=dbgZ_h[h, :, i * SB:(i + 1) * SB], in_=zBv[:, h]))(h)
                for h in range(12)])
        em.wait("tensor", v_zB)
        if i > 0:
            em.wait("tensor", ("v", em.cnt["v"]))
        for j in range(2):
            for ch in range(2):
                bank = pout[2 * j + ch]
                for ht in range(12):
                    em.op("tensor", lambda e, j=j, ch=ch, ht=ht, bank=bank:
                          nc.tensor.matmul(
                              bank[:],
                              zBv[:, ht, j * 128:(j + 1) * 128],
                              ow[:, ht * D + ch * 512: ht * D + (ch + 1) * 512],
                              start=(ht == 0), stop=(ht == 11)))
        tt = em.mark("tensor", "t")
        em.wait("vector", tt)
        if i > 0:
            em.wait("vector", d_x1_out)
        for j in range(2):
            for ch in range(2):
                em.op("vector", lambda e, j=j, ch=ch: nc.vector.tensor_tensor(
                    x1bv[:, j, ch * 512:(ch + 1) * 512], pout[2 * j + ch][:],
                    x0cv[:, j, ch * 512:(ch + 1) * 512], OP.add))
        v_x1 = em.mark("vector", "v")
        v_stripe_done = v_x1

        em.wait("sync", v_x1)
        d_x1_out = em.dma("sync", [lambda e, i=i: e.dma_start(
            out=x1_h[i * SB:(i + 1) * SB, :].rearrange("(j p) d -> p j d", p=128),
            in_=x1bv[:])])

    em.barrier()
    em.drain_dma()
    esBp.close()
    esB.close()

    if stop_after == "B":
        em.check()
        return nc, em, es

    SC = 512
    NSC = T // SC
    esC = ExitStack()
    wrow = sb(esC, "wrowC", [128, D])
    wqb = sb(esC, "wqbC", [128, D], BF16)
    wdT = sb(esC, "wdTC", [128, 8 * D], BF16)
    wdTv = wdT[:].rearrange("p (dt d) -> p dt d", dt=8)
    bhlC = sb(esC, "bhlC", [2, D], BF16)
    tmprow = sb(esC, "tmprowC", [1, D])
    x1c = sb(esC, "x1cC", [128, 4 * D])
    x1cv = x1c[:].rearrange("p (j d) -> p j d", j=4)
    xlnf = sb(esC, "xlnfC", [128, D])
    xqb = sb(esC, "xqbC", [128, D], BF16)
    xdT = sb(esC, "xdTC", [128, 8 * 128], BF16)
    xdTv = xdT[:].rearrange("p (dt t) -> p dt t", dt=8)
    x2b = sb(esC, "x2bC", [128, 4 * D])
    x2bv = x2b[:].rearrange("p (j d) -> p j d", j=4)
    qB1 = sb(esC, "qB1C", [128, D])
    qB2 = sb(esC, "qB2C", [128, D])
    qB3 = sb(esC, "qB3C", [128, D])
    qB4 = sb(esC, "qB4C", [128, D])
    qI = sb(esC, "qIC", [128, D], I32)
    qS = sb(esC, "qSC", [128, 96])

    esCp = ExitStack()
    ptrC = pst(esCp, "ptrC", [128, 1024], BF16)
    poutC = [pst(esCp, f"poutC{i}", [128, 512]) for i in range(4)]

    def emit_quantize(x_f32_ap, out_bf_ap):
        amax = qS[:, 0:32]; qsc = qS[:, 32:64]; qrs = qS[:, 64:96]
        xv = x_f32_ap.rearrange("p (a b) -> p a b", b=32)
        em.seq("vector", lambda e: nc.vector.tensor_reduce(
            amax, xv, AX.X, OP.max, apply_absolute_value=True))
        em.seq("vector", lambda e: nc.vector.tensor_scalar(
            qsc, amax, 1.0 / 7.5, 1e-30, OP.mult, OP.max))
        em.seq("vector", lambda e: nc.vector.reciprocal(qrs, qsc))
        em.seq("vector", lambda e: nc.vector.tensor_tensor(
            qB1[:].rearrange("p (a b) -> p a b", b=32), xv,
            qrs[:, :, None].broadcast_to([128, 32, 32]), OP.mult))
        def rhd(mult, dst):
            em.seq("vector", lambda e: nc.vector.tensor_scalar(
                qB4[:], qB1[:], float(mult), None, OP.mult))
            em.seq("vector", lambda e: nc.vector.tensor_copy(qI[:], qB4[:]))
            em.seq("vector", lambda e: nc.vector.tensor_copy(dst, qI[:]))
            em.seq("vector", lambda e: nc.vector.tensor_tensor(
                qB4[:], qB4[:], dst, OP.subtract))
            em.seq("vector", lambda e: nc.vector.scalar_tensor_tensor(
                dst, qB4[:], 0.5, dst, OP.is_gt, OP.add))
            em.seq("vector", lambda e: nc.vector.tensor_scalar(
                qB4[:], qB4[:], -0.5, None, OP.is_le))
            em.seq("vector", lambda e: nc.vector.tensor_tensor(
                dst, dst, qB4[:], OP.subtract))
            em.seq("vector", lambda e: nc.vector.tensor_scalar(
                dst, dst, 1.0 / mult, None, OP.mult))
        rhd(8.0, qB2[:])
        rhd(4.0, qB3[:])
        em.seq("vector", lambda e: nc.vector.tensor_tensor(qB4[:], qB1[:], qB1[:], OP.mult))
        em.seq("vector", lambda e: nc.vector.tensor_scalar(
            qB4[:], qB4[:], 4.0, None, OP.is_ge))
        em.seq("vector", lambda e: nc.vector.tensor_tensor(qB3[:], qB3[:], qB2[:], OP.subtract))
        em.seq("vector", lambda e: nc.vector.tensor_tensor(qB3[:], qB3[:], qB4[:], OP.mult))
        em.seq("vector", lambda e: nc.vector.tensor_tensor(qB2[:], qB2[:], qB3[:], OP.add))
        rhd(2.0, qB3[:])
        em.seq("vector", lambda e: nc.vector.tensor_tensor(qB4[:], qB1[:], qB1[:], OP.mult))
        em.seq("vector", lambda e: nc.vector.tensor_scalar(
            qB4[:], qB4[:], 16.0, None, OP.is_ge))
        em.seq("vector", lambda e: nc.vector.tensor_tensor(qB3[:], qB3[:], qB2[:], OP.subtract))
        em.seq("vector", lambda e: nc.vector.tensor_tensor(qB3[:], qB3[:], qB4[:], OP.mult))
        em.seq("vector", lambda e: nc.vector.tensor_tensor(qB2[:], qB2[:], qB3[:], OP.add))
        vq = em.op("vector", lambda e: nc.vector.tensor_tensor(
            qB1[:].rearrange("p (a b) -> p a b", b=32),
            qB2[:].rearrange("p (a b) -> p a b", b=32),
            qsc[:, :, None].broadcast_to([128, 32, 32]), OP.mult), inc=("v", 1))
        em.wait("scalar", vq)
        return em.op("scalar", lambda e: nc.scalar.copy(out_bf_ap, qB1[:]), inc=("s", 1))

    def emit_ln_f32(x_ap, out_f32_ap):
        sumx = stats[:, 0:1]; sumsq = stats[:, 1:2]; m = stats[:, 2:3]
        var = stats[:, 3:4]; r = stats[:, 4:5]; nmr = stats[:, 5:6]
        u = stats[:, 6:7]; mm = stats[:, 7:8]
        em.op("vector", lambda e: nc.vector.tensor_reduce(sumx, x_ap, AX.X, OP.add),
              inc=("v", 1))
        sv = em.op("scalar", lambda e: nc.scalar.activation(
            dump[:], x_ap, AF.Square, accum_out=sumsq), inc=("s", 1))
        em.wait("vector", sv)
        em.seq("vector", lambda e: nc.vector.tensor_scalar(m, sumx, 1.0 / D, None, OP.mult))
        em.seq("vector", lambda e: nc.vector.tensor_scalar(var, sumsq, 1.0 / D, None, OP.mult))
        em.seq("vector", lambda e: nc.vector.tensor_tensor(mm, m, m, OP.mult))
        vv = em.op("vector", lambda e: nc.vector.tensor_tensor(var, var, mm, OP.subtract),
                   inc=("v", 1))
        em.wait("scalar", vv)
        em.seq("scalar", lambda e: nc.scalar.activation(u, var, AF.Ln, bias=kc[:, 0:1]))
        sv2 = em.op("scalar", lambda e: nc.scalar.activation(r, u, AF.Exp, scale=-0.5),
                    inc=("s", 1))
        em.wait("vector", sv2)
        em.seq("vector", lambda e: nc.vector.scalar_tensor_tensor(
            nmr, m, -1.0, r, OP.mult, OP.mult))
        return em.op("vector", lambda e: nc.vector.tensor_scalar(
            out_f32_ap, x_ap, r, nmr, OP.mult, OP.add), inc=("v", 1))

    lorow = sb(esC, "lorowC", [1, D], BF16)

    def emit_bias_hilo(src_h, dst_bhl):
        d_b = em.dma("sync", [lambda e: e.dma_start(out=tmprow[:], in_=src_h[:])])
        em.wait("scalar", d_b)
        sh = em.op("scalar", lambda e: nc.scalar.copy(dst_bhl[0:1, :], tmprow[:]),
                   inc=("s", 1))
        em.wait("vector", sh)
        em.seq("vector", lambda e: nc.vector.tensor_copy(dump[0:1, :], dst_bhl[0:1, :]))
        vlo = em.op("vector", lambda e: nc.vector.tensor_tensor(
            lorow[:], tmprow[:], dump[0:1, :], OP.subtract), inc=("v", 1))
        em.wait("sync", vlo)
        return em.dma("sync", [lambda e: e.dma_start(out=dst_bhl[1:2, :], in_=lorow[:])])

    v_bC = emit_bias_hilo(ffn_b_h, bhlC)

    s_wq_evac = None
    s_q = None
    for rtile in range(8):
        em.wait("sync", s_q)
        d_w = em.dma("sync", [lambda e, rtile=rtile: e.dma_start(
            out=wrow[:], in_=ffn_w_h[rtile * 128:(rtile + 1) * 128, :])])
        em.wait("vector", d_w)
        s_q = emit_quantize(wrow[:], wqb[:])
        em.wait("tensor", s_q)
        em.wait("tensor", s_wq_evac)
        for dt in range(8):
            em.op("tensor", lambda e, dt=dt: nc.tensor.transpose(
                ptrC[:, dt * 128:(dt + 1) * 128],
                wqb[:, dt * 128:(dt + 1) * 128], ident[:]))
        tt = em.mark("tensor", "t")
        em.wait("scalar", tt)
        s_wq_evac = em.op("scalar", lambda e, rtile=rtile: nc.scalar.copy(
            wdTv[:, :, rtile * 128:(rtile + 1) * 128],
            ptrC[:].rearrange("p (dt t) -> p dt t", dt=8)), inc=("s", 1))

    em.wait("tensor", v_bC)
    em.wait("tensor", s_wq_evac)
    v_ln_doneC = None
    d_x2_out = None
    s_xdT = None
    for i in range(NSC):
        if i >= 1:
            em.wait("sync", v_ln_doneC)
        d_in = em.dma("sync", [lambda e, i=i: e.dma_start(
            out=x1cv[:], in_=x1_h[i * SC:(i + 1) * SC, :].rearrange(
                "(j p) d -> p j d", p=128))])
        em.wait("vector", d_in)
        em.wait("scalar", d_in)
        for j in range(4):
            emit_ln_f32(x1cv[:, j], xlnf[:])
            s_q = emit_quantize(xlnf[:], xqb[:])
            if "dbgXD" in dbg:
                em.wait("sync", s_q)
                em.dma("sync", [lambda e, i=i, j=j: e.dma_start(
                    out=dbgXD_h[i * SC + j * 128: i * SC + (j + 1) * 128, :],
                    in_=xqb[:])])
            em.wait("tensor", s_q)
            em.wait("tensor", s_xdT)
            for dt in range(8):
                em.op("tensor", lambda e, dt=dt: nc.tensor.transpose(
                    ptrC[:, dt * 128:(dt + 1) * 128],
                    xqb[:, dt * 128:(dt + 1) * 128], ident[:]))
            tt = em.mark("tensor", "t")
            em.wait("scalar", tt)
            s_xdT = em.op("scalar", lambda e: nc.scalar.copy(
                xdTv[:], ptrC[:].rearrange("p (dt t) -> p dt t", dt=8)), inc=("s", 1))
            em.wait("tensor", s_xdT)
            if i > 0 or j > 0:
                em.wait("tensor", ("v", em.cnt["v"]))
            for ch in range(2):
                bank = poutC[2 * (j % 2) + ch]
                for dt in range(8):
                    em.op("tensor", lambda e, dt=dt, ch=ch, bank=bank: nc.tensor.matmul(
                        bank[:], xdTv[:, dt],
                        wdTv[:, dt, ch * 512:(ch + 1) * 512],
                        start=(dt == 0), stop=False))
                em.op("tensor", lambda e, ch=ch, bank=bank: nc.tensor.matmul(
                    bank[:], ones2[:], bhlC[:, ch * 512:(ch + 1) * 512],
                    start=False, stop=True))
            tt2 = em.mark("tensor", "t")
            em.wait("vector", tt2)
            if i > 0 and j == 0:
                em.wait("vector", d_x2_out)
            for ch in range(2):
                em.op("vector", lambda e, j=j, ch=ch: nc.vector.tensor_tensor(
                    x2bv[:, j, ch * 512:(ch + 1) * 512], poutC[2 * (j % 2) + ch][:],
                    x1cv[:, j, ch * 512:(ch + 1) * 512], OP.add),
                    inc=("v", 1) if ch == 1 else None)
        v_ln_doneC = ("v", em.cnt["v"])
        em.wait("sync", v_ln_doneC)
        d_x2_out = em.dma("sync", [lambda e, i=i: e.dma_start(
            out=x2_h[i * SC:(i + 1) * SC, :].rearrange("(j p) d -> p j d", p=128),
            in_=x2bv[:])])

    em.barrier()
    em.drain_dma()
    esCp.close()
    esC.close()

    if stop_after == "C":
        em.check()
        return nc, em, es

    CD = 512
    NCD = T // CD
    esD = ExitStack()
    wqkv = sb(esD, "wqkvD", [128, 8 * 3072], BF16)
    wqkvv = wqkv[:].rearrange("p (dt f) -> p dt f", dt=8)
    wpro = sb(esD, "wproD", [128, 8 * D], BF16)
    wprov = wpro[:].rearrange("p (dt d) -> p dt d", dt=8)
    bhlD = sb(esD, "bhlD", [2, D], BF16)
    tmprowD = sb(esD, "tmprowD", [1, D])
    x2r = sb(esD, "x2rD", [128, 2 * 4 * D])
    x2rv = x2r[:].rearrange("p (c j d) -> p c j d", c=2, j=4)
    xlnbD = sb(esD, "xlnbD", [128, D], BF16)
    xlnTD = sb(esD, "xlnTD", [128, 8 * CD], BF16)
    xlnTDv = xlnTD[:].rearrange("p (dt t) -> p dt t", dt=8)
    qF = sb(esD, "qFD", [128, 8 * 1024], BF16)
    qFv = qF[:].rearrange("p (f t) -> p f t", f=8)
    kF = sb(esD, "kFD", [128, 8 * 1024], BF16)
    kFv = kF[:].rearrange("p (f t) -> p f t", f=8)
    vT = sb(esD, "vTD", [128, 8 * 16 * 65], BF16)
    vTv = vT[:].rearrange("p (r h c) -> p r h c", r=8, h=16)
    expb = sb(esD, "expbD", [128, 2 * 384], BF16)
    aob = sb(esD, "aobD", [128, D])
    aobb = sb(esD, "aobbD", [128, D], BF16)
    aoT = sb(esD, "aoTD", [128, 8 * 128], BF16)
    aoTv = aoT[:].rearrange("p (dt t) -> p dt t", dt=8)
    x3b = sb(esD, "x3bD", [128, D])
    rcol = sb(esD, "rcolD", [128, 1])

    esDp = ExitStack()
    ptrD = pst(esDp, "ptrD", [128, 1024], BF16)
    pqk = [pst(esDp, f"pqkD{i}", [128, 512]) for i in range(2)]
    psc = [pst(esDp, f"pscD{i}", [128, 384]) for i in range(2)]
    pav = [pst(esDp, f"pavD{i}", [128, 65]) for i in range(2)]
    ppr = pst(esDp, "pprD", [128, 512])

    fns = []
    for dt in range(8):
        fns.append(lambda e, dt=dt: e.dma_start(
            out=wqkvv[:, dt], in_=w_qkv_h[dt * 128:(dt + 1) * 128, :]))
        fns.append(lambda e, dt=dt: e.dma_start(
            out=wprov[:, dt], in_=w_proj_h[dt * 128:(dt + 1) * 128, :]))
    dGw = em.dma("gpsimd", fns)
    lorowD = sb(esD, "lorowD", [1, D], BF16)
    d_b = em.dma("sync", [lambda e: e.dma_start(out=tmprowD[:], in_=b_proj_h[:])])
    em.wait("scalar", d_b)
    sh = em.op("scalar", lambda e: nc.scalar.copy(bhlD[0:1, :], tmprowD[:]), inc=("s", 1))
    em.wait("vector", sh)
    em.seq("vector", lambda e: nc.vector.tensor_copy(dump[0:1, :], bhlD[0:1, :]))
    vloD = em.op("vector", lambda e: nc.vector.tensor_tensor(
        lorowD[:], tmprowD[:], dump[0:1, :], OP.subtract), inc=("v", 1))
    em.wait("sync", vloD)
    v_bD = em.dma("sync", [lambda e: e.dma_start(out=bhlD[1:2, :], in_=lorowD[:])])
    v_ones = em.op("vector", lambda e: nc.vector.memset(vTv[:, :, :, 64:65], 1.0),
                   inc=("v", 1))
    em.wait("tensor", dGw)
    em.wait("tensor", v_bD)
    em.wait("tensor", v_ones)

    s_qk_evac = [None] * 2
    s_exp = [None] * 2
    v_attn_done = None
    s_attn_done = None
    d_x3_out = None
    v_ln_doneD = None
    s_ao_evac = None
    s_aocast = None

    def emit_attn_block(n):
        nonlocal v_attn_done, s_attn_done, d_x3_out, s_ao_evac, s_aocast
        rn = n % 8
        em.wait("vector", s_aocast)
        em.wait("vector", d_x3_out)
        for h in range(16):
            pb = psc[h % 2]
            eb = expb[:, (h % 2) * 384:(h % 2 + 1) * 384]
            ebv = expb[:].rearrange("p (b k q) -> p b k q", b=2, k=3)[:, h % 2]
            em.wait("tensor", s_exp[h % 2])
            for kb in range(3):
                nk = n - 1 + kb
                if nk < 0 or nk > 31:
                    continue
                rk = nk % 8
                hp = (h % 2) * 64
                ft = h // 2
                em.op("tensor", lambda e, kb=kb, rk=rk, hp=hp, ft=ft, pb=pb:
                      nc.tensor.matmul(
                          pb[:, kb * 128:(kb + 1) * 128],
                          kFv[hp:hp + 64, ft, rk * 128:(rk + 1) * 128],
                          qFv[hp:hp + 64, ft, rn * 128:(rn + 1) * 128],
                          start=True, stop=True))
            tt = em.mark("tensor", "t")
            em.wait("scalar", tt)
            lo = 128 if n == 0 else 0
            hi = 256 if n == 31 else 384
            s_e = em.op("scalar", lambda e, pb=pb, eb=eb, lo=lo, hi=hi:
                        nc.scalar.activation(
                eb[:, lo:hi], pb[:, lo:hi], AF.Exp, scale=0.125), inc=("s", 1))
            em.wait("vector", s_e)
            triv = trimask[:].rearrange("p (b q) -> p b q", b=2)
            if n == 0:
                em.seq("vector", lambda e, ebv=ebv: nc.vector.memset(ebv[:, 0], 0.0))
            else:
                em.seq("vector", lambda e, ebv=ebv, triv=triv: nc.vector.tensor_tensor(
                    ebv[:, 0], ebv[:, 0], triv[:, 0], OP.mult))
            if n == 31:
                vm = em.op("vector", lambda e, ebv=ebv: nc.vector.memset(ebv[:, 2], 0.0),
                           inc=("v", 1))
            else:
                vm = em.op("vector", lambda e, ebv=ebv, triv=triv: nc.vector.tensor_tensor(
                    ebv[:, 2], ebv[:, 2], triv[:, 1], OP.mult), inc=("v", 1))
            s_exp[h % 2] = vm
            em.wait("tensor", vm)
            pa = pav[h % 2]
            em.wait("tensor", v_attn_done)
            kbs = [kb for kb in range(3) if 0 <= n - 1 + kb <= 31]
            for kb in kbs:
                rk = (n - 1 + kb) % 8
                em.op("tensor", lambda e, kb=kb, rk=rk, h=h, pa=pa, ebv=ebv:
                      nc.tensor.matmul(
                          pa[:], ebv[:, kb],
                          vTv[:, rk, h, :],
                          start=(kb == kbs[0]), stop=(kb == kbs[-1])))
            tta = em.mark("tensor", "t")
            em.wait("vector", tta)
            em.seq("vector", lambda e, pa=pa: nc.vector.reciprocal(rcol[:], pa[:, 64:65]))
            v_attn_done = em.op("vector", lambda e, pa=pa, h=h: nc.vector.tensor_scalar(
                aob[:, h * 64:(h + 1) * 64], pa[:, 0:64], rcol[:], None, OP.mult),
                inc=("v", 1))
        em.wait("scalar", v_attn_done)
        s_c = em.op("scalar", lambda e: nc.scalar.copy(aobb[:], aob[:]), inc=("s", 1))
        s_aocast = s_c
        em.wait("tensor", s_c)
        em.wait("tensor", s_ao_evac)
        for dt in range(8):
            em.op("tensor", lambda e, dt=dt: nc.tensor.transpose(
                ptrD[:, dt * 128:(dt + 1) * 128],
                aobb[:, dt * 128:(dt + 1) * 128], ident[:]))
        tt = em.mark("tensor", "t")
        em.wait("scalar", tt)
        s_ao_evac = em.op("scalar", lambda e: nc.scalar.copy(
            aoTv[:], ptrD[:].rearrange("p (dt t) -> p dt t", dt=8)), inc=("s", 1))
        em.wait("tensor", s_ao_evac)
        em.wait("tensor", s_attn_done)
        for ch in range(2):
            for dt in range(8):
                em.op("tensor", lambda e, dt=dt, ch=ch: nc.tensor.matmul(
                    ppr[:], aoTv[:, dt],
                    wprov[:, dt, ch * 512:(ch + 1) * 512],
                    start=(dt == 0), stop=False))
            em.op("tensor", lambda e, ch=ch: nc.tensor.matmul(
                ppr[:], ones2[:], bhlD[:, ch * 512:(ch + 1) * 512],
                start=False, stop=True))
            tt = em.mark("tensor", "t")
            em.wait("vector", tt)
            cslot = (n // 4) % 2
            v_r = em.op("vector", lambda e, n=n, ch=ch, cslot=cslot:
                        nc.vector.tensor_tensor(
                            x3b[:, ch * 512:(ch + 1) * 512], ppr[:],
                            x2rv[:, cslot, n % 4, ch * 512:(ch + 1) * 512], OP.add),
                        inc=("v", 1))
            s_attn_done = v_r
            em.wait("tensor", v_r)
        em.wait("sync", s_attn_done)
        d_x3_out = em.dma("sync", [lambda e, n=n: e.dma_start(
            out=x3_h[n * 128:(n + 1) * 128, :], in_=x3b[:])])

    for c in range(NCD):
        cslot = c % 2
        if c >= 2:
            em.wait("sync", s_attn_done)
        d_in = em.dma("sync", [lambda e, c=c, cslot=cslot: e.dma_start(
            out=x2rv[:, cslot], in_=x2_h[c * CD:(c + 1) * CD, :].rearrange(
                "(j p) d -> p j d", p=128))])
        em.wait("vector", d_in)
        em.wait("scalar", d_in)
        s_evac = None
        for j in range(4):
            if not (c == 0 and j == 0):
                em.wait("scalar", ("t", em.cnt["t"]))
            s_apply = emit_ln(x2rv[:, cslot, j], xlnbD[:])
            em.wait("tensor", s_apply)
            em.wait("tensor", s_evac)
            for dt in range(8):
                em.op("tensor", lambda e, dt=dt: nc.tensor.transpose(
                    ptrD[:, dt * 128:(dt + 1) * 128],
                    xlnbD[:, dt * 128:(dt + 1) * 128], ident[:]))
            tt = em.mark("tensor", "t")
            em.wait("scalar", tt)
            s_evac = em.op("scalar", lambda e, j=j: nc.scalar.copy(
                xlnTDv[:, :, j * 128:(j + 1) * 128],
                ptrD[:].rearrange("p (dt t) -> p dt t", dt=8)), inc=("s", 1))
        v_ln_doneD = ("v", em.cnt["v"])
        em.wait("tensor", s_evac)

        rbase = (4 * c) % 8
        sqk = [None, None]
        for f in range(16):
            bank = pqk[f % 2]
            em.wait("tensor", s_qk_evac[f % 2])
            for dt in range(8):
                em.op("tensor", lambda e, f=f, dt=dt, bank=bank: nc.tensor.matmul(
                    bank[:], wqkvv[:, dt, f * 128:(f + 1) * 128],
                    xlnTDv[:, dt], start=(dt == 0), stop=(dt == 7)))
            tt = em.mark("tensor", "t")
            em.wait("scalar", tt)
            dstv = qFv if f < 8 else kFv
            fi = f % 8
            s_qk_evac[f % 2] = em.op("scalar", lambda e, dstv=dstv, fi=fi, bank=bank,
                                     rbase=rbase: nc.scalar.copy(
                dstv[:, fi, rbase * 128:(rbase + 4) * 128], bank[:]), inc=("s", 1))
        for j in range(4):
            for ch in range(2):
                bank = pqk[ch]
                em.wait("tensor", s_qk_evac[ch])
                for dt in range(8):
                    em.op("tensor", lambda e, j=j, ch=ch, dt=dt, bank=bank:
                          nc.tensor.matmul(
                              bank[:], xlnTDv[:, dt, j * 128:(j + 1) * 128],
                              wqkvv[:, dt, 2048 + ch * 512:2048 + (ch + 1) * 512],
                              start=(dt == 0), stop=(dt == 7)))
                tt = em.mark("tensor", "t")
                em.wait("scalar", tt)
                s_qk_evac[ch] = em.op("scalar", lambda e, j=j, ch=ch, bank=bank,
                                      rbase=rbase: nc.scalar.copy(
                    vTv[:, rbase + j, ch * 8:(ch + 1) * 8, 0:64],
                    bank[:].rearrange("p (h dv) -> p h dv", h=8)), inc=("s", 1))
        em.wait("tensor", s_qk_evac[0])
        em.wait("tensor", s_qk_evac[1])

        if c == 0:
            blocks = [0, 1, 2]
        elif c == NCD - 1:
            blocks = [4 * c - 1, 4 * c, 4 * c + 1, 4 * c + 2, 4 * c + 3]
        else:
            blocks = [4 * c - 1, 4 * c, 4 * c + 1, 4 * c + 2]
        for n in blocks:
            emit_attn_block(n)

    em.barrier()
    em.drain_dma()
    esDp.close()
    esD.close()

    if stop_after == "D":
        em.check()
        return nc, em, es

    SE = 1024
    NSE = T // SE
    esE = ExitStack()
    x3e = sb(esE, "x3eE", [128, 8 * D])
    x3ev = x3e[:].rearrange("p (j d) -> p j d", j=8)
    xlnbE = sb(esE, "xlnbE", [128, D], BF16)
    xlnTE = sb(esE, "xlnTE", [128, 8 * SE], BF16)
    xlnTEv = xlnTE[:].rearrange("p (dt t) -> p dt t", dt=8)
    rwb = sb(esE, "rwbE", [128, 8 * 8], BF16)
    rwbv = rwb[:].rearrange("p (dt e) -> p dt e", dt=8)
    rbb = sb(esE, "rbbE", [128, 8])
    scb = sb(esE, "scbE", [128, 8])
    scpb = sb(esE, "scpbE", [128, 8])
    gsb = sb(esE, "gsbE", [128, 4])
    rtmp = sb(esE, "rtmpE", [128, 24])
    cwb = sb(esE, "cwbE", [128, 8 * 8])
    cwbv = cwb[:].rearrange("p (j e) -> p j e", j=8)
    wgE = [sb(esE, f"wgE{i}", [128, 8 * 512], BF16) for i in range(2)]
    wuE = [sb(esE, f"wuE{i}", [128, 8 * 512], BF16) for i in range(2)]
    wdE = [sb(esE, f"wdE{i}", [128, 4 * D], BF16) for i in range(2)]
    gb = sb(esE, "gbE", [128, 512])
    ub = sb(esE, "ubE", [128, 512])
    gub = sb(esE, "gubE", [128, 512], BF16)
    guT = [sb(esE, f"guTE{i}", [128, 4 * 128], BF16) for i in range(2)]
    acc = sb(esE, "accE", [128, 8 * D])
    accv = acc[:].rearrange("p (j d) -> p j d", j=8)

    esEp = ExitStack()
    ptrE = pst(esEp, "ptrE", [128, 512], BF16)
    pgE = pst(esEp, "pgE", [128, 512])
    puE = pst(esEp, "puE", [128, 512])
    prt = pst(esEp, "prtE", [128, 512])
    pwd = [pst(esEp, f"pwdE{i}", [128, 512]) for i in range(2)]

    d_rb = em.dma("sync", [lambda e: e.dma_start(out=rbb[:], in_=router_bb_h[:])])
    fns = []
    for dt in range(8):
        fns.append(lambda e, dt=dt: e.dma_start(
            out=rwbv[:, dt], in_=router_wT_h[dt * 128:(dt + 1) * 128, :]))
    d_rw = em.dma("gpsimd", fns)
    em.wait("tensor", d_rb)
    em.wait("vector", d_rb)
    em.wait("tensor", d_rw)
    em.wait("vector", d_rw)

    def wg_src(e_):
        if e_ < 8:
            return lambda dt, buf: (lambda e, dt=dt, buf=buf: e.dma_start(
                out=wgE[buf][:, dt * 512:(dt + 1) * 512],
                in_=wgT_h[e_, dt * 128:(dt + 1) * 128, :]))
        half = e_ - 8
        return lambda dt, buf: (lambda e, dt=dt, buf=buf, half=half: e.dma_start(
            out=wgE[buf][:, dt * 512:(dt + 1) * 512],
            in_=sgT_h[dt * 128:(dt + 1) * 128, half * 512:(half + 1) * 512]))

    def wu_src(e_):
        if e_ < 8:
            return lambda dt, buf: (lambda e, dt=dt, buf=buf: e.dma_start(
                out=wuE[buf][:, dt * 512:(dt + 1) * 512],
                in_=wuT_h[e_, dt * 128:(dt + 1) * 128, :]))
        half = e_ - 8
        return lambda dt, buf: (lambda e, dt=dt, buf=buf, half=half: e.dma_start(
            out=wuE[buf][:, dt * 512:(dt + 1) * 512],
            in_=suT_h[dt * 128:(dt + 1) * 128, half * 512:(half + 1) * 512]))

    def wd_src(e_):
        if e_ < 8:
            return lambda it, buf: (lambda e, it=it, buf=buf: e.dma_start(
                out=wdE[buf][:, it * D:(it + 1) * D],
                in_=wdT_h[e_, it * 128:(it + 1) * 128, :]))
        half = e_ - 8
        return lambda it, buf: (lambda e, it=it, buf=buf, half=half: e.dma_start(
            out=wdE[buf][:, it * D:(it + 1) * D],
            in_=sdT_h[half * 512 + it * 128:half * 512 + (it + 1) * 128, :]))

    v_stripe_doneE = None
    d_y_out = None
    s_guT = [None, None]
    v_acc = None
    t_pair_done = None
    s_g_prev = None
    v_u_prev = None
    for i in range(NSE):
        if i >= 1:
            em.wait("sync", v_stripe_doneE)
        d_in = em.dma("sync", [lambda e, i=i: e.dma_start(
            out=x3ev[:], in_=x3_h[i * SE:(i + 1) * SE, :].rearrange(
                "(j p) d -> p j d", p=128))])
        em.wait("vector", d_in)
        em.wait("scalar", d_in)

        s_evac = None
        s_sc_prev = None
        for j in range(8):
            if not (i == 0 and j == 0):
                em.wait("scalar", ("t", em.cnt["t"]))
            s_apply = emit_ln(x3ev[:, j], xlnbE[:])
            em.wait("tensor", s_apply)
            em.wait("tensor", s_evac)
            for half in range(2):
                for q4 in range(4):
                    dt = half * 4 + q4
                    em.op("tensor", lambda e, dt=dt, q4=q4: nc.tensor.transpose(
                        ptrE[:, q4 * 128:(q4 + 1) * 128],
                        xlnbE[:, dt * 128:(dt + 1) * 128], ident[:]))
                tt = em.mark("tensor", "t")
                em.wait("scalar", tt)
                s_evac = em.op("scalar", lambda e, j=j, half=half: nc.scalar.copy(
                    xlnTEv[:, half * 4:half * 4 + 4, j * 128:(j + 1) * 128],
                    ptrE[:].rearrange("p (dt t) -> p dt t", dt=4)), inc=("s", 1))
                if half == 0:
                    em.wait("tensor", s_evac)

            em.wait("tensor", s_evac)
            em.wait("tensor", s_sc_prev)
            for dt in range(8):
                em.op("tensor", lambda e, j=j, dt=dt: nc.tensor.matmul(
                    prt[:, 0:8], xlnTEv[:, dt, j * 128:(j + 1) * 128],
                    rwbv[:, dt], start=(dt == 0), stop=(dt == 7)))
            tt = em.mark("tensor", "t")
            em.wait("scalar", tt)
            s_sc = em.op("scalar", lambda e: nc.scalar.activation(
                scb[:], prt[:, 0:8], AF.Sigmoid), inc=("s", 1))
            s_sc_prev = s_sc
            em.wait("vector", s_sc)
            em.seq("vector", lambda e: nc.vector.tensor_tensor(
                scpb[:], scb[:], rbb[:], OP.add))
            em.seq("vector", lambda e: nc.vector.tensor_tensor(
                gsb[:], scpb[:].rearrange("p (g two) -> p g two", two=2)[:, :, 0],
                scpb[:].rearrange("p (g two) -> p g two", two=2)[:, :, 1], OP.add))
            w01 = rtmp[:, 0:1]; w02 = rtmp[:, 1:2]; w03 = rtmp[:, 2:3]
            w12 = rtmp[:, 3:4]; w13 = rtmp[:, 4:5]; w23 = rtmp[:, 5:6]
            pairs = [(0, 1, w01), (0, 2, w02), (0, 3, w03),
                     (1, 2, w12), (1, 3, w13), (2, 3, w23)]
            for a_, b_, wv in pairs:
                em.seq("vector", lambda e, a_=a_, b_=b_, wv=wv: nc.vector.tensor_tensor(
                    wv, gsb[:, a_:a_ + 1], gsb[:, b_:b_ + 1], OP.is_ge))
            wins = rtmp[:, 8:12]
            em.seq("vector", lambda e: nc.vector.tensor_tensor(
                wins[:, 0:1], w01, w02, OP.add))
            em.seq("vector", lambda e: nc.vector.tensor_tensor(
                wins[:, 0:1], wins[:, 0:1], w03, OP.add))
            em.seq("vector", lambda e: nc.vector.scalar_tensor_tensor(
                wins[:, 1:2], w01, -1.0, w12, OP.mult, OP.add))
            em.seq("vector", lambda e: nc.vector.tensor_tensor(
                wins[:, 1:2], wins[:, 1:2], w13, OP.add))
            em.seq("vector", lambda e: nc.vector.tensor_scalar(
                wins[:, 1:2], wins[:, 1:2], 1.0, None, OP.add))
            em.seq("vector", lambda e: nc.vector.scalar_tensor_tensor(
                wins[:, 2:3], w02, -1.0, w23, OP.mult, OP.add))
            em.seq("vector", lambda e: nc.vector.scalar_tensor_tensor(
                wins[:, 2:3], w12, -1.0, wins[:, 2:3], OP.mult, OP.add))
            em.seq("vector", lambda e: nc.vector.tensor_scalar(
                wins[:, 2:3], wins[:, 2:3], 2.0, None, OP.add))
            em.seq("vector", lambda e: nc.vector.tensor_tensor(
                wins[:, 3:4], w03, w13, OP.add))
            em.seq("vector", lambda e: nc.vector.tensor_tensor(
                wins[:, 3:4], wins[:, 3:4], w23, OP.add))
            em.seq("vector", lambda e: nc.vector.tensor_scalar(
                wins[:, 3:4], wins[:, 3:4], -1.0, 3.0, OP.mult, OP.add))
            gmask = rtmp[:, 12:16]
            em.seq("vector", lambda e: nc.vector.tensor_scalar(
                gmask[:], wins[:], 2.0, None, OP.is_ge))
            smask = rtmp[:, 16:24]
            em.seq("vector", lambda e: nc.vector.tensor_copy(
                smask[:].rearrange("p (g two) -> p g two", two=2)[:, :, 0], gmask[:]))
            em.seq("vector", lambda e: nc.vector.tensor_copy(
                smask[:].rearrange("p (g two) -> p g two", two=2)[:, :, 1], gmask[:]))
            masked = rtmp[:, 0:8]
            em.seq("vector", lambda e: nc.vector.tensor_tensor(
                masked[:], scpb[:], smask[:], OP.mult))
            m1 = rtmp[:, 8:9]; m2 = rtmp[:, 9:10]; den = rtmp[:, 10:11]
            sel = cwbv[:, j]
            em.seq("vector", lambda e: nc.vector.tensor_reduce(m1, masked[:], AX.X, OP.max))
            em.seq("vector", lambda e, sel=sel: nc.vector.tensor_scalar(
                sel, masked[:], m1, None, OP.is_ge))
            em.seq("vector", lambda e, sel=sel: nc.vector.tensor_scalar(
                rtmp[:, 16:24], sel, m1, None, OP.mult))
            em.seq("vector", lambda e: nc.vector.tensor_tensor(
                masked[:], masked[:], rtmp[:, 16:24], OP.subtract))
            em.seq("vector", lambda e: nc.vector.tensor_reduce(m2, masked[:], AX.X, OP.max))
            em.seq("vector", lambda e: nc.vector.tensor_scalar(
                rtmp[:, 16:24], masked[:], m2, None, OP.is_ge))
            em.seq("vector", lambda e, sel=sel: nc.vector.tensor_tensor(
                sel, sel, rtmp[:, 16:24], OP.add))
            em.seq("vector", lambda e, sel=sel: nc.vector.tensor_tensor(
                sel, sel, scb[:], OP.mult))
            em.seq("vector", lambda e, sel=sel: nc.vector.tensor_reduce(den, sel, AX.X, OP.add))
            em.seq("vector", lambda e: nc.vector.tensor_scalar(
                den, den, 1e-20, None, OP.add))
            em.seq("vector", lambda e: nc.vector.reciprocal(den, den))
            em.op("vector", lambda e, sel=sel: nc.vector.tensor_scalar(
                sel, sel, den, None, OP.mult), inc=("v", 1))
        v_route_done = ("v", em.cnt["v"])
        s_lnT_done = s_evac
        if i > 0:
            em.wait("vector", d_y_out)

        for p_ in range(5):
            e0, e1 = 2 * p_, 2 * p_ + 1
            buf0, buf1 = 0, 1
            fns = []
            for dt in range(8):
                fns.append(wg_src(e0)(dt, buf0))
                fns.append(wu_src(e0)(dt, buf0))
                fns.append(wg_src(e1)(dt, buf1))
                fns.append(wu_src(e1)(dt, buf1))
            for it in range(4):
                fns.append(wd_src(e0)(it, buf0))
                fns.append(wd_src(e1)(it, buf1))
            if t_pair_done is not None:
                em.wait("gpsimd", t_pair_done)
            d_w = em.dma("gpsimd", fns)
            em.wait("tensor", d_w)
            em.wait("tensor", v_route_done)
            em.wait("tensor", s_lnT_done)
            for j in range(8):
                for ei, (ex, buf) in enumerate(((e0, buf0), (e1, buf1))):
                    em.wait("tensor", s_guT[ei])
                    em.wait("tensor", s_g_prev)
                    em.wait("tensor", v_u_prev)
                    for dt in range(8):
                        em.op("tensor", lambda e, j=j, dt=dt, buf=buf: nc.tensor.matmul(
                            pgE[:], xlnTEv[:, dt, j * 128:(j + 1) * 128],
                            wgE[buf][:, dt * 512:(dt + 1) * 512],
                            start=(dt == 0), stop=(dt == 7)))
                    ttg = em.mark("tensor", "t")
                    for dt in range(8):
                        em.op("tensor", lambda e, j=j, dt=dt, buf=buf: nc.tensor.matmul(
                            puE[:], xlnTEv[:, dt, j * 128:(j + 1) * 128],
                            wuE[buf][:, dt * 512:(dt + 1) * 512],
                            start=(dt == 0), stop=(dt == 7)))
                    ttu = em.mark("tensor", "t")
                    em.wait("scalar", ttg)
                    s_g = em.op("scalar", lambda e: nc.scalar.activation(
                        gb[:], pgE[:], AF.Silu), inc=("s", 1))
                    s_g_prev = s_g
                    em.wait("vector", ttu)
                    em.wait("vector", s_g)
                    v_u_prev = em.seq("vector", lambda e: nc.vector.tensor_copy(ub[:], puE[:]))
                    vgu = em.op("vector", lambda e: nc.vector.tensor_tensor(
                        ub[:], ub[:], gb[:], OP.mult), inc=("v", 1))
                    if ex < 8:
                        vgu = em.op("vector", lambda e, j=j, ex=ex: nc.vector.tensor_scalar(
                            ub[:], ub[:], cwbv[:, j, ex:ex + 1], None, OP.mult),
                            inc=("v", 1))
                    em.wait("scalar", vgu)
                    s_gub = em.op("scalar", lambda e: nc.scalar.copy(gub[:], ub[:]),
                                  inc=("s", 1))
                    em.wait("tensor", s_gub)
                    for it in range(4):
                        em.op("tensor", lambda e, it=it: nc.tensor.transpose(
                            ptrE[:, it * 128:(it + 1) * 128],
                            gub[:, it * 128:(it + 1) * 128], ident[:]))
                    ttt = em.mark("tensor", "t")
                    em.wait("scalar", ttt)
                    s_guT[ei] = em.op("scalar", lambda e, ei=ei: nc.scalar.copy(
                        guT[ei][:], ptrE[:, 0:512]), inc=("s", 1))
                em.wait("tensor", s_guT[0])
                em.wait("tensor", s_guT[1])
                em.wait("tensor", v_acc)
                for ch in range(2):
                    bank = pwd[ch]
                    first = True
                    for ei, buf in ((0, buf0), (1, buf1)):
                        for it in range(4):
                            em.op("tensor", lambda e, ei=ei, it=it, ch=ch, buf=buf,
                                  bank=bank, first=first: nc.tensor.matmul(
                                bank[:], guT[ei][:, it * 128:(it + 1) * 128],
                                wdE[buf][:, it * D + ch * 512: it * D + (ch + 1) * 512],
                                start=first, stop=(ei == 1 and it == 3)))
                            first = False
                ttw = em.mark("tensor", "t")
                em.wait("vector", ttw)
                for ch in range(2):
                    if p_ == 0:
                        em.op("vector", lambda e, j=j, ch=ch: nc.vector.tensor_copy(
                            accv[:, j, ch * 512:(ch + 1) * 512], pwd[ch][:]),
                            inc=("v", 1) if ch == 1 else None)
                    else:
                        em.op("vector", lambda e, j=j, ch=ch: nc.vector.tensor_tensor(
                            accv[:, j, ch * 512:(ch + 1) * 512], pwd[ch][:],
                            accv[:, j, ch * 512:(ch + 1) * 512], OP.add),
                            inc=("v", 1) if ch == 1 else None)
                v_acc = ("v", em.cnt["v"])
            t_pair_done = ("t", em.cnt["t"])
        for j in range(8):
            em.seq("vector", lambda e, j=j: nc.vector.tensor_tensor(
                accv[:, j], accv[:, j], x3ev[:, j], OP.add))
        v_stripe_doneE = em.mark("vector", "v")
        em.wait("sync", v_stripe_doneE)
        d_y_out = em.dma("sync", [lambda e, i=i: e.dma_start(
            out=y_h[i * SE:(i + 1) * SE, :].rearrange("(j p) d -> p j d", p=128),
            in_=accv[:])])

    em.barrier()
    em.drain_dma()
    esEp.close()
    esE.close()

    em.check()
    return nc, em, es


def _finish(nc, em, es):
    for h in em.sem.values():
        nc.gpsimd.sem_clear(h)
    nc.all_engine_barrier()
    with nc.Block() as block:
        @block.sync
        def _(sync):
            for fn in em.prog["sync"]:
                fn(sync)

        @block.gpsimd
        def _(g):
            for fn in em.prog["gpsimd"]:
                fn(g)

        @block.tensor
        def _(te):
            for fn in em.prog["tensor"]:
                fn(te)

        @block.scalar
        def _(sc):
            for fn in em.prog["scalar"]:
                fn(sc)

        @block.vector
        def _(v):
            for fn in em.prog["vector"]:
                fn(v)
    es.close()
    return nc


def _run(inputs, stop_after="E", dbg=(), cores=None):
    g = _host_prep(inputs)
    nc, em, es = build_program(stop_after=stop_after, dbg=dbg)
    nc = _finish(nc, em, es)
    cores = cores if cores is not None else list(range(NCORES))
    in_maps = []
    for c in cores:
        m = {"x0": g["x0"][c]}
        for k in INPUT_NAMES:
            m[k] = g[k]
        in_maps.append(m)
    res = run_bass_kernel_spmd(nc, in_maps, core_ids=list(range(len(cores))))
    return res


def kernel(**inputs):
    res = _run(inputs, stop_after="E")
    return np.stack([np.asarray(r["y"], dtype=np.float32) for r in res.results])


# version 04
# speedup vs baseline: 1.0465x; 1.0465x over previous
import numpy as np
import ml_dtypes
from contextlib import ExitStack

import concourse.bass as bass
import concourse.mybir as mybir
from concourse.bass_utils import run_bass_kernel_spmd

F32 = mybir.dt.float32
BF16 = mybir.dt.bfloat16
I32 = mybir.dt.int32
AF = mybir.ActivationFunctionType
OP = mybir.AluOpType
AX = mybir.AxisListType
BF = ml_dtypes.bfloat16

T = 4096
D = 1024
HID = 1536
NCORES = 8
EPS = 1e-5
C_CONST = 8.0

SA = 512
NSA = T // SA
SB = 256
NSB = T // SB


class EM:
    ENGS = ("sync", "gpsimd", "tensor", "scalar", "vector")
    POOLS = {"sync": 4, "gpsimd": 4, "scalar": 2}

    def __init__(self, nc, es):
        self.nc = nc
        self.prog = {e: [] for e in self.ENGS}
        self.sem = {}
        self.cnt = {}
        self.max_wait = {}
        for name in ("t", "v", "s"):
            self._mk(es, name)
        self.pool = {}
        self.pool_idx = {}
        for q, n in self.POOLS.items():
            self.pool[q] = []
            self.pool_idx[q] = 0
            for i in range(n):
                nm = f"d_{q}_{i}"
                self._mk(es, nm)
                self.pool[q].append(nm)

    def _mk(self, es, name):
        self.sem[name] = es.enter_context(self.nc.semaphore(name))
        self.cnt[name] = 0
        self.max_wait[name] = 0

    def op(self, eng, fn, inc=None):
        if inc is not None:
            name, n = inc
            self.cnt[name] += n
            sem = self.sem[name]
            if eng in ("tensor", "scalar", "vector"):
                def thunk(e, fn=fn, sem=sem, n=n):
                    fn(e)
                    e.drain().then_inc(sem, n)
            else:
                def thunk(e, fn=fn, sem=sem, n=n):
                    fn(e).then_inc(sem, n)
        else:
            def thunk(e, fn=fn):
                fn(e)
        self.prog[eng].append(thunk)
        return (inc[0], self.cnt[inc[0]]) if inc else None

    def wait(self, eng, tgt):
        if tgt is None:
            return
        name, val = tgt
        if val <= 0:
            return
        self.max_wait[name] = max(self.max_wait[name], val)
        sem = self.sem[name]
        self.prog[eng].append(lambda e, sem=sem, v=val: e.wait_ge(sem, v))

    def dma(self, queue, fns):
        nm = self.pool[queue][self.pool_idx[queue]]
        self.pool_idx[queue] = (self.pool_idx[queue] + 1) % len(self.pool[queue])
        self.wait(queue, (nm, self.cnt[nm]))
        tgt = None
        for fn in fns:
            tgt = self.op(queue, fn, inc=(nm, 16))
        return tgt

    def seq(self, eng, fn):
        name = {"tensor": "t", "scalar": "s", "vector": "v"}[eng]
        tgt = self.op(eng, fn, inc=(name, 1))
        self.wait(eng, tgt)
        return tgt

    def mark(self, eng, name):
        self.cnt[name] += 1
        sem = self.sem[name]
        self.prog[eng].append(lambda e, sem=sem: e.drain().then_inc(sem, 1))
        return (name, self.cnt[name])

    def phase(self, name):
        if not hasattr(self, "phase_marks"):
            self.phase_marks = []
        self.phase_marks.append((name, {e: len(self.prog[e]) for e in self.ENGS}))

    def barrier(self):
        targets = [self.mark(eng, n) for eng, n in
                   (("tensor", "t"), ("scalar", "s"), ("vector", "v"))]
        for eng in self.ENGS:
            for tgt in targets:
                self.wait(eng, tgt)

    def drain_dma(self, eng="sync"):
        for q, names in self.pool.items():
            for nm in names:
                self.wait(eng, (nm, self.cnt[nm]))

    def check(self):
        for name, v in self.max_wait.items():
            assert v <= self.cnt[name], \
                f"wait on {name} for {v} but only {self.cnt[name]} emitted"


def _host_prep(inputs):
    g = {}
    x = np.ascontiguousarray(np.asarray(inputs["L0_attn_x"], dtype=np.float32))
    g["x0"] = [np.ascontiguousarray(x[i]) for i in range(NCORES)]

    tr = lambda a: np.ascontiguousarray(np.asarray(a, dtype=np.float32).T)
    fm = lambda a, n: np.ascontiguousarray(
        np.asarray(a, dtype=np.float32).reshape(n, 128).T)

    g["proj_wT"] = tr(inputs["L0_attn_proj_w"])
    cw = np.asarray(inputs["L0_attn_conv_w"], dtype=np.float32).reshape(HID, 4)
    g["conv_w"] = np.ascontiguousarray(
        cw.reshape(12, 128, 4).transpose(1, 0, 2).reshape(128, 48))
    g["conv_b"] = fm(inputs["L0_attn_conv_b"], 12)
    g["ip_wT"] = tr(inputs["L0_attn_ip_w"])
    g["gp_wT"] = tr(inputs["L0_attn_gp_w"])
    g["ip_b"] = fm(inputs["L0_attn_ip_b"], 12)
    g["gp_b"] = fm(inputs["L0_attn_gp_b"], 12)
    g["lam"] = fm(inputs["L0_attn_forget_lambda"], 12)
    g["out_wT"] = tr(inputs["L0_attn_out_w"])

    g["ffn_w"] = np.ascontiguousarray(np.asarray(inputs["L0_ffn_weight"], dtype=np.float32))
    g["ffn_b"] = np.ascontiguousarray(
        np.asarray(inputs["L0_ffn_bias"], dtype=np.float32).reshape(1, D))

    g["w_qkv"] = np.ascontiguousarray(np.asarray(inputs["L1_attn_w_qkv"], dtype=np.float32))
    g["w_proj"] = np.ascontiguousarray(np.asarray(inputs["L1_attn_w_proj"], dtype=np.float32))
    g["b_proj"] = np.ascontiguousarray(
        np.asarray(inputs["L1_attn_b_proj"], dtype=np.float32).reshape(1, D))

    g["router_wT"] = tr(inputs["L1_ffn_router_w"])
    g["router_bb"] = np.ascontiguousarray(np.broadcast_to(
        np.asarray(inputs["L1_ffn_bias"], dtype=np.float32).reshape(1, 8), (128, 8)))
    g["router_b"] = np.ascontiguousarray(
        np.asarray(inputs["L1_ffn_bias"], dtype=np.float32).reshape(1, 8))
    wg = np.asarray(inputs["L1_ffn_wg"], dtype=np.float32)
    wu = np.asarray(inputs["L1_ffn_wu"], dtype=np.float32)
    wd = np.asarray(inputs["L1_ffn_wd"], dtype=np.float32)
    g["wgT"] = np.ascontiguousarray(wg.transpose(0, 2, 1))
    g["wuT"] = np.ascontiguousarray(wu.transpose(0, 2, 1))
    g["wdT"] = np.ascontiguousarray(wd.transpose(0, 2, 1))
    g["sgT"] = tr(inputs["L1_ffn_sg"])
    g["suT"] = tr(inputs["L1_ffn_su"])
    g["sdT"] = tr(inputs["L1_ffn_sd"])

    g["ident"] = np.eye(128, dtype=BF)
    g["ones2"] = np.ones((2, 128), dtype=BF)
    k_idx = np.arange(128)[:, None]
    q_idx = np.arange(128)[None, :]
    g["trimask"] = np.ascontiguousarray(np.concatenate(
        [(k_idx >= q_idx).astype(BF), (k_idx <= q_idx).astype(BF)], axis=1))
    return g


INPUT_NAMES = ["proj_wT", "conv_w", "conv_b", "ip_wT", "gp_wT", "ip_b", "gp_b",
               "lam", "out_wT", "ffn_w", "ffn_b", "w_qkv", "w_proj", "b_proj",
               "router_wT", "router_b", "router_bb", "wgT", "wuT", "wdT", "sgT", "suT", "sdT",
               "ident", "ones2", "trimask"]


def build_program(stop_after="E", dbg=()):
    es = ExitStack()
    nc = bass.Bass("TRN2", target_bir_lowering=False, detect_race_conditions=False)
    em = EM(nc, es)

    def din(name, shape, dtype=F32):
        return nc.dram_tensor(name, list(shape), dtype, kind="ExternalInput").ap()

    def dscratch(name, shape, dtype=F32):
        kind = "ExternalOutput" if name in dbg else "Internal"
        return nc.dram_tensor(name, list(shape), dtype, kind=kind).ap()

    x0_h = din("x0", [T, D])
    proj_wT_h = din("proj_wT", [D, 3072])
    conv_w_h = din("conv_w", [128, 48])
    conv_b_h = din("conv_b", [128, 12])
    ip_wT_h = din("ip_wT", [HID, HID])
    gp_wT_h = din("gp_wT", [HID, HID])
    ip_b_h = din("ip_b", [128, 12])
    gp_b_h = din("gp_b", [128, 12])
    lam_h = din("lam", [128, 12])
    out_wT_h = din("out_wT", [HID, D])
    ffn_w_h = din("ffn_w", [D, D])
    ffn_b_h = din("ffn_b", [1, D])
    w_qkv_h = din("w_qkv", [D, 3072])
    w_proj_h = din("w_proj", [D, D])
    b_proj_h = din("b_proj", [1, D])
    router_wT_h = din("router_wT", [D, 8])
    router_b_h = din("router_b", [1, 8])
    router_bb_h = din("router_bb", [128, 8])
    wgT_h = din("wgT", [8, D, 512])
    wuT_h = din("wuT", [8, D, 512])
    wdT_h = din("wdT", [8, 512, D])
    sgT_h = din("sgT", [D, D])
    suT_h = din("suT", [D, D])
    sdT_h = din("sdT", [D, D])
    ident_h = din("ident", [128, 128], BF16)
    ones2_h = din("ones2", [2, 128], BF16)
    trimask_h = din("trimask", [128, 256], BF16)

    dbgZ_h = dscratch("dbgZ", [12, 128, T], BF16)
    dbgXD_h = dscratch("dbgXD", [T, D], BF16)
    gate_h = dscratch("gateS", [12, 128, T])
    xc_h = dscratch("xcS", [12, 128, T])
    x1_h = dscratch("x1S", [T, D])
    x2_h = dscratch("x2S", [T, D])
    x3_h = dscratch("x3S", [T, D])
    y_h = nc.dram_tensor("y", [T, D], F32, kind="ExternalOutput").ap()

    def sb(stack, name, shape, dt=F32):
        return stack.enter_context(nc.sbuf_tensor(name, shape, dt))

    def pst(stack, name, shape, dt=F32):
        return stack.enter_context(nc.psum_tensor(name, shape, dt))

    ident = sb(es, "identB", [128, 128], BF16)
    ones2 = sb(es, "ones2B", [2, 128], BF16)
    trimask = sb(es, "trimaskB", [128, 256], BF16)
    cst = sb(es, "cst", [128, 80])
    gpb = sb(es, "gpb", [128, 24])
    stats = sb(es, "stats", [128, 16])
    dump = sb(es, "dump", [128, D])
    kc = sb(es, "kc", [128, 4])

    d_init = em.dma("sync", [
        lambda e: e.dma_start(out=ident[:], in_=ident_h[:]),
        lambda e: e.dma_start(out=ones2[:], in_=ones2_h[:]),
        lambda e: e.dma_start(out=trimask[:], in_=trimask_h[:]),
        lambda e: e.dma_start(out=cst[:, 0:48], in_=conv_w_h[:]),
        lambda e: e.dma_start(out=cst[:, 48:60], in_=conv_b_h[:]),
        lambda e: e.dma_start(out=gpb[:, 0:12], in_=gp_b_h[:]),
        lambda e: e.dma_start(out=gpb[:, 12:24], in_=ip_b_h[:]),
    ])
    lamt = sb(es, "lamt", [128, 12])
    d_lam = em.dma("sync", [lambda e: e.dma_start(out=lamt[:], in_=lam_h[:])])

    em.op("vector", lambda e: nc.vector.memset(kc[:, 0:1], EPS))
    em.op("vector", lambda e: nc.vector.memset(kc[:, 1:2], 1.0 + 1e-6))
    em.op("vector", lambda e: nc.vector.memset(kc[:, 2:3], 1e-6), inc=("v", 1))

    em.wait("scalar", d_lam)
    ctmp = sb(es, "ctmp", [128, 12])
    s_c1 = em.seq("scalar", lambda e: nc.scalar.activation(ctmp[:], lamt[:], AF.Exp))
    em.wait("vector", s_c1)
    v_c1 = em.seq("vector", lambda e: nc.vector.tensor_scalar(
        ctmp[:], ctmp[:], 1.0, None, OP.add))
    em.wait("scalar", v_c1)
    s_c2 = em.op("scalar", lambda e: nc.scalar.activation(lamt[:], ctmp[:], AF.Ln),
                 inc=("s", 1))
    em.wait("vector", s_c2)
    v_c2 = em.op("vector", lambda e: nc.vector.tensor_scalar(
        cst[:, 60:72], lamt[:], -C_CONST, None, OP.mult), inc=("v", 1))
    em.wait("scalar", v_c2)
    em.wait("tensor", v_c2)
    em.wait("scalar", d_init)
    em.wait("vector", d_init)
    em.wait("tensor", d_init)

    def emit_ln(x_ap, out_bf_ap):
        sumx = stats[:, 0:1]; sumsq = stats[:, 1:2]; m = stats[:, 2:3]
        var = stats[:, 3:4]; r = stats[:, 4:5]; nmr = stats[:, 5:6]
        u = stats[:, 6:7]; mm = stats[:, 7:8]
        em.op("vector", lambda e: nc.vector.tensor_reduce(sumx, x_ap, AX.X, OP.add),
              inc=("v", 1))
        sv = em.op("scalar", lambda e: nc.scalar.activation(
            dump[:], x_ap, AF.Square, accum_out=sumsq), inc=("s", 1))
        em.wait("vector", sv)
        em.seq("vector", lambda e: nc.vector.tensor_scalar(m, sumx, 1.0 / D, None, OP.mult))
        em.seq("vector", lambda e: nc.vector.tensor_scalar(var, sumsq, 1.0 / D, None, OP.mult))
        em.seq("vector", lambda e: nc.vector.tensor_tensor(mm, m, m, OP.mult))
        vv = em.op("vector", lambda e: nc.vector.tensor_tensor(var, var, mm, OP.subtract),
                   inc=("v", 1))
        em.wait("scalar", vv)
        em.seq("scalar", lambda e: nc.scalar.activation(u, var, AF.Ln, bias=kc[:, 0:1]))
        sv2 = em.op("scalar", lambda e: nc.scalar.activation(r, u, AF.Exp, scale=-0.5),
                    inc=("s", 1))
        em.wait("vector", sv2)
        em.seq("vector", lambda e: nc.vector.scalar_tensor_tensor(
            nmr, m, -1.0, r, OP.mult, OP.mult))
        vap = em.op("vector", lambda e: nc.vector.tensor_scalar(
            dump[:], x_ap, r, nmr, OP.mult, OP.add), inc=("v", 1))
        em.wait("scalar", vap)
        return em.op("scalar", lambda e: nc.scalar.copy(out_bf_ap, dump[:]),
                     inc=("s", 1))

    em.phase("A")
    esA = ExitStack()
    pw = sb(esA, "pwA", [128, 8 * 3072], BF16)
    dGw = em.dma("gpsimd", [
        (lambda dt: lambda e: e.dma_start(
            out=pw[:, dt * 3072:(dt + 1) * 3072],
            in_=proj_wT_h[dt * 128:(dt + 1) * 128, :]))(dt)
        for dt in range(8)])

    x0b = sb(esA, "x0bA", [128, 2 * 4 * D])
    x0v = x0b[:].rearrange("p (b j d) -> p b j d", b=2, j=4)
    xlnb = sb(esA, "xlnbA", [128, D], BF16)
    xlnT = sb(esA, "xlnTA", [128, 8 * SA], BF16)
    xlnTv = xlnT[:].rearrange("p (dt t) -> p dt t", dt=8)
    gateb = sb(esA, "gatebA", [128, 12 * SA])
    gatev = gateb[:].rearrange("p (h t) -> p h t", h=12)
    xhb = sb(esA, "xhbA", [128, 12 * (SA + 3)])
    xhv = xhb[:].rearrange("p (h t) -> p h t", h=12)
    xcb = sb(esA, "xcbA", [128, 12 * SA])
    xcv = xcb[:].rearrange("p (h t) -> p h t", h=12)


    esAp = ExitStack()
    ptr = pst(esAp, "ptrA", [128, 1024], BF16)
    pgx = [pst(esAp, f"pgxA{i}", [128, SA]) for i in range(4)]

    em.wait("tensor", dGw)

    v_ln_done = [None] * NSA
    d_out_done = [None] * NSA

    for i in range(NSA):
        buf = i % 2
        if i >= 2:
            em.wait("sync", v_ln_done[i - 2])
        d_in = em.dma("sync", [lambda e, i=i, buf=buf: e.dma_start(
            out=x0v[:, buf], in_=x0_h[i * SA:(i + 1) * SA, :].rearrange(
                "(j p) d -> p j d", p=128))])

        em.wait("vector", d_in)
        em.wait("scalar", d_in)
        s_evac = None
        for j in range(4):
            if not (i == 0 and j == 0):
                em.wait("scalar", ("t", em.cnt["t"]))
            s_apply = emit_ln(x0v[:, buf, j], xlnb[:])
            em.wait("tensor", s_apply)
            em.wait("tensor", s_evac)
            for dt in range(8):
                em.op("tensor", lambda e, dt=dt: nc.tensor.transpose(
                    ptr[:, dt * 128:(dt + 1) * 128],
                    xlnb[:, dt * 128:(dt + 1) * 128], ident[:]))
            tt = em.mark("tensor", "t")
            em.wait("scalar", tt)
            s_evac = em.op("scalar", lambda e, j=j: nc.scalar.copy(
                xlnTv[:, :, j * 128:(j + 1) * 128],
                ptr[:].rearrange("p (dt t) -> p dt t", dt=8)), inc=("s", 1))
        v_ln_done[i] = ("v", em.cnt["v"])
        s_xlnT_done = s_evac

        if i == 0:
            em.op("vector", lambda e: nc.vector.memset(xhv[:, :, 0:3], 0.0), inc=("v", 1))
        else:
            em.op("vector", lambda e: nc.vector.tensor_copy(
                xhv[:, :, 0:3], xhv[:, :, SA:SA + 3]), inc=("v", 1))

        em.wait("tensor", ("v", em.cnt["v"]))
        em.wait("tensor", s_xlnT_done)
        if i > 0:
            em.wait("scalar", d_out_done[i - 1])
            em.wait("vector", d_out_done[i - 1])
        s_base = em.cnt["s"]
        v_base = em.cnt["v"]
        for f in range(24):
            bank = pgx[f % 4]
            if f >= 4:
                fp = f - 4
                if fp < 12:
                    em.wait("tensor", ("s", s_base + fp + 1))
                else:
                    em.wait("tensor", ("v", v_base + (fp - 12) + 1))
            for dt in range(8):
                em.op("tensor", lambda e, f=f, dt=dt, bank=bank: nc.tensor.matmul(
                    bank[:], pw[:, dt * 3072 + f * 128: dt * 3072 + (f + 1) * 128],
                    xlnTv[:, dt], start=(dt == 0), stop=(dt == 7)))
            tt = em.mark("tensor", "t")
            if f < 12:
                em.wait("scalar", tt)
                em.op("scalar", lambda e, f=f, bank=bank: nc.scalar.activation(
                    gatev[:, f], bank[:], AF.Gelu), inc=("s", 1))
            else:
                em.wait("vector", tt)
                em.op("vector", lambda e, f=f, bank=bank: nc.vector.tensor_copy(
                    xhv[:, f - 12, 3:3 + SA], bank[:]), inc=("v", 1))
        s_gate_done = ("s", em.cnt["s"])

        for h in range(12):
            em.seq("vector", lambda e, h=h: nc.vector.tensor_scalar(
                xcv[:, h], xhv[:, h, 0:SA], cst[:, 4 * h:4 * h + 1],
                cst[:, 48 + h:49 + h], OP.mult, OP.add))
            for k in range(1, 4):
                em.seq("vector", lambda e, h=h, k=k: nc.vector.scalar_tensor_tensor(
                    xcv[:, h], xhv[:, h, k:k + SA], cst[:, 4 * h + k:4 * h + k + 1],
                    xcv[:, h], OP.mult, OP.add))
        v_conv_done = em.mark("vector", "v")

        em.wait("sync", s_gate_done)
        em.wait("sync", v_conv_done)
        fns = []
        for h in range(12):
            fns.append(lambda e, i=i, h=h: e.dma_start(
                out=gate_h[h, :, i * SA:(i + 1) * SA], in_=gatev[:, h]))
            fns.append(lambda e, i=i, h=h: e.dma_start(
                out=xc_h[h, :, i * SA:(i + 1) * SA], in_=xcv[:, h]))
        d_out_done[i] = em.dma("sync", fns)

    em.barrier()
    em.drain_dma()
    esAp.close()
    esA.close()

    if stop_after == "A":
        em.check()
        return nc, em, es

    em.phase("B")
    esB = ExitStack()
    gw = sb(esB, "gwB", [128, 12 * HID], BF16)
    iw = sb(esB, "iwB", [128, 12 * HID], BF16)
    ow = sb(esB, "owB", [128, 12 * D], BF16)
    fns = []
    for ht in range(12):
        fns.append(lambda e, ht=ht: e.dma_start(
            out=gw[:, ht * HID:(ht + 1) * HID],
            in_=gp_wT_h[ht * 128:(ht + 1) * 128, :]))
        fns.append(lambda e, ht=ht: e.dma_start(
            out=iw[:, ht * HID:(ht + 1) * HID],
            in_=ip_wT_h[ht * 128:(ht + 1) * 128, :]))
        fns.append(lambda e, ht=ht: e.dma_start(
            out=ow[:, ht * D:(ht + 1) * D],
            in_=out_wT_h[ht * 128:(ht + 1) * 128, :]))
    dGw = em.dma("gpsimd", fns)

    xcf = sb(esB, "xcfB", [128, 12 * SB])
    xcfv = xcf[:].rearrange("p (h t) -> p h t", h=12)
    xcB = sb(esB, "xcBB", [128, 12 * SB], BF16)
    xcBv = xcB[:].rearrange("p (h t) -> p h t", h=12)
    gteb = sb(esB, "gtebB", [128, 12 * SB])
    gtev = gteb[:].rearrange("p (h t) -> p h t", h=12)
    x0c = sb(esB, "x0cB", [128, 2 * D])
    x0cv = x0c[:].rearrange("p (j d) -> p j d", j=2)
    recF = sb(esB, "recFB", [128, 12 * SB])
    recFv = recF[:].rearrange("p (g t) -> p g t", g=12)
    inpF = sb(esB, "inpFB", [128, 12 * SB])
    inpFv = inpF[:].rearrange("p (g t) -> p g t", g=12)
    alfb = sb(esB, "alfbB", [128, SB])
    t3b = sb(esB, "t3bB", [128, SB])
    t6b = sb(esB, "t6bB", [128, SB])
    zBb = sb(esB, "zBbB", [128, 12 * SB], BF16)
    zBv = zBb[:].rearrange("p (h t) -> p h t", h=12)
    x1b = sb(esB, "x1bB", [128, 2 * D])
    x1bv = x1b[:].rearrange("p (j d) -> p j d", j=2)
    carry = sb(esB, "carryB", [128, 12])
    em.op("vector", lambda e: nc.vector.memset(carry[:], 0.0), inc=("v", 1))

    esBp = ExitStack()
    prec = [pst(esBp, f"precB{i}", [128, 512]) for i in range(4)]
    pout = [pst(esBp, f"poutB{i}", [128, 512]) for i in range(4)]

    em.wait("tensor", dGw)

    v_stripe_done = None
    d_x1_out = None
    for i in range(NSB):
        if i >= 1:
            em.wait("sync", v_stripe_done)
        fns = []
        for h in range(12):
            fns.append(lambda e, i=i, h=h: e.dma_start(
                out=xcfv[:, h], in_=xc_h[h, :, i * SB:(i + 1) * SB]))
            fns.append(lambda e, i=i, h=h: e.dma_start(
                out=gtev[:, h], in_=gate_h[h, :, i * SB:(i + 1) * SB]))
        fns.append(lambda e, i=i: e.dma_start(
            out=x0cv[:],
            in_=x0_h[i * SB:(i + 1) * SB, :].rearrange("(j p) d -> p j d", p=128)))
        d_in = em.dma("sync", fns)

        em.wait("vector", d_in)
        v_xcB = em.op("vector", lambda e: nc.vector.tensor_copy(xcB[:], xcf[:]),
                      inc=("v", 1))

        em.wait("tensor", v_xcB)
        s_base = em.cnt["s"]
        for gt in range(12):
            for proj in range(2):
                bank = prec[(2 * gt + proj) % 4]
                if 2 * gt + proj >= 4:
                    em.wait("tensor", ("s", s_base + (2 * gt + proj) - 4 + 1))
                w = gw if proj == 0 else iw
                for ht in range(12):
                    em.op("tensor", lambda e, w=w, ht=ht, gt=gt, bank=bank:
                          nc.tensor.matmul(
                              bank[:, 0:SB],
                              w[:, ht * HID + gt * 128: ht * HID + (gt + 1) * 128],
                              xcBv[:, ht], start=(ht == 0), stop=(ht == 11)))
                tt = em.mark("tensor", "t")
                em.wait("scalar", tt)
                dst = recFv[:, gt] if proj == 0 else inpFv[:, gt]
                bias = gpb[:, gt:gt + 1] if proj == 0 else gpb[:, 12 + gt:13 + gt]
                em.op("scalar", lambda e, dst=dst, bank=bank, bias=bias:
                      nc.scalar.activation(dst, bank[:, 0:SB], AF.Sigmoid, bias=bias),
                      inc=("s", 1))
        s_sig_done = ("s", em.cnt["s"])

        em.wait("vector", s_sig_done)
        em.wait("scalar", s_sig_done)
        for gt in range(12):
            em.wait("scalar", ("v", em.cnt["v"]))
            sv = em.op("scalar", lambda e, gt=gt: nc.scalar.activation(
                alfb[:], recFv[:, gt], AF.Exp, scale=cst[:, 60 + gt:61 + gt]),
                inc=("s", 1))
            em.wait("vector", sv)
            vv = em.op("vector", lambda e: nc.vector.tensor_tensor(
                t3b[:], alfb[:], alfb[:], OP.mult), inc=("v", 1))
            em.seq("vector", lambda e: nc.vector.tensor_scalar(
                t3b[:], t3b[:], -1.0, 1.0, OP.mult, OP.add))
            em.wait("scalar", vv)
            em.wait("scalar", ("v", em.cnt["v"]))
            em.seq("scalar", lambda e: nc.scalar.activation(
                t6b[:], t3b[:], AF.Ln, scale=1.0, bias=kc[:, 2:3]))
            sv2 = em.op("scalar", lambda e: nc.scalar.activation(
                t3b[:], t6b[:], AF.Exp, scale=0.5), inc=("s", 1))
            em.wait("vector", sv2)
            em.seq("vector", lambda e, gt=gt: nc.vector.tensor_tensor(
                t6b[:], t3b[:], inpFv[:, gt], OP.mult))
            em.seq("vector", lambda e, gt=gt: nc.vector.tensor_tensor(
                t6b[:], t6b[:], xcfv[:, gt], OP.mult))
            em.seq("vector", lambda e, gt=gt: nc.vector.tensor_tensor_scan(
                t3b[:], alfb[:], t6b[:], carry[:, gt:gt + 1], OP.mult, OP.add))
            em.seq("vector", lambda e, gt=gt: nc.vector.tensor_copy(
                carry[:, gt:gt + 1], t3b[:, SB - 1:SB]))
            em.seq("vector", lambda e, gt=gt: nc.vector.tensor_tensor(
                t3b[:], t3b[:], gtev[:, gt], OP.mult))
            em.op("vector", lambda e, gt=gt: nc.vector.tensor_copy(
                zBv[:, gt], t3b[:]), inc=("v", 1))
        v_zB = ("v", em.cnt["v"])

        if "dbgZ" in dbg:
            em.wait("sync", ("v", em.cnt["v"]))
            em.dma("sync", [(lambda h: lambda e, i=i: e.dma_start(
                out=dbgZ_h[h, :, i * SB:(i + 1) * SB], in_=zBv[:, h]))(h)
                for h in range(12)])
        em.wait("tensor", v_zB)
        if i > 0:
            em.wait("tensor", ("v", em.cnt["v"]))
        for j in range(2):
            for ch in range(2):
                bank = pout[2 * j + ch]
                for ht in range(12):
                    em.op("tensor", lambda e, j=j, ch=ch, ht=ht, bank=bank:
                          nc.tensor.matmul(
                              bank[:],
                              zBv[:, ht, j * 128:(j + 1) * 128],
                              ow[:, ht * D + ch * 512: ht * D + (ch + 1) * 512],
                              start=(ht == 0), stop=(ht == 11)))
        tt = em.mark("tensor", "t")
        em.wait("vector", tt)
        if i > 0:
            em.wait("vector", d_x1_out)
        for j in range(2):
            for ch in range(2):
                em.op("vector", lambda e, j=j, ch=ch: nc.vector.tensor_tensor(
                    x1bv[:, j, ch * 512:(ch + 1) * 512], pout[2 * j + ch][:],
                    x0cv[:, j, ch * 512:(ch + 1) * 512], OP.add))
        v_x1 = em.mark("vector", "v")
        v_stripe_done = v_x1

        em.wait("sync", v_x1)
        d_x1_out = em.dma("sync", [lambda e, i=i: e.dma_start(
            out=x1_h[i * SB:(i + 1) * SB, :].rearrange("(j p) d -> p j d", p=128),
            in_=x1bv[:])])

    em.barrier()
    em.drain_dma()
    esBp.close()
    esB.close()

    if stop_after == "B":
        em.check()
        return nc, em, es

    em.phase("C")
    SC = 512
    NSC = T // SC
    esC = ExitStack()
    wrow = sb(esC, "wrowC", [128, D])
    wqb = sb(esC, "wqbC", [128, D], BF16)
    wdT = sb(esC, "wdTC", [128, 8 * D], BF16)
    wdTv = wdT[:].rearrange("p (dt d) -> p dt d", dt=8)
    bhlC = sb(esC, "bhlC", [2, D], BF16)
    tmprow = sb(esC, "tmprowC", [1, D])
    x1c = sb(esC, "x1cC", [128, 4 * D])
    x1cv = x1c[:].rearrange("p (j d) -> p j d", j=4)
    xlnf = sb(esC, "xlnfC", [128, D])
    xqb = sb(esC, "xqbC", [128, D], BF16)
    xdT = sb(esC, "xdTC", [128, 8 * 128], BF16)
    xdTv = xdT[:].rearrange("p (dt t) -> p dt t", dt=8)
    x2b = sb(esC, "x2bC", [128, 4 * D])
    x2bv = x2b[:].rearrange("p (j d) -> p j d", j=4)
    qB1 = sb(esC, "qB1C", [128, D])
    qB2 = sb(esC, "qB2C", [128, D])
    qB3 = sb(esC, "qB3C", [128, D])
    qB4 = sb(esC, "qB4C", [128, D])
    qI = sb(esC, "qIC", [128, D], I32)
    qS = sb(esC, "qSC", [128, 96])

    esCp = ExitStack()
    ptrC = pst(esCp, "ptrC", [128, 1024], BF16)
    poutC = [pst(esCp, f"poutC{i}", [128, 512]) for i in range(4)]

    def emit_quantize(x_f32_ap, out_bf_ap):
        amax = qS[:, 0:32]; qsc = qS[:, 32:64]; qrs = qS[:, 64:96]
        xv = x_f32_ap.rearrange("p (a b) -> p a b", b=32)
        em.seq("vector", lambda e: nc.vector.tensor_reduce(
            amax, xv, AX.X, OP.max, apply_absolute_value=True))
        em.seq("vector", lambda e: nc.vector.tensor_scalar(
            qsc, amax, 1.0 / 7.5, 1e-30, OP.mult, OP.max))
        em.seq("vector", lambda e: nc.vector.reciprocal(qrs, qsc))
        em.seq("vector", lambda e: nc.vector.tensor_tensor(
            qB1[:].rearrange("p (a b) -> p a b", b=32), xv,
            qrs[:, :, None].broadcast_to([128, 32, 32]), OP.mult))
        def rhd(mult, dst):
            em.seq("vector", lambda e: nc.vector.tensor_scalar(
                qB4[:], qB1[:], float(mult), None, OP.mult))
            em.seq("vector", lambda e: nc.vector.tensor_copy(qI[:], qB4[:]))
            em.seq("vector", lambda e: nc.vector.tensor_copy(dst, qI[:]))
            em.seq("vector", lambda e: nc.vector.tensor_tensor(
                qB4[:], qB4[:], dst, OP.subtract))
            em.seq("vector", lambda e: nc.vector.scalar_tensor_tensor(
                dst, qB4[:], 0.5, dst, OP.is_gt, OP.add))
            em.seq("vector", lambda e: nc.vector.tensor_scalar(
                qB4[:], qB4[:], -0.5, None, OP.is_le))
            em.seq("vector", lambda e: nc.vector.tensor_tensor(
                dst, dst, qB4[:], OP.subtract))
            em.seq("vector", lambda e: nc.vector.tensor_scalar(
                dst, dst, 1.0 / mult, None, OP.mult))
        rhd(8.0, qB2[:])
        rhd(4.0, qB3[:])
        em.seq("vector", lambda e: nc.vector.tensor_tensor(qB4[:], qB1[:], qB1[:], OP.mult))
        em.seq("vector", lambda e: nc.vector.tensor_scalar(
            qB4[:], qB4[:], 4.0, None, OP.is_ge))
        em.seq("vector", lambda e: nc.vector.tensor_tensor(qB3[:], qB3[:], qB2[:], OP.subtract))
        em.seq("vector", lambda e: nc.vector.tensor_tensor(qB3[:], qB3[:], qB4[:], OP.mult))
        em.seq("vector", lambda e: nc.vector.tensor_tensor(qB2[:], qB2[:], qB3[:], OP.add))
        rhd(2.0, qB3[:])
        em.seq("vector", lambda e: nc.vector.tensor_tensor(qB4[:], qB1[:], qB1[:], OP.mult))
        em.seq("vector", lambda e: nc.vector.tensor_scalar(
            qB4[:], qB4[:], 16.0, None, OP.is_ge))
        em.seq("vector", lambda e: nc.vector.tensor_tensor(qB3[:], qB3[:], qB2[:], OP.subtract))
        em.seq("vector", lambda e: nc.vector.tensor_tensor(qB3[:], qB3[:], qB4[:], OP.mult))
        em.seq("vector", lambda e: nc.vector.tensor_tensor(qB2[:], qB2[:], qB3[:], OP.add))
        vq = em.op("vector", lambda e: nc.vector.tensor_tensor(
            qB1[:].rearrange("p (a b) -> p a b", b=32),
            qB2[:].rearrange("p (a b) -> p a b", b=32),
            qsc[:, :, None].broadcast_to([128, 32, 32]), OP.mult), inc=("v", 1))
        em.wait("scalar", vq)
        return em.op("scalar", lambda e: nc.scalar.copy(out_bf_ap, qB1[:]), inc=("s", 1))

    def emit_ln_f32(x_ap, out_f32_ap):
        sumx = stats[:, 0:1]; sumsq = stats[:, 1:2]; m = stats[:, 2:3]
        var = stats[:, 3:4]; r = stats[:, 4:5]; nmr = stats[:, 5:6]
        u = stats[:, 6:7]; mm = stats[:, 7:8]
        em.op("vector", lambda e: nc.vector.tensor_reduce(sumx, x_ap, AX.X, OP.add),
              inc=("v", 1))
        sv = em.op("scalar", lambda e: nc.scalar.activation(
            dump[:], x_ap, AF.Square, accum_out=sumsq), inc=("s", 1))
        em.wait("vector", sv)
        em.seq("vector", lambda e: nc.vector.tensor_scalar(m, sumx, 1.0 / D, None, OP.mult))
        em.seq("vector", lambda e: nc.vector.tensor_scalar(var, sumsq, 1.0 / D, None, OP.mult))
        em.seq("vector", lambda e: nc.vector.tensor_tensor(mm, m, m, OP.mult))
        vv = em.op("vector", lambda e: nc.vector.tensor_tensor(var, var, mm, OP.subtract),
                   inc=("v", 1))
        em.wait("scalar", vv)
        em.seq("scalar", lambda e: nc.scalar.activation(u, var, AF.Ln, bias=kc[:, 0:1]))
        sv2 = em.op("scalar", lambda e: nc.scalar.activation(r, u, AF.Exp, scale=-0.5),
                    inc=("s", 1))
        em.wait("vector", sv2)
        em.seq("vector", lambda e: nc.vector.scalar_tensor_tensor(
            nmr, m, -1.0, r, OP.mult, OP.mult))
        return em.op("vector", lambda e: nc.vector.tensor_scalar(
            out_f32_ap, x_ap, r, nmr, OP.mult, OP.add), inc=("v", 1))

    lorow = sb(esC, "lorowC", [1, D], BF16)

    def emit_bias_hilo(src_h, dst_bhl):
        d_b = em.dma("sync", [lambda e: e.dma_start(out=tmprow[:], in_=src_h[:])])
        em.wait("scalar", d_b)
        sh = em.op("scalar", lambda e: nc.scalar.copy(dst_bhl[0:1, :], tmprow[:]),
                   inc=("s", 1))
        em.wait("vector", sh)
        em.seq("vector", lambda e: nc.vector.tensor_copy(dump[0:1, :], dst_bhl[0:1, :]))
        vlo = em.op("vector", lambda e: nc.vector.tensor_tensor(
            lorow[:], tmprow[:], dump[0:1, :], OP.subtract), inc=("v", 1))
        em.wait("sync", vlo)
        return em.dma("sync", [lambda e: e.dma_start(out=dst_bhl[1:2, :], in_=lorow[:])])

    v_bC = emit_bias_hilo(ffn_b_h, bhlC)

    s_wq_evac = None
    s_q = None
    for rtile in range(8):
        em.wait("sync", s_q)
        d_w = em.dma("sync", [lambda e, rtile=rtile: e.dma_start(
            out=wrow[:], in_=ffn_w_h[rtile * 128:(rtile + 1) * 128, :])])
        em.wait("vector", d_w)
        s_q = emit_quantize(wrow[:], wqb[:])
        em.wait("tensor", s_q)
        em.wait("tensor", s_wq_evac)
        for dt in range(8):
            em.op("tensor", lambda e, dt=dt: nc.tensor.transpose(
                ptrC[:, dt * 128:(dt + 1) * 128],
                wqb[:, dt * 128:(dt + 1) * 128], ident[:]))
        tt = em.mark("tensor", "t")
        em.wait("scalar", tt)
        s_wq_evac = em.op("scalar", lambda e, rtile=rtile: nc.scalar.copy(
            wdTv[:, :, rtile * 128:(rtile + 1) * 128],
            ptrC[:].rearrange("p (dt t) -> p dt t", dt=8)), inc=("s", 1))

    em.wait("tensor", v_bC)
    em.wait("tensor", s_wq_evac)
    v_ln_doneC = None
    d_x2_out = None
    s_xdT = None
    for i in range(NSC):
        if i >= 1:
            em.wait("sync", v_ln_doneC)
        d_in = em.dma("sync", [lambda e, i=i: e.dma_start(
            out=x1cv[:], in_=x1_h[i * SC:(i + 1) * SC, :].rearrange(
                "(j p) d -> p j d", p=128))])
        em.wait("vector", d_in)
        em.wait("scalar", d_in)
        for j in range(4):
            emit_ln_f32(x1cv[:, j], xlnf[:])
            s_q = emit_quantize(xlnf[:], xqb[:])
            if "dbgXD" in dbg:
                em.wait("sync", s_q)
                em.dma("sync", [lambda e, i=i, j=j: e.dma_start(
                    out=dbgXD_h[i * SC + j * 128: i * SC + (j + 1) * 128, :],
                    in_=xqb[:])])
            em.wait("tensor", s_q)
            em.wait("tensor", s_xdT)
            for dt in range(8):
                em.op("tensor", lambda e, dt=dt: nc.tensor.transpose(
                    ptrC[:, dt * 128:(dt + 1) * 128],
                    xqb[:, dt * 128:(dt + 1) * 128], ident[:]))
            tt = em.mark("tensor", "t")
            em.wait("scalar", tt)
            s_xdT = em.op("scalar", lambda e: nc.scalar.copy(
                xdTv[:], ptrC[:].rearrange("p (dt t) -> p dt t", dt=8)), inc=("s", 1))
            em.wait("tensor", s_xdT)
            if i > 0 or j > 0:
                em.wait("tensor", ("v", em.cnt["v"]))
            for ch in range(2):
                bank = poutC[2 * (j % 2) + ch]
                for dt in range(8):
                    em.op("tensor", lambda e, dt=dt, ch=ch, bank=bank: nc.tensor.matmul(
                        bank[:], xdTv[:, dt],
                        wdTv[:, dt, ch * 512:(ch + 1) * 512],
                        start=(dt == 0), stop=False))
                em.op("tensor", lambda e, ch=ch, bank=bank: nc.tensor.matmul(
                    bank[:], ones2[:], bhlC[:, ch * 512:(ch + 1) * 512],
                    start=False, stop=True))
            tt2 = em.mark("tensor", "t")
            em.wait("vector", tt2)
            if i > 0 and j == 0:
                em.wait("vector", d_x2_out)
            for ch in range(2):
                em.op("vector", lambda e, j=j, ch=ch: nc.vector.tensor_tensor(
                    x2bv[:, j, ch * 512:(ch + 1) * 512], poutC[2 * (j % 2) + ch][:],
                    x1cv[:, j, ch * 512:(ch + 1) * 512], OP.add),
                    inc=("v", 1) if ch == 1 else None)
        v_ln_doneC = ("v", em.cnt["v"])
        em.wait("sync", v_ln_doneC)
        d_x2_out = em.dma("sync", [lambda e, i=i: e.dma_start(
            out=x2_h[i * SC:(i + 1) * SC, :].rearrange("(j p) d -> p j d", p=128),
            in_=x2bv[:])])

    em.barrier()
    em.drain_dma()
    esCp.close()
    esC.close()

    if stop_after == "C":
        em.check()
        return nc, em, es

    em.phase("D")
    CD = 512
    NCD = T // CD
    esD = ExitStack()
    wqkv = sb(esD, "wqkvD", [128, 8 * 3072], BF16)
    wqkvv = wqkv[:].rearrange("p (dt f) -> p dt f", dt=8)
    wpro = sb(esD, "wproD", [128, 8 * D], BF16)
    wprov = wpro[:].rearrange("p (dt d) -> p dt d", dt=8)
    bhlD = sb(esD, "bhlD", [2, D], BF16)
    tmprowD = sb(esD, "tmprowD", [1, D])
    x2r = sb(esD, "x2rD", [128, 2 * 4 * D])
    x2rv = x2r[:].rearrange("p (c j d) -> p c j d", c=2, j=4)
    xlnbD = sb(esD, "xlnbD", [128, D], BF16)
    xlnTD = sb(esD, "xlnTD", [128, 8 * CD], BF16)
    xlnTDv = xlnTD[:].rearrange("p (dt t) -> p dt t", dt=8)
    qF = sb(esD, "qFD", [128, 8 * 1024], BF16)
    qFv = qF[:].rearrange("p (f t) -> p f t", f=8)
    kF = sb(esD, "kFD", [128, 8 * 1024], BF16)
    kFv = kF[:].rearrange("p (f t) -> p f t", f=8)
    vT = sb(esD, "vTD", [128, 8 * 16 * 65], BF16)
    vTv = vT[:].rearrange("p (r h c) -> p r h c", r=8, h=16)
    expb = sb(esD, "expbD", [128, 2 * 384], BF16)
    aob = sb(esD, "aobD", [128, D])
    aobb = sb(esD, "aobbD", [128, D], BF16)
    aoT = sb(esD, "aoTD", [128, 8 * 128], BF16)
    aoTv = aoT[:].rearrange("p (dt t) -> p dt t", dt=8)
    x3b = sb(esD, "x3bD", [128, D])
    rcol = sb(esD, "rcolD", [128, 1])

    esDp = ExitStack()
    ptrD = pst(esDp, "ptrD", [128, 1024], BF16)
    pqk = [pst(esDp, f"pqkD{i}", [128, 512]) for i in range(2)]
    psc = [pst(esDp, f"pscD{i}", [128, 384]) for i in range(2)]
    pav = [pst(esDp, f"pavD{i}", [128, 65]) for i in range(2)]
    ppr = pst(esDp, "pprD", [128, 512])

    fns = []
    for dt in range(8):
        fns.append(lambda e, dt=dt: e.dma_start(
            out=wqkvv[:, dt], in_=w_qkv_h[dt * 128:(dt + 1) * 128, :]))
        fns.append(lambda e, dt=dt: e.dma_start(
            out=wprov[:, dt], in_=w_proj_h[dt * 128:(dt + 1) * 128, :]))
    dGw = em.dma("gpsimd", fns)
    lorowD = sb(esD, "lorowD", [1, D], BF16)
    d_b = em.dma("sync", [lambda e: e.dma_start(out=tmprowD[:], in_=b_proj_h[:])])
    em.wait("scalar", d_b)
    sh = em.op("scalar", lambda e: nc.scalar.copy(bhlD[0:1, :], tmprowD[:]), inc=("s", 1))
    em.wait("vector", sh)
    em.seq("vector", lambda e: nc.vector.tensor_copy(dump[0:1, :], bhlD[0:1, :]))
    vloD = em.op("vector", lambda e: nc.vector.tensor_tensor(
        lorowD[:], tmprowD[:], dump[0:1, :], OP.subtract), inc=("v", 1))
    em.wait("sync", vloD)
    v_bD = em.dma("sync", [lambda e: e.dma_start(out=bhlD[1:2, :], in_=lorowD[:])])
    v_ones = em.op("vector", lambda e: nc.vector.memset(vTv[:, :, :, 64:65], 1.0),
                   inc=("v", 1))
    em.wait("tensor", dGw)
    em.wait("tensor", v_bD)
    em.wait("tensor", v_ones)

    s_qk_evac = [None] * 2
    s_exp = [None] * 2
    v_attn_done = None
    s_attn_done = None
    d_x3_out = None
    v_ln_doneD = None
    s_ao_evac = None
    s_aocast = None

    def emit_attn_block(n):
        nonlocal v_attn_done, s_attn_done, d_x3_out, s_ao_evac, s_aocast
        rn = n % 8
        em.wait("vector", s_aocast)
        em.wait("vector", d_x3_out)
        for h in range(16):
            pb = psc[h % 2]
            eb = expb[:, (h % 2) * 384:(h % 2 + 1) * 384]
            ebv = expb[:].rearrange("p (b k q) -> p b k q", b=2, k=3)[:, h % 2]
            em.wait("tensor", s_exp[h % 2])
            for kb in range(3):
                nk = n - 1 + kb
                if nk < 0 or nk > 31:
                    continue
                rk = nk % 8
                hp = (h % 2) * 64
                ft = h // 2
                em.op("tensor", lambda e, kb=kb, rk=rk, hp=hp, ft=ft, pb=pb:
                      nc.tensor.matmul(
                          pb[:, kb * 128:(kb + 1) * 128],
                          kFv[hp:hp + 64, ft, rk * 128:(rk + 1) * 128],
                          qFv[hp:hp + 64, ft, rn * 128:(rn + 1) * 128],
                          start=True, stop=True))
            tt = em.mark("tensor", "t")
            em.wait("scalar", tt)
            lo = 128 if n == 0 else 0
            hi = 256 if n == 31 else 384
            s_e = em.op("scalar", lambda e, pb=pb, eb=eb, lo=lo, hi=hi:
                        nc.scalar.activation(
                eb[:, lo:hi], pb[:, lo:hi], AF.Exp, scale=0.125), inc=("s", 1))
            em.wait("vector", s_e)
            triv = trimask[:].rearrange("p (b q) -> p b q", b=2)
            if n == 0:
                em.seq("vector", lambda e, ebv=ebv: nc.vector.memset(ebv[:, 0], 0.0))
            else:
                em.seq("vector", lambda e, ebv=ebv, triv=triv: nc.vector.tensor_tensor(
                    ebv[:, 0], ebv[:, 0], triv[:, 0], OP.mult))
            if n == 31:
                vm = em.op("vector", lambda e, ebv=ebv: nc.vector.memset(ebv[:, 2], 0.0),
                           inc=("v", 1))
            else:
                vm = em.op("vector", lambda e, ebv=ebv, triv=triv: nc.vector.tensor_tensor(
                    ebv[:, 2], ebv[:, 2], triv[:, 1], OP.mult), inc=("v", 1))
            s_exp[h % 2] = vm
            em.wait("tensor", vm)
            pa = pav[h % 2]
            em.wait("tensor", v_attn_done)
            kbs = [kb for kb in range(3) if 0 <= n - 1 + kb <= 31]
            for kb in kbs:
                rk = (n - 1 + kb) % 8
                em.op("tensor", lambda e, kb=kb, rk=rk, h=h, pa=pa, ebv=ebv:
                      nc.tensor.matmul(
                          pa[:], ebv[:, kb],
                          vTv[:, rk, h, :],
                          start=(kb == kbs[0]), stop=(kb == kbs[-1])))
            tta = em.mark("tensor", "t")
            em.wait("vector", tta)
            em.seq("vector", lambda e, pa=pa: nc.vector.reciprocal(rcol[:], pa[:, 64:65]))
            v_attn_done = em.op("vector", lambda e, pa=pa, h=h: nc.vector.tensor_scalar(
                aob[:, h * 64:(h + 1) * 64], pa[:, 0:64], rcol[:], None, OP.mult),
                inc=("v", 1))
        em.wait("scalar", v_attn_done)
        s_c = em.op("scalar", lambda e: nc.scalar.copy(aobb[:], aob[:]), inc=("s", 1))
        s_aocast = s_c
        em.wait("tensor", s_c)
        em.wait("tensor", s_ao_evac)
        for dt in range(8):
            em.op("tensor", lambda e, dt=dt: nc.tensor.transpose(
                ptrD[:, dt * 128:(dt + 1) * 128],
                aobb[:, dt * 128:(dt + 1) * 128], ident[:]))
        tt = em.mark("tensor", "t")
        em.wait("scalar", tt)
        s_ao_evac = em.op("scalar", lambda e: nc.scalar.copy(
            aoTv[:], ptrD[:].rearrange("p (dt t) -> p dt t", dt=8)), inc=("s", 1))
        em.wait("tensor", s_ao_evac)
        em.wait("tensor", s_attn_done)
        for ch in range(2):
            for dt in range(8):
                em.op("tensor", lambda e, dt=dt, ch=ch: nc.tensor.matmul(
                    ppr[:], aoTv[:, dt],
                    wprov[:, dt, ch * 512:(ch + 1) * 512],
                    start=(dt == 0), stop=False))
            em.op("tensor", lambda e, ch=ch: nc.tensor.matmul(
                ppr[:], ones2[:], bhlD[:, ch * 512:(ch + 1) * 512],
                start=False, stop=True))
            tt = em.mark("tensor", "t")
            em.wait("vector", tt)
            cslot = (n // 4) % 2
            v_r = em.op("vector", lambda e, n=n, ch=ch, cslot=cslot:
                        nc.vector.tensor_tensor(
                            x3b[:, ch * 512:(ch + 1) * 512], ppr[:],
                            x2rv[:, cslot, n % 4, ch * 512:(ch + 1) * 512], OP.add),
                        inc=("v", 1))
            s_attn_done = v_r
            em.wait("tensor", v_r)
        em.wait("sync", s_attn_done)
        d_x3_out = em.dma("sync", [lambda e, n=n: e.dma_start(
            out=x3_h[n * 128:(n + 1) * 128, :], in_=x3b[:])])

    for c in range(NCD):
        cslot = c % 2
        if c >= 2:
            em.wait("sync", s_attn_done)
        d_in = em.dma("sync", [lambda e, c=c, cslot=cslot: e.dma_start(
            out=x2rv[:, cslot], in_=x2_h[c * CD:(c + 1) * CD, :].rearrange(
                "(j p) d -> p j d", p=128))])
        em.wait("vector", d_in)
        em.wait("scalar", d_in)
        s_evac = None
        for j in range(4):
            if not (c == 0 and j == 0):
                em.wait("scalar", ("t", em.cnt["t"]))
            s_apply = emit_ln(x2rv[:, cslot, j], xlnbD[:])
            em.wait("tensor", s_apply)
            em.wait("tensor", s_evac)
            for dt in range(8):
                em.op("tensor", lambda e, dt=dt: nc.tensor.transpose(
                    ptrD[:, dt * 128:(dt + 1) * 128],
                    xlnbD[:, dt * 128:(dt + 1) * 128], ident[:]))
            tt = em.mark("tensor", "t")
            em.wait("scalar", tt)
            s_evac = em.op("scalar", lambda e, j=j: nc.scalar.copy(
                xlnTDv[:, :, j * 128:(j + 1) * 128],
                ptrD[:].rearrange("p (dt t) -> p dt t", dt=8)), inc=("s", 1))
        v_ln_doneD = ("v", em.cnt["v"])
        em.wait("tensor", s_evac)

        rbase = (4 * c) % 8
        sqk = [None, None]
        for f in range(16):
            bank = pqk[f % 2]
            em.wait("tensor", s_qk_evac[f % 2])
            for dt in range(8):
                em.op("tensor", lambda e, f=f, dt=dt, bank=bank: nc.tensor.matmul(
                    bank[:], wqkvv[:, dt, f * 128:(f + 1) * 128],
                    xlnTDv[:, dt], start=(dt == 0), stop=(dt == 7)))
            tt = em.mark("tensor", "t")
            em.wait("scalar", tt)
            dstv = qFv if f < 8 else kFv
            fi = f % 8
            s_qk_evac[f % 2] = em.op("scalar", lambda e, dstv=dstv, fi=fi, bank=bank,
                                     rbase=rbase: nc.scalar.copy(
                dstv[:, fi, rbase * 128:(rbase + 4) * 128], bank[:]), inc=("s", 1))
        for j in range(4):
            for ch in range(2):
                bank = pqk[ch]
                em.wait("tensor", s_qk_evac[ch])
                for dt in range(8):
                    em.op("tensor", lambda e, j=j, ch=ch, dt=dt, bank=bank:
                          nc.tensor.matmul(
                              bank[:], xlnTDv[:, dt, j * 128:(j + 1) * 128],
                              wqkvv[:, dt, 2048 + ch * 512:2048 + (ch + 1) * 512],
                              start=(dt == 0), stop=(dt == 7)))
                tt = em.mark("tensor", "t")
                em.wait("scalar", tt)
                s_qk_evac[ch] = em.op("scalar", lambda e, j=j, ch=ch, bank=bank,
                                      rbase=rbase: nc.scalar.copy(
                    vTv[:, rbase + j, ch * 8:(ch + 1) * 8, 0:64],
                    bank[:].rearrange("p (h dv) -> p h dv", h=8)), inc=("s", 1))
        em.wait("tensor", s_qk_evac[0])
        em.wait("tensor", s_qk_evac[1])

        if c == 0:
            blocks = [0, 1, 2]
        elif c == NCD - 1:
            blocks = [4 * c - 1, 4 * c, 4 * c + 1, 4 * c + 2, 4 * c + 3]
        else:
            blocks = [4 * c - 1, 4 * c, 4 * c + 1, 4 * c + 2]
        for n in blocks:
            emit_attn_block(n)

    em.barrier()
    em.drain_dma()
    esDp.close()
    esD.close()

    if stop_after == "D":
        em.check()
        return nc, em, es

    em.phase("E")
    SE = 1024
    NSE = T // SE
    esE = ExitStack()
    x3r = sb(esE, "x3rE", [128, 2 * D])
    x3rv = x3r[:].rearrange("p (b d) -> p b d", b=2)
    xlnbE = sb(esE, "xlnbE", [128, D], BF16)
    xlnTE = sb(esE, "xlnTE", [128, 8 * SE], BF16)
    xlnTEv = xlnTE[:].rearrange("p (dt t) -> p dt t", dt=8)
    rwb = sb(esE, "rwbE", [128, 8 * 8], BF16)
    rwbv = rwb[:].rearrange("p (dt e) -> p dt e", dt=8)
    rbb = sb(esE, "rbbE", [128, 8])
    scb = sb(esE, "scbE", [128, 8])
    scpb = sb(esE, "scpbE", [128, 8])
    gsb = sb(esE, "gsbE", [128, 4])
    rtmp = sb(esE, "rtmpE", [128, 24])
    cwb = sb(esE, "cwbE", [128, 8 * 8])
    cwbv = cwb[:].rearrange("p (j e) -> p j e", j=8)
    wgE = [sb(esE, f"wgE{i}", [128, 8 * 512], BF16) for i in range(4)]
    wuE = [sb(esE, f"wuE{i}", [128, 8 * 512], BF16) for i in range(4)]
    wdE = [sb(esE, f"wdE{i}", [128, 4 * D], BF16) for i in range(4)]
    gb = [sb(esE, f"gbE{i}", [128, 512]) for i in range(2)]
    ub = [sb(esE, f"ubE{i}", [128, 512]) for i in range(2)]
    gub = [sb(esE, f"gubE{i}", [128, 512], BF16) for i in range(2)]
    guT = [sb(esE, f"guTE{i}", [128, 4 * 128], BF16) for i in range(2)]
    acc = sb(esE, "accE", [128, 8 * D])
    accv = acc[:].rearrange("p (j d) -> p j d", j=8)

    esEp = ExitStack()
    ptrE = pst(esEp, "ptrE", [128, 512], BF16)
    pgE = [pst(esEp, f"pgE{i}", [128, 512]) for i in range(2)]
    puE = [pst(esEp, f"puE{i}", [128, 512]) for i in range(2)]
    pwd = [pst(esEp, f"pwdE{i}", [128, 512]) for i in range(2)]
    prt = pwd[0]

    d_rb = em.dma("sync", [lambda e: e.dma_start(out=rbb[:], in_=router_bb_h[:])])
    fns = []
    for dt in range(8):
        fns.append(lambda e, dt=dt: e.dma_start(
            out=rwbv[:, dt], in_=router_wT_h[dt * 128:(dt + 1) * 128, :]))
    d_rw = em.dma("gpsimd", fns)
    em.wait("tensor", d_rb)
    em.wait("vector", d_rb)
    em.wait("tensor", d_rw)

    def wg_src(e_, slot):
        if e_ < 8:
            return [(lambda dt: lambda e: e.dma_start(
                out=wgE[slot][:, dt * 512:(dt + 1) * 512],
                in_=wgT_h[e_, dt * 128:(dt + 1) * 128, :]))(dt) for dt in range(8)]
        half = e_ - 8
        return [(lambda dt: lambda e: e.dma_start(
            out=wgE[slot][:, dt * 512:(dt + 1) * 512],
            in_=sgT_h[dt * 128:(dt + 1) * 128, half * 512:(half + 1) * 512]))(dt)
            for dt in range(8)]

    def wu_src(e_, slot):
        if e_ < 8:
            return [(lambda dt: lambda e: e.dma_start(
                out=wuE[slot][:, dt * 512:(dt + 1) * 512],
                in_=wuT_h[e_, dt * 128:(dt + 1) * 128, :]))(dt) for dt in range(8)]
        half = e_ - 8
        return [(lambda dt: lambda e: e.dma_start(
            out=wuE[slot][:, dt * 512:(dt + 1) * 512],
            in_=suT_h[dt * 128:(dt + 1) * 128, half * 512:(half + 1) * 512]))(dt)
            for dt in range(8)]

    def wd_src(e_, slot):
        if e_ < 8:
            return [(lambda it: lambda e: e.dma_start(
                out=wdE[slot][:, it * D:(it + 1) * D],
                in_=wdT_h[e_, it * 128:(it + 1) * 128, :]))(it) for it in range(4)]
        half = e_ - 8
        return [(lambda it: lambda e: e.dma_start(
            out=wdE[slot][:, it * D:(it + 1) * D],
            in_=sdT_h[half * 512 + it * 128:half * 512 + (it + 1) * 128, :]))(it)
            for it in range(4)]

    def pair_dma(gp):
        p_ = gp % 5
        base = (gp % 2) * 2
        fns = (wg_src(2 * p_, base) + wu_src(2 * p_, base) + wd_src(2 * p_, base)
               + wg_src(2 * p_ + 1, base + 1) + wu_src(2 * p_ + 1, base + 1)
               + wd_src(2 * p_ + 1, base + 1))
        return em.dma("gpsimd", fns)

    NPAIR_TOT = 5 * NSE
    pair_w_tgt = [None] * (NPAIR_TOT + 1)
    t_pair_done = [None] * NPAIR_TOT
    pair_w_tgt[0] = pair_dma(0)

    v_stripe_doneE = None
    d_y_out = None
    s_guT = [None, None]
    v_acc = None
    s_g_prev = [None, None]
    v_u_prev = [None, None]
    d_x3t = [None, None]
    v_ln_read = [None, None]
    for i in range(NSE):
        if i >= 1:
            em.wait("sync", v_stripe_doneE)
        s_evac = s_guT[1]
        s_sc_prev = None
        for j in range(8):
            em.wait("sync", v_ln_read[j % 2])
            d_x3t[j % 2] = em.dma("sync", [lambda e, i=i, j=j: e.dma_start(
                out=x3rv[:, j % 2], in_=x3_h[i * SE + j * 128:i * SE + (j + 1) * 128, :])])
            em.wait("vector", d_x3t[j % 2])
            em.wait("scalar", d_x3t[j % 2])
            if not (i == 0 and j == 0):
                em.wait("scalar", ("t", em.cnt["t"]))
            s_apply = emit_ln(x3rv[:, j % 2], xlnbE[:])
            v_ln_read[j % 2] = ("v", em.cnt["v"])
            em.wait("tensor", s_apply)
            em.wait("tensor", s_evac)
            for half in range(2):
                for q4 in range(4):
                    dt = half * 4 + q4
                    em.op("tensor", lambda e, dt=dt, q4=q4: nc.tensor.transpose(
                        ptrE[:, q4 * 128:(q4 + 1) * 128],
                        xlnbE[:, dt * 128:(dt + 1) * 128], ident[:]))
                tt = em.mark("tensor", "t")
                em.wait("scalar", tt)
                s_evac = em.op("scalar", lambda e, j=j, half=half: nc.scalar.copy(
                    xlnTEv[:, half * 4:half * 4 + 4, j * 128:(j + 1) * 128],
                    ptrE[:].rearrange("p (dt t) -> p dt t", dt=4)), inc=("s", 1))
                if half == 0:
                    em.wait("tensor", s_evac)
        s_lnT_done = s_evac
        em.wait("tensor", s_lnT_done)
        em.wait("tensor", v_acc)
        v_route_prev = None
        for j in range(8):
            em.wait("tensor", s_sc_prev)
            for dt in range(8):
                em.op("tensor", lambda e, j=j, dt=dt: nc.tensor.matmul(
                    prt[:, 0:8], xlnTEv[:, dt, j * 128:(j + 1) * 128],
                    rwbv[:, dt], start=(dt == 0), stop=(dt == 7)))
            tt = em.mark("tensor", "t")
            em.wait("scalar", tt)
            em.wait("scalar", v_route_prev)
            s_sc = em.op("scalar", lambda e: nc.scalar.activation(
                scb[:], prt[:, 0:8], AF.Sigmoid), inc=("s", 1))
            s_sc_prev = s_sc
            em.wait("vector", s_sc)
            em.seq("vector", lambda e: nc.vector.tensor_tensor(
                scpb[:], scb[:], rbb[:], OP.add))
            em.seq("vector", lambda e: nc.vector.tensor_tensor(
                gsb[:], scpb[:].rearrange("p (g two) -> p g two", two=2)[:, :, 0],
                scpb[:].rearrange("p (g two) -> p g two", two=2)[:, :, 1], OP.add))
            w01 = rtmp[:, 0:1]; w02 = rtmp[:, 1:2]; w03 = rtmp[:, 2:3]
            w12 = rtmp[:, 3:4]; w13 = rtmp[:, 4:5]; w23 = rtmp[:, 5:6]
            pairs = [(0, 1, w01), (0, 2, w02), (0, 3, w03),
                     (1, 2, w12), (1, 3, w13), (2, 3, w23)]
            for a_, b_, wv in pairs:
                em.seq("vector", lambda e, a_=a_, b_=b_, wv=wv: nc.vector.tensor_tensor(
                    wv, gsb[:, a_:a_ + 1], gsb[:, b_:b_ + 1], OP.is_ge))
            wins = rtmp[:, 8:12]
            em.seq("vector", lambda e: nc.vector.tensor_tensor(
                wins[:, 0:1], w01, w02, OP.add))
            em.seq("vector", lambda e: nc.vector.tensor_tensor(
                wins[:, 0:1], wins[:, 0:1], w03, OP.add))
            em.seq("vector", lambda e: nc.vector.scalar_tensor_tensor(
                wins[:, 1:2], w01, -1.0, w12, OP.mult, OP.add))
            em.seq("vector", lambda e: nc.vector.tensor_tensor(
                wins[:, 1:2], wins[:, 1:2], w13, OP.add))
            em.seq("vector", lambda e: nc.vector.tensor_scalar(
                wins[:, 1:2], wins[:, 1:2], 1.0, None, OP.add))
            em.seq("vector", lambda e: nc.vector.scalar_tensor_tensor(
                wins[:, 2:3], w02, -1.0, w23, OP.mult, OP.add))
            em.seq("vector", lambda e: nc.vector.scalar_tensor_tensor(
                wins[:, 2:3], w12, -1.0, wins[:, 2:3], OP.mult, OP.add))
            em.seq("vector", lambda e: nc.vector.tensor_scalar(
                wins[:, 2:3], wins[:, 2:3], 2.0, None, OP.add))
            em.seq("vector", lambda e: nc.vector.tensor_tensor(
                wins[:, 3:4], w03, w13, OP.add))
            em.seq("vector", lambda e: nc.vector.tensor_tensor(
                wins[:, 3:4], wins[:, 3:4], w23, OP.add))
            em.seq("vector", lambda e: nc.vector.tensor_scalar(
                wins[:, 3:4], wins[:, 3:4], -1.0, 3.0, OP.mult, OP.add))
            gmask = rtmp[:, 12:16]
            em.seq("vector", lambda e: nc.vector.tensor_scalar(
                gmask[:], wins[:], 2.0, None, OP.is_ge))
            smask = rtmp[:, 16:24]
            em.seq("vector", lambda e: nc.vector.tensor_copy(
                smask[:].rearrange("p (g two) -> p g two", two=2)[:, :, 0], gmask[:]))
            em.seq("vector", lambda e: nc.vector.tensor_copy(
                smask[:].rearrange("p (g two) -> p g two", two=2)[:, :, 1], gmask[:]))
            masked = rtmp[:, 0:8]
            em.seq("vector", lambda e: nc.vector.tensor_tensor(
                masked[:], scpb[:], smask[:], OP.mult))
            m1 = rtmp[:, 8:9]; m2 = rtmp[:, 9:10]; den = rtmp[:, 10:11]
            sel = cwbv[:, j]
            em.seq("vector", lambda e: nc.vector.tensor_reduce(m1, masked[:], AX.X, OP.max))
            em.seq("vector", lambda e, sel=sel: nc.vector.tensor_scalar(
                sel, masked[:], m1, None, OP.is_ge))
            em.seq("vector", lambda e, sel=sel: nc.vector.tensor_scalar(
                rtmp[:, 16:24], sel, m1, None, OP.mult))
            em.seq("vector", lambda e: nc.vector.tensor_tensor(
                masked[:], masked[:], rtmp[:, 16:24], OP.subtract))
            em.seq("vector", lambda e: nc.vector.tensor_reduce(m2, masked[:], AX.X, OP.max))
            em.seq("vector", lambda e: nc.vector.tensor_scalar(
                rtmp[:, 16:24], masked[:], m2, None, OP.is_ge))
            em.seq("vector", lambda e, sel=sel: nc.vector.tensor_tensor(
                sel, sel, rtmp[:, 16:24], OP.add))
            em.seq("vector", lambda e, sel=sel: nc.vector.tensor_tensor(
                sel, sel, scb[:], OP.mult))
            em.seq("vector", lambda e, sel=sel: nc.vector.tensor_reduce(
                den, sel, AX.X, OP.add))
            em.seq("vector", lambda e: nc.vector.tensor_scalar(
                den, den, 1e-20, None, OP.add))
            em.seq("vector", lambda e: nc.vector.reciprocal(den, den))
            em.op("vector", lambda e, sel=sel: nc.vector.tensor_scalar(
                sel, sel, den, None, OP.mult), inc=("v", 1))
            v_route_prev = ("v", em.cnt["v"])
        v_route_done = ("v", em.cnt["v"])
        if i > 0:
            em.wait("vector", d_y_out)

        for p_ in range(5):
            gp = i * 5 + p_
            if gp + 1 < NPAIR_TOT:
                if gp >= 1:
                    em.wait("gpsimd", t_pair_done[gp - 1])
                pair_w_tgt[gp + 1] = pair_dma(gp + 1)
            em.wait("tensor", pair_w_tgt[gp])
            em.wait("tensor", v_route_done)
            em.wait("tensor", s_lnT_done)
            base = (gp % 2) * 2
            for j in range(8):
                for ei in range(2):
                    ex = 2 * p_ + ei
                    slot = base + ei
                    em.wait("tensor", s_guT[ei])
                    em.wait("tensor", s_g_prev[ei])
                    for dt in range(8):
                        em.op("tensor", lambda e, j=j, dt=dt, slot=slot, ei=ei:
                              nc.tensor.matmul(
                                  pgE[ei][:], xlnTEv[:, dt, j * 128:(j + 1) * 128],
                                  wgE[slot][:, dt * 512:(dt + 1) * 512],
                                  start=(dt == 0), stop=(dt == 7)))
                    ttg = em.mark("tensor", "t")
                    em.wait("tensor", v_u_prev[ei])
                    for dt in range(8):
                        em.op("tensor", lambda e, j=j, dt=dt, slot=slot, ei=ei:
                              nc.tensor.matmul(
                                  puE[ei][:], xlnTEv[:, dt, j * 128:(j + 1) * 128],
                                  wuE[slot][:, dt * 512:(dt + 1) * 512],
                                  start=(dt == 0), stop=(dt == 7)))
                    ttu = em.mark("tensor", "t")
                    em.wait("scalar", ttg)
                    s_g = em.op("scalar", lambda e, ei=ei: nc.scalar.activation(
                        gb[ei][:], pgE[ei][:], AF.Silu), inc=("s", 1))
                    s_g_prev[ei] = s_g
                    em.wait("vector", ttu)
                    em.wait("vector", s_g)
                    v_u_prev[ei] = em.seq("vector", lambda e, ei=ei: nc.vector.tensor_copy(
                        ub[ei][:], puE[ei][:]))
                    vgu = em.op("vector", lambda e, ei=ei: nc.vector.tensor_tensor(
                        ub[ei][:], ub[ei][:], gb[ei][:], OP.mult), inc=("v", 1))
                    if ex < 8:
                        vgu = em.op("vector", lambda e, j=j, ex=ex, ei=ei:
                                    nc.vector.tensor_scalar(
                            ub[ei][:], ub[ei][:], cwbv[:, j, ex:ex + 1], None, OP.mult),
                            inc=("v", 1))
                    em.wait("scalar", vgu)
                    s_gub = em.op("scalar", lambda e, ei=ei: nc.scalar.copy(
                        gub[ei][:], ub[ei][:]), inc=("s", 1))
                    em.wait("tensor", s_gub)
                    em.wait("tensor", s_guT[1 - ei])
                    for it in range(4):
                        em.op("tensor", lambda e, it=it, ei=ei: nc.tensor.transpose(
                            ptrE[:, it * 128:(it + 1) * 128],
                            gub[ei][:, it * 128:(it + 1) * 128], ident[:]))
                    ttt = em.mark("tensor", "t")
                    em.wait("scalar", ttt)
                    s_guT[ei] = em.op("scalar", lambda e, ei=ei: nc.scalar.copy(
                        guT[ei][:], ptrE[:, 0:512]), inc=("s", 1))
                em.wait("tensor", s_guT[0])
                em.wait("tensor", s_guT[1])
                em.wait("tensor", v_acc)
                for ch in range(2):
                    bank = pwd[ch]
                    first = True
                    for ei in range(2):
                        slot = base + ei
                        for it in range(4):
                            em.op("tensor", lambda e, ei=ei, it=it, ch=ch, slot=slot,
                                  bank=bank, first=first: nc.tensor.matmul(
                                bank[:], guT[ei][:, it * 128:(it + 1) * 128],
                                wdE[slot][:, it * D + ch * 512: it * D + (ch + 1) * 512],
                                start=first, stop=(ei == 1 and it == 3)))
                            first = False
                ttw = em.mark("tensor", "t")
                em.wait("vector", ttw)
                for ch in range(2):
                    if p_ == 0:
                        em.op("vector", lambda e, j=j, ch=ch: nc.vector.tensor_copy(
                            accv[:, j, ch * 512:(ch + 1) * 512], pwd[ch][:]),
                            inc=("v", 1) if ch == 1 else None)
                    else:
                        em.op("vector", lambda e, j=j, ch=ch: nc.vector.tensor_tensor(
                            accv[:, j, ch * 512:(ch + 1) * 512], pwd[ch][:],
                            accv[:, j, ch * 512:(ch + 1) * 512], OP.add),
                            inc=("v", 1) if ch == 1 else None)
                v_acc = ("v", em.cnt["v"])
            t_pair_done[gp] = ("t", em.cnt["t"])
        v_res = [None, None]
        for j in range(8):
            em.wait("sync", v_res[j % 2] if j >= 2 else v_ln_read[j % 2])
            d_x3t[j % 2] = em.dma("sync", [lambda e, i=i, j=j: e.dma_start(
                out=x3rv[:, j % 2], in_=x3_h[i * SE + j * 128:i * SE + (j + 1) * 128, :])])
            em.wait("vector", d_x3t[j % 2])
            v_res[j % 2] = em.seq("vector", lambda e, j=j: nc.vector.tensor_tensor(
                accv[:, j], accv[:, j], x3rv[:, j % 2], OP.add))
        v_ln_read = [None, None]
        v_stripe_doneE = em.mark("vector", "v")
        em.wait("sync", v_stripe_doneE)
        d_y_out = em.dma("sync", [lambda e, i=i: e.dma_start(
            out=y_h[i * SE:(i + 1) * SE, :].rearrange("(j p) d -> p j d", p=128),
            in_=accv[:])])

    em.barrier()
    em.drain_dma()
    esEp.close()
    esE.close()

    em.check()
    return nc, em, es


def _run_thunks(nc, em, eng, e):
    marks = getattr(em, "phase_marks", [])
    bounds = [(nm, idx[eng]) for nm, idx in marks] + [("end", len(em.prog[eng]))]
    pos = 0
    for k in range(len(bounds) - 1):
        nm, start = bounds[k]
        stop = bounds[k + 1][1]
        for fn in em.prog[eng][pos:start]:
            fn(e)
        pos = start
        with nc.named_scope(f"ph{nm}"):
            for fn in em.prog[eng][start:stop]:
                fn(e)
        pos = stop
    for fn in em.prog[eng][pos:]:
        fn(e)


def _finish(nc, em, es):
    for h in em.sem.values():
        nc.gpsimd.sem_clear(h)
    nc.all_engine_barrier()
    with nc.Block() as block:
        @block.sync
        def _(sync):
            _run_thunks(nc, em, "sync", sync)

        @block.gpsimd
        def _(g):
            _run_thunks(nc, em, "gpsimd", g)

        @block.tensor
        def _(te):
            _run_thunks(nc, em, "tensor", te)

        @block.scalar
        def _(sc):
            _run_thunks(nc, em, "scalar", sc)

        @block.vector
        def _(v):
            _run_thunks(nc, em, "vector", v)
    es.close()
    return nc


def _run(inputs, stop_after="E", dbg=(), cores=None):
    g = _host_prep(inputs)
    nc, em, es = build_program(stop_after=stop_after, dbg=dbg)
    nc = _finish(nc, em, es)
    cores = cores if cores is not None else list(range(NCORES))
    in_maps = []
    for c in cores:
        m = {"x0": g["x0"][c]}
        for k in INPUT_NAMES:
            m[k] = g[k]
        in_maps.append(m)
    res = run_bass_kernel_spmd(nc, in_maps, core_ids=list(range(len(cores))))
    return res


def kernel(**inputs):
    res = _run(inputs, stop_after="E")
    return np.stack([np.asarray(r["y"], dtype=np.float32) for r in res.results])


# version 05 (final)
# speedup vs baseline: 1.1009x; 1.0521x over previous
import numpy as np
import ml_dtypes
from contextlib import ExitStack

import concourse.bass as bass
import concourse.mybir as mybir
from concourse.bass_utils import run_bass_kernel_spmd

F32 = mybir.dt.float32
BF16 = mybir.dt.bfloat16
I32 = mybir.dt.int32
AF = mybir.ActivationFunctionType
OP = mybir.AluOpType
AX = mybir.AxisListType
BF = ml_dtypes.bfloat16

T = 4096
D = 1024
HID = 1536
NCORES = 8
EPS = 1e-5
C_CONST = 8.0

SA = 512
NSA = T // SA
SB = 256
NSB = T // SB


class EM:
    ENGS = ("sync", "gpsimd", "tensor", "scalar", "vector")
    POOLS = {"sync": 4, "gpsimd": 4, "scalar": 2}

    def __init__(self, nc, es):
        self.nc = nc
        self.prog = {e: [] for e in self.ENGS}
        self.sem = {}
        self.cnt = {}
        self.max_wait = {}
        for name in ("t", "v", "s"):
            self._mk(es, name)
        self.pool = {}
        self.pool_idx = {}
        for q, n in self.POOLS.items():
            self.pool[q] = []
            self.pool_idx[q] = 0
            for i in range(n):
                nm = f"d_{q}_{i}"
                self._mk(es, nm)
                self.pool[q].append(nm)

    def _mk(self, es, name):
        self.sem[name] = es.enter_context(self.nc.semaphore(name))
        self.cnt[name] = 0
        self.max_wait[name] = 0

    def op(self, eng, fn, inc=None):
        if inc is not None:
            name, n = inc
            self.cnt[name] += n
            sem = self.sem[name]
            if eng in ("tensor", "scalar", "vector"):
                def thunk(e, fn=fn, sem=sem, n=n):
                    fn(e)
                    e.drain().then_inc(sem, n)
            else:
                def thunk(e, fn=fn, sem=sem, n=n):
                    fn(e).then_inc(sem, n)
        else:
            def thunk(e, fn=fn):
                fn(e)
        self.prog[eng].append(thunk)
        return (inc[0], self.cnt[inc[0]]) if inc else None

    def wait(self, eng, tgt):
        if tgt is None:
            return
        name, val = tgt
        if val <= 0:
            return
        self.max_wait[name] = max(self.max_wait[name], val)
        sem = self.sem[name]
        self.prog[eng].append(lambda e, sem=sem, v=val: e.wait_ge(sem, v))

    def dma(self, queue, fns):
        nm = self.pool[queue][self.pool_idx[queue]]
        self.pool_idx[queue] = (self.pool_idx[queue] + 1) % len(self.pool[queue])
        self.wait(queue, (nm, self.cnt[nm]))
        tgt = None
        for fn in fns:
            tgt = self.op(queue, fn, inc=(nm, 16))
        return tgt

    def seq(self, eng, fn):
        name = {"tensor": "t", "scalar": "s", "vector": "v"}[eng]
        tgt = self.op(eng, fn, inc=(name, 1))
        self.wait(eng, tgt)
        return tgt

    def mark(self, eng, name):
        self.cnt[name] += 1
        sem = self.sem[name]
        self.prog[eng].append(lambda e, sem=sem: e.drain().then_inc(sem, 1))
        return (name, self.cnt[name])

    def phase(self, name):
        if not hasattr(self, "phase_marks"):
            self.phase_marks = []
        self.phase_marks.append((name, {e: len(self.prog[e]) for e in self.ENGS}))

    def barrier(self):
        targets = [self.mark(eng, n) for eng, n in
                   (("tensor", "t"), ("scalar", "s"), ("vector", "v"))]
        for eng in self.ENGS:
            for tgt in targets:
                self.wait(eng, tgt)

    def drain_dma(self, eng="sync"):
        for q, names in self.pool.items():
            for nm in names:
                self.wait(eng, (nm, self.cnt[nm]))

    def check(self):
        for name, v in self.max_wait.items():
            assert v <= self.cnt[name], \
                f"wait on {name} for {v} but only {self.cnt[name]} emitted"


def _host_prep(inputs):
    g = {}
    x = np.ascontiguousarray(np.asarray(inputs["L0_attn_x"], dtype=np.float32))
    g["x0"] = [np.ascontiguousarray(x[i]) for i in range(NCORES)]

    tr = lambda a: np.ascontiguousarray(np.asarray(a, dtype=np.float32).T)
    fm = lambda a, n: np.ascontiguousarray(
        np.asarray(a, dtype=np.float32).reshape(n, 128).T)

    g["proj_wT"] = tr(inputs["L0_attn_proj_w"])
    cw = np.asarray(inputs["L0_attn_conv_w"], dtype=np.float32).reshape(HID, 4)
    g["conv_w"] = np.ascontiguousarray(
        cw.reshape(12, 128, 4).transpose(1, 0, 2).reshape(128, 48))
    g["conv_b"] = fm(inputs["L0_attn_conv_b"], 12)
    g["ip_wT"] = tr(inputs["L0_attn_ip_w"])
    g["gp_wT"] = tr(inputs["L0_attn_gp_w"])
    g["ip_b"] = fm(inputs["L0_attn_ip_b"], 12)
    g["gp_b"] = fm(inputs["L0_attn_gp_b"], 12)
    g["lam"] = fm(inputs["L0_attn_forget_lambda"], 12)
    g["out_wT"] = tr(inputs["L0_attn_out_w"])

    g["ffn_w"] = np.ascontiguousarray(np.asarray(inputs["L0_ffn_weight"], dtype=np.float32))
    g["ffn_b"] = np.ascontiguousarray(
        np.asarray(inputs["L0_ffn_bias"], dtype=np.float32).reshape(1, D))

    g["w_qkv"] = np.ascontiguousarray(np.asarray(inputs["L1_attn_w_qkv"], dtype=np.float32))
    g["w_proj"] = np.ascontiguousarray(np.asarray(inputs["L1_attn_w_proj"], dtype=np.float32))
    g["b_proj"] = np.ascontiguousarray(
        np.asarray(inputs["L1_attn_b_proj"], dtype=np.float32).reshape(1, D))

    g["router_wT"] = tr(inputs["L1_ffn_router_w"])
    g["router_bb"] = np.ascontiguousarray(np.broadcast_to(
        np.asarray(inputs["L1_ffn_bias"], dtype=np.float32).reshape(1, 8), (128, 8)))
    g["router_b"] = np.ascontiguousarray(
        np.asarray(inputs["L1_ffn_bias"], dtype=np.float32).reshape(1, 8))
    wg = np.asarray(inputs["L1_ffn_wg"], dtype=np.float32)
    wu = np.asarray(inputs["L1_ffn_wu"], dtype=np.float32)
    wd = np.asarray(inputs["L1_ffn_wd"], dtype=np.float32)
    g["wgT"] = np.ascontiguousarray(wg.transpose(0, 2, 1))
    g["wuT"] = np.ascontiguousarray(wu.transpose(0, 2, 1))
    g["wdT"] = np.ascontiguousarray(wd.transpose(0, 2, 1))
    g["sgT"] = tr(inputs["L1_ffn_sg"])
    g["suT"] = tr(inputs["L1_ffn_su"])
    g["sdT"] = tr(inputs["L1_ffn_sd"])

    g["ident"] = np.eye(128, dtype=BF)
    g["ones2"] = np.ones((2, 128), dtype=BF)
    k_idx = np.arange(128)[:, None]
    q_idx = np.arange(128)[None, :]
    g["trimask"] = np.ascontiguousarray(np.concatenate(
        [(k_idx >= q_idx).astype(BF), (k_idx <= q_idx).astype(BF)], axis=1))
    return g


INPUT_NAMES = ["proj_wT", "conv_w", "conv_b", "ip_wT", "gp_wT", "ip_b", "gp_b",
               "lam", "out_wT", "ffn_w", "ffn_b", "w_qkv", "w_proj", "b_proj",
               "router_wT", "router_b", "router_bb", "wgT", "wuT", "wdT", "sgT", "suT", "sdT",
               "ident", "ones2", "trimask"]


def build_program(stop_after="E", dbg=()):
    es = ExitStack()
    nc = bass.Bass("TRN2", target_bir_lowering=False, detect_race_conditions=False)
    em = EM(nc, es)

    def din(name, shape, dtype=F32):
        return nc.dram_tensor(name, list(shape), dtype, kind="ExternalInput").ap()

    def dscratch(name, shape, dtype=F32):
        kind = "ExternalOutput" if name in dbg else "Internal"
        return nc.dram_tensor(name, list(shape), dtype, kind=kind).ap()

    x0_h = din("x0", [T, D])
    proj_wT_h = din("proj_wT", [D, 3072])
    conv_w_h = din("conv_w", [128, 48])
    conv_b_h = din("conv_b", [128, 12])
    ip_wT_h = din("ip_wT", [HID, HID])
    gp_wT_h = din("gp_wT", [HID, HID])
    ip_b_h = din("ip_b", [128, 12])
    gp_b_h = din("gp_b", [128, 12])
    lam_h = din("lam", [128, 12])
    out_wT_h = din("out_wT", [HID, D])
    ffn_w_h = din("ffn_w", [D, D])
    ffn_b_h = din("ffn_b", [1, D])
    w_qkv_h = din("w_qkv", [D, 3072])
    w_proj_h = din("w_proj", [D, D])
    b_proj_h = din("b_proj", [1, D])
    router_wT_h = din("router_wT", [D, 8])
    router_b_h = din("router_b", [1, 8])
    router_bb_h = din("router_bb", [128, 8])
    wgT_h = din("wgT", [8, D, 512])
    wuT_h = din("wuT", [8, D, 512])
    wdT_h = din("wdT", [8, 512, D])
    sgT_h = din("sgT", [D, D])
    suT_h = din("suT", [D, D])
    sdT_h = din("sdT", [D, D])
    ident_h = din("ident", [128, 128], BF16)
    ones2_h = din("ones2", [2, 128], BF16)
    trimask_h = din("trimask", [128, 256], BF16)

    dbgZ_h = dscratch("dbgZ", [12, 128, T], BF16)
    dbgXD_h = dscratch("dbgXD", [T, D], BF16)
    gate_h = dscratch("gateS", [12, 128, T])
    xc_h = dscratch("xcS", [12, 128, T])
    x1_h = dscratch("x1S", [T, D])
    x2_h = dscratch("x2S", [T, D])
    x3_h = dscratch("x3S", [T, D])
    y_h = nc.dram_tensor("y", [T, D], F32, kind="ExternalOutput").ap()

    def sb(stack, name, shape, dt=F32):
        return stack.enter_context(nc.sbuf_tensor(name, shape, dt))

    def pst(stack, name, shape, dt=F32):
        return stack.enter_context(nc.psum_tensor(name, shape, dt))

    ident = sb(es, "identB", [128, 128], BF16)
    ones2 = sb(es, "ones2B", [2, 128], BF16)
    trimask = sb(es, "trimaskB", [128, 256], BF16)
    cst = sb(es, "cst", [128, 80])
    gpb = sb(es, "gpb", [128, 24])
    stats = sb(es, "stats", [128, 16])
    dump = sb(es, "dump", [128, D])
    kc = sb(es, "kc", [128, 4])

    d_init = em.dma("sync", [
        lambda e: e.dma_start(out=ident[:], in_=ident_h[:]),
        lambda e: e.dma_start(out=ones2[:], in_=ones2_h[:]),
        lambda e: e.dma_start(out=trimask[:], in_=trimask_h[:]),
        lambda e: e.dma_start(out=cst[:, 0:48], in_=conv_w_h[:]),
        lambda e: e.dma_start(out=cst[:, 48:60], in_=conv_b_h[:]),
        lambda e: e.dma_start(out=gpb[:, 0:12], in_=gp_b_h[:]),
        lambda e: e.dma_start(out=gpb[:, 12:24], in_=ip_b_h[:]),
    ])
    lamt = sb(es, "lamt", [128, 12])
    d_lam = em.dma("sync", [lambda e: e.dma_start(out=lamt[:], in_=lam_h[:])])

    em.op("vector", lambda e: nc.vector.memset(kc[:, 0:1], EPS))
    em.op("vector", lambda e: nc.vector.memset(kc[:, 1:2], 1.0 + 1e-6))
    em.op("vector", lambda e: nc.vector.memset(kc[:, 2:3], 1e-6), inc=("v", 1))

    em.wait("scalar", d_lam)
    ctmp = sb(es, "ctmp", [128, 12])
    s_c1 = em.seq("scalar", lambda e: nc.scalar.activation(ctmp[:], lamt[:], AF.Exp))
    em.wait("vector", s_c1)
    v_c1 = em.seq("vector", lambda e: nc.vector.tensor_scalar(
        ctmp[:], ctmp[:], 1.0, None, OP.add))
    em.wait("scalar", v_c1)
    s_c2 = em.op("scalar", lambda e: nc.scalar.activation(lamt[:], ctmp[:], AF.Ln),
                 inc=("s", 1))
    em.wait("vector", s_c2)
    v_c2 = em.op("vector", lambda e: nc.vector.tensor_scalar(
        cst[:, 60:72], lamt[:], -C_CONST, None, OP.mult), inc=("v", 1))
    em.wait("scalar", v_c2)
    em.wait("tensor", v_c2)
    em.wait("scalar", d_init)
    em.wait("vector", d_init)
    em.wait("tensor", d_init)

    def emit_ln(x_ap, out_bf_ap):
        sumx = stats[:, 0:1]; sumsq = stats[:, 1:2]; m = stats[:, 2:3]
        var = stats[:, 3:4]; r = stats[:, 4:5]; nmr = stats[:, 5:6]
        u = stats[:, 6:7]; mm = stats[:, 7:8]
        em.op("vector", lambda e: nc.vector.tensor_reduce(sumx, x_ap, AX.X, OP.add),
              inc=("v", 1))
        sv = em.op("scalar", lambda e: nc.scalar.activation(
            dump[:], x_ap, AF.Square, accum_out=sumsq), inc=("s", 1))
        em.wait("vector", sv)
        em.seq("vector", lambda e: nc.vector.tensor_scalar(m, sumx, 1.0 / D, None, OP.mult))
        em.seq("vector", lambda e: nc.vector.tensor_scalar(var, sumsq, 1.0 / D, None, OP.mult))
        em.seq("vector", lambda e: nc.vector.tensor_tensor(mm, m, m, OP.mult))
        vv = em.op("vector", lambda e: nc.vector.tensor_tensor(var, var, mm, OP.subtract),
                   inc=("v", 1))
        em.wait("scalar", vv)
        em.seq("scalar", lambda e: nc.scalar.activation(u, var, AF.Ln, bias=kc[:, 0:1]))
        sv2 = em.op("scalar", lambda e: nc.scalar.activation(r, u, AF.Exp, scale=-0.5),
                    inc=("s", 1))
        em.wait("vector", sv2)
        em.seq("vector", lambda e: nc.vector.scalar_tensor_tensor(
            nmr, m, -1.0, r, OP.mult, OP.mult))
        vap = em.op("vector", lambda e: nc.vector.tensor_scalar(
            dump[:], x_ap, r, nmr, OP.mult, OP.add), inc=("v", 1))
        em.wait("scalar", vap)
        return em.op("scalar", lambda e: nc.scalar.copy(out_bf_ap, dump[:]),
                     inc=("s", 1))

    em.phase("A")
    esA = ExitStack()
    pw = sb(esA, "pwA", [128, 8 * 3072], BF16)
    dGw = em.dma("gpsimd", [
        (lambda dt: lambda e: e.dma_start(
            out=pw[:, dt * 3072:(dt + 1) * 3072],
            in_=proj_wT_h[dt * 128:(dt + 1) * 128, :]))(dt)
        for dt in range(8)])

    x0b = sb(esA, "x0bA", [128, 2 * 4 * D])
    x0v = x0b[:].rearrange("p (b j d) -> p b j d", b=2, j=4)
    xlnb = sb(esA, "xlnbA", [128, D], BF16)
    xlnT = sb(esA, "xlnTA", [128, 8 * SA], BF16)
    xlnTv = xlnT[:].rearrange("p (dt t) -> p dt t", dt=8)
    gateb = sb(esA, "gatebA", [128, 12 * SA])
    gatev = gateb[:].rearrange("p (h t) -> p h t", h=12)
    xhb = sb(esA, "xhbA", [128, 12 * (SA + 3)])
    xhv = xhb[:].rearrange("p (h t) -> p h t", h=12)
    xcb = sb(esA, "xcbA", [128, 12 * SA])
    xcv = xcb[:].rearrange("p (h t) -> p h t", h=12)


    esAp = ExitStack()
    ptr = pst(esAp, "ptrA", [128, 1024], BF16)
    pgx = [pst(esAp, f"pgxA{i}", [128, SA]) for i in range(4)]

    em.wait("tensor", dGw)

    v_ln_done = [None] * NSA
    d_out_done = [None] * NSA

    for i in range(NSA):
        buf = i % 2
        if i >= 2:
            em.wait("sync", v_ln_done[i - 2])
        d_in = em.dma("sync", [lambda e, i=i, buf=buf: e.dma_start(
            out=x0v[:, buf], in_=x0_h[i * SA:(i + 1) * SA, :].rearrange(
                "(j p) d -> p j d", p=128))])

        em.wait("vector", d_in)
        em.wait("scalar", d_in)
        s_evac = None
        for j in range(4):
            if not (i == 0 and j == 0):
                em.wait("scalar", ("t", em.cnt["t"]))
            s_apply = emit_ln(x0v[:, buf, j], xlnb[:])
            em.wait("tensor", s_apply)
            em.wait("tensor", s_evac)
            for dt in range(8):
                em.op("tensor", lambda e, dt=dt: nc.tensor.transpose(
                    ptr[:, dt * 128:(dt + 1) * 128],
                    xlnb[:, dt * 128:(dt + 1) * 128], ident[:]))
            tt = em.mark("tensor", "t")
            em.wait("scalar", tt)
            s_evac = em.op("scalar", lambda e, j=j: nc.scalar.copy(
                xlnTv[:, :, j * 128:(j + 1) * 128],
                ptr[:].rearrange("p (dt t) -> p dt t", dt=8)), inc=("s", 1))
        v_ln_done[i] = ("v", em.cnt["v"])
        s_xlnT_done = s_evac

        if i == 0:
            em.op("vector", lambda e: nc.vector.memset(xhv[:, :, 0:3], 0.0), inc=("v", 1))
        else:
            em.op("vector", lambda e: nc.vector.tensor_copy(
                xhv[:, :, 0:3], xhv[:, :, SA:SA + 3]), inc=("v", 1))

        em.wait("tensor", ("v", em.cnt["v"]))
        em.wait("tensor", s_xlnT_done)
        if i > 0:
            em.wait("scalar", d_out_done[i - 1])
            em.wait("vector", d_out_done[i - 1])
        s_base = em.cnt["s"]
        v_base = em.cnt["v"]
        for f in range(24):
            bank = pgx[f % 4]
            if f >= 4:
                fp = f - 4
                if fp < 12:
                    em.wait("tensor", ("s", s_base + fp + 1))
                else:
                    em.wait("tensor", ("v", v_base + (fp - 12) + 1))
            for dt in range(8):
                em.op("tensor", lambda e, f=f, dt=dt, bank=bank: nc.tensor.matmul(
                    bank[:], pw[:, dt * 3072 + f * 128: dt * 3072 + (f + 1) * 128],
                    xlnTv[:, dt], start=(dt == 0), stop=(dt == 7)))
            tt = em.mark("tensor", "t")
            if f < 12:
                em.wait("scalar", tt)
                em.op("scalar", lambda e, f=f, bank=bank: nc.scalar.activation(
                    gatev[:, f], bank[:], AF.Gelu), inc=("s", 1))
            else:
                em.wait("vector", tt)
                em.op("vector", lambda e, f=f, bank=bank: nc.vector.tensor_copy(
                    xhv[:, f - 12, 3:3 + SA], bank[:]), inc=("v", 1))
        s_gate_done = ("s", em.cnt["s"])

        for h in range(12):
            em.seq("vector", lambda e, h=h: nc.vector.tensor_scalar(
                xcv[:, h], xhv[:, h, 0:SA], cst[:, 4 * h:4 * h + 1],
                cst[:, 48 + h:49 + h], OP.mult, OP.add))
            for k in range(1, 4):
                em.seq("vector", lambda e, h=h, k=k: nc.vector.scalar_tensor_tensor(
                    xcv[:, h], xhv[:, h, k:k + SA], cst[:, 4 * h + k:4 * h + k + 1],
                    xcv[:, h], OP.mult, OP.add))
        v_conv_done = em.mark("vector", "v")

        em.wait("sync", s_gate_done)
        em.wait("sync", v_conv_done)
        fns = []
        for h in range(12):
            fns.append(lambda e, i=i, h=h: e.dma_start(
                out=gate_h[h, :, i * SA:(i + 1) * SA], in_=gatev[:, h]))
            fns.append(lambda e, i=i, h=h: e.dma_start(
                out=xc_h[h, :, i * SA:(i + 1) * SA], in_=xcv[:, h]))
        d_out_done[i] = em.dma("sync", fns)

    em.barrier()
    em.drain_dma()
    esAp.close()
    esA.close()

    if stop_after == "A":
        em.check()
        return nc, em, es

    em.phase("B")
    esB = ExitStack()
    gw = sb(esB, "gwB", [128, 12 * HID], BF16)
    iw = sb(esB, "iwB", [128, 12 * HID], BF16)
    ow = sb(esB, "owB", [128, 12 * D], BF16)
    fns = []
    for ht in range(12):
        fns.append(lambda e, ht=ht: e.dma_start(
            out=gw[:, ht * HID:(ht + 1) * HID],
            in_=gp_wT_h[ht * 128:(ht + 1) * 128, :]))
        fns.append(lambda e, ht=ht: e.dma_start(
            out=iw[:, ht * HID:(ht + 1) * HID],
            in_=ip_wT_h[ht * 128:(ht + 1) * 128, :]))
        fns.append(lambda e, ht=ht: e.dma_start(
            out=ow[:, ht * D:(ht + 1) * D],
            in_=out_wT_h[ht * 128:(ht + 1) * 128, :]))
    dGw = em.dma("gpsimd", fns)

    xcf = sb(esB, "xcfB", [128, 12 * SB])
    xcfv = xcf[:].rearrange("p (h t) -> p h t", h=12)
    xcB = sb(esB, "xcBB", [128, 12 * SB], BF16)
    xcBv = xcB[:].rearrange("p (h t) -> p h t", h=12)
    gteb = sb(esB, "gtebB", [128, 12 * SB])
    gtev = gteb[:].rearrange("p (h t) -> p h t", h=12)
    x0c = sb(esB, "x0cB", [128, 2 * D])
    x0cv = x0c[:].rearrange("p (j d) -> p j d", j=2)
    recF = sb(esB, "recFB", [128, 12 * SB])
    recFv = recF[:].rearrange("p (g t) -> p g t", g=12)
    inpF = sb(esB, "inpFB", [128, 12 * SB])
    inpFv = inpF[:].rearrange("p (g t) -> p g t", g=12)
    alfb = sb(esB, "alfbB", [128, SB])
    t3b = sb(esB, "t3bB", [128, SB])
    t6b = sb(esB, "t6bB", [128, SB])
    zBb = sb(esB, "zBbB", [128, 12 * SB], BF16)
    zBv = zBb[:].rearrange("p (h t) -> p h t", h=12)
    x1b = sb(esB, "x1bB", [128, 2 * D])
    x1bv = x1b[:].rearrange("p (j d) -> p j d", j=2)
    carry = sb(esB, "carryB", [128, 12])
    em.op("vector", lambda e: nc.vector.memset(carry[:], 0.0), inc=("v", 1))

    esBp = ExitStack()
    prec = [pst(esBp, f"precB{i}", [128, 512]) for i in range(4)]
    pout = [pst(esBp, f"poutB{i}", [128, 512]) for i in range(4)]

    em.wait("tensor", dGw)

    v_stripe_done = None
    d_x1_out = None
    for i in range(NSB):
        if i >= 1:
            em.wait("sync", v_stripe_done)
        fns = []
        for h in range(12):
            fns.append(lambda e, i=i, h=h: e.dma_start(
                out=xcfv[:, h], in_=xc_h[h, :, i * SB:(i + 1) * SB]))
            fns.append(lambda e, i=i, h=h: e.dma_start(
                out=gtev[:, h], in_=gate_h[h, :, i * SB:(i + 1) * SB]))
        fns.append(lambda e, i=i: e.dma_start(
            out=x0cv[:],
            in_=x0_h[i * SB:(i + 1) * SB, :].rearrange("(j p) d -> p j d", p=128)))
        d_in = em.dma("sync", fns)

        em.wait("vector", d_in)
        v_xcB = em.op("vector", lambda e: nc.vector.tensor_copy(xcB[:], xcf[:]),
                      inc=("v", 1))

        em.wait("tensor", v_xcB)
        s_base = em.cnt["s"]
        for gt in range(12):
            for proj in range(2):
                bank = prec[(2 * gt + proj) % 4]
                if 2 * gt + proj >= 4:
                    em.wait("tensor", ("s", s_base + (2 * gt + proj) - 4 + 1))
                w = gw if proj == 0 else iw
                for ht in range(12):
                    em.op("tensor", lambda e, w=w, ht=ht, gt=gt, bank=bank:
                          nc.tensor.matmul(
                              bank[:, 0:SB],
                              w[:, ht * HID + gt * 128: ht * HID + (gt + 1) * 128],
                              xcBv[:, ht], start=(ht == 0), stop=(ht == 11)))
                tt = em.mark("tensor", "t")
                em.wait("scalar", tt)
                dst = recFv[:, gt] if proj == 0 else inpFv[:, gt]
                bias = gpb[:, gt:gt + 1] if proj == 0 else gpb[:, 12 + gt:13 + gt]
                em.op("scalar", lambda e, dst=dst, bank=bank, bias=bias:
                      nc.scalar.activation(dst, bank[:, 0:SB], AF.Sigmoid, bias=bias),
                      inc=("s", 1))
        s_sig_done = ("s", em.cnt["s"])

        em.wait("vector", s_sig_done)
        em.wait("scalar", s_sig_done)
        for gt in range(12):
            em.wait("scalar", ("v", em.cnt["v"]))
            sv = em.op("scalar", lambda e, gt=gt: nc.scalar.activation(
                alfb[:], recFv[:, gt], AF.Exp, scale=cst[:, 60 + gt:61 + gt]),
                inc=("s", 1))
            em.wait("vector", sv)
            vv = em.op("vector", lambda e: nc.vector.tensor_tensor(
                t3b[:], alfb[:], alfb[:], OP.mult), inc=("v", 1))
            em.seq("vector", lambda e: nc.vector.tensor_scalar(
                t3b[:], t3b[:], -1.0, 1.0, OP.mult, OP.add))
            em.wait("scalar", vv)
            em.wait("scalar", ("v", em.cnt["v"]))
            em.seq("scalar", lambda e: nc.scalar.activation(
                t6b[:], t3b[:], AF.Ln, scale=1.0, bias=kc[:, 2:3]))
            sv2 = em.op("scalar", lambda e: nc.scalar.activation(
                t3b[:], t6b[:], AF.Exp, scale=0.5), inc=("s", 1))
            em.wait("vector", sv2)
            em.seq("vector", lambda e, gt=gt: nc.vector.tensor_tensor(
                t6b[:], t3b[:], inpFv[:, gt], OP.mult))
            em.seq("vector", lambda e, gt=gt: nc.vector.tensor_tensor(
                t6b[:], t6b[:], xcfv[:, gt], OP.mult))
            em.seq("vector", lambda e, gt=gt: nc.vector.tensor_tensor_scan(
                t3b[:], alfb[:], t6b[:], carry[:, gt:gt + 1], OP.mult, OP.add))
            em.seq("vector", lambda e, gt=gt: nc.vector.tensor_copy(
                carry[:, gt:gt + 1], t3b[:, SB - 1:SB]))
            em.seq("vector", lambda e, gt=gt: nc.vector.tensor_tensor(
                t3b[:], t3b[:], gtev[:, gt], OP.mult))
            em.op("vector", lambda e, gt=gt: nc.vector.tensor_copy(
                zBv[:, gt], t3b[:]), inc=("v", 1))
        v_zB = ("v", em.cnt["v"])

        if "dbgZ" in dbg:
            em.wait("sync", ("v", em.cnt["v"]))
            em.dma("sync", [(lambda h: lambda e, i=i: e.dma_start(
                out=dbgZ_h[h, :, i * SB:(i + 1) * SB], in_=zBv[:, h]))(h)
                for h in range(12)])
        em.wait("tensor", v_zB)
        if i > 0:
            em.wait("tensor", ("v", em.cnt["v"]))
        for j in range(2):
            for ch in range(2):
                bank = pout[2 * j + ch]
                for ht in range(12):
                    em.op("tensor", lambda e, j=j, ch=ch, ht=ht, bank=bank:
                          nc.tensor.matmul(
                              bank[:],
                              zBv[:, ht, j * 128:(j + 1) * 128],
                              ow[:, ht * D + ch * 512: ht * D + (ch + 1) * 512],
                              start=(ht == 0), stop=(ht == 11)))
        tt = em.mark("tensor", "t")
        em.wait("vector", tt)
        if i > 0:
            em.wait("vector", d_x1_out)
        for j in range(2):
            for ch in range(2):
                em.op("vector", lambda e, j=j, ch=ch: nc.vector.tensor_tensor(
                    x1bv[:, j, ch * 512:(ch + 1) * 512], pout[2 * j + ch][:],
                    x0cv[:, j, ch * 512:(ch + 1) * 512], OP.add))
        v_x1 = em.mark("vector", "v")
        v_stripe_done = v_x1

        em.wait("sync", v_x1)
        d_x1_out = em.dma("sync", [lambda e, i=i: e.dma_start(
            out=x1_h[i * SB:(i + 1) * SB, :].rearrange("(j p) d -> p j d", p=128),
            in_=x1bv[:])])

    em.barrier()
    em.drain_dma()
    esBp.close()
    esB.close()

    if stop_after == "B":
        em.check()
        return nc, em, es

    em.phase("C")
    SC = 512
    NSC = T // SC
    esC = ExitStack()
    wrow = sb(esC, "wrowC", [128, D])
    wqb = sb(esC, "wqbC", [128, D], BF16)
    wdT = sb(esC, "wdTC", [128, 8 * D], BF16)
    wdTv = wdT[:].rearrange("p (dt d) -> p dt d", dt=8)
    bhlC = sb(esC, "bhlC", [2, D], BF16)
    tmprow = sb(esC, "tmprowC", [1, D])
    x1c = sb(esC, "x1cC", [128, 4 * D])
    x1cv = x1c[:].rearrange("p (j d) -> p j d", j=4)
    xlnf = sb(esC, "xlnfC", [128, D])
    xqb = sb(esC, "xqbC", [128, D], BF16)
    xdT = sb(esC, "xdTC", [128, 8 * 128], BF16)
    xdTv = xdT[:].rearrange("p (dt t) -> p dt t", dt=8)
    x2b = sb(esC, "x2bC", [128, 4 * D])
    x2bv = x2b[:].rearrange("p (j d) -> p j d", j=4)
    qB1 = sb(esC, "qB1C", [128, D])
    qB2 = sb(esC, "qB2C", [128, D])
    qB3 = sb(esC, "qB3C", [128, D])
    qB4 = sb(esC, "qB4C", [128, D])
    qI = sb(esC, "qIC", [128, D], I32)
    qS = sb(esC, "qSC", [128, 96])

    esCp = ExitStack()
    ptrC = pst(esCp, "ptrC", [128, 1024], BF16)
    poutC = [pst(esCp, f"poutC{i}", [128, 512]) for i in range(4)]

    def emit_quantize(x_f32_ap, out_bf_ap):
        amax = qS[:, 0:32]; qsc = qS[:, 32:64]; qrs = qS[:, 64:96]
        xv = x_f32_ap.rearrange("p (a b) -> p a b", b=32)
        em.seq("vector", lambda e: nc.vector.tensor_reduce(
            amax, xv, AX.X, OP.max, apply_absolute_value=True))
        em.seq("vector", lambda e: nc.vector.tensor_scalar(
            qsc, amax, 1.0 / 7.5, 1e-30, OP.mult, OP.max))
        em.seq("vector", lambda e: nc.vector.reciprocal(qrs, qsc))
        em.seq("vector", lambda e: nc.vector.tensor_tensor(
            qB1[:].rearrange("p (a b) -> p a b", b=32), xv,
            qrs[:, :, None].broadcast_to([128, 32, 32]), OP.mult))
        def rhd(mult, dst):
            em.seq("vector", lambda e: nc.vector.tensor_scalar(
                qB4[:], qB1[:], float(mult), None, OP.mult))
            em.seq("vector", lambda e: nc.vector.tensor_copy(qI[:], qB4[:]))
            em.seq("vector", lambda e: nc.vector.tensor_copy(dst, qI[:]))
            em.seq("vector", lambda e: nc.vector.tensor_tensor(
                qB4[:], qB4[:], dst, OP.subtract))
            em.seq("vector", lambda e: nc.vector.scalar_tensor_tensor(
                dst, qB4[:], 0.5, dst, OP.is_gt, OP.add))
            em.seq("vector", lambda e: nc.vector.tensor_scalar(
                qB4[:], qB4[:], -0.5, None, OP.is_le))
            em.seq("vector", lambda e: nc.vector.tensor_tensor(
                dst, dst, qB4[:], OP.subtract))
            em.seq("vector", lambda e: nc.vector.tensor_scalar(
                dst, dst, 1.0 / mult, None, OP.mult))
        rhd(8.0, qB2[:])
        rhd(4.0, qB3[:])
        em.seq("vector", lambda e: nc.vector.tensor_tensor(qB4[:], qB1[:], qB1[:], OP.mult))
        em.seq("vector", lambda e: nc.vector.tensor_scalar(
            qB4[:], qB4[:], 4.0, None, OP.is_ge))
        em.seq("vector", lambda e: nc.vector.tensor_tensor(qB3[:], qB3[:], qB2[:], OP.subtract))
        em.seq("vector", lambda e: nc.vector.tensor_tensor(qB3[:], qB3[:], qB4[:], OP.mult))
        em.seq("vector", lambda e: nc.vector.tensor_tensor(qB2[:], qB2[:], qB3[:], OP.add))
        rhd(2.0, qB3[:])
        em.seq("vector", lambda e: nc.vector.tensor_tensor(qB4[:], qB1[:], qB1[:], OP.mult))
        em.seq("vector", lambda e: nc.vector.tensor_scalar(
            qB4[:], qB4[:], 16.0, None, OP.is_ge))
        em.seq("vector", lambda e: nc.vector.tensor_tensor(qB3[:], qB3[:], qB2[:], OP.subtract))
        em.seq("vector", lambda e: nc.vector.tensor_tensor(qB3[:], qB3[:], qB4[:], OP.mult))
        em.seq("vector", lambda e: nc.vector.tensor_tensor(qB2[:], qB2[:], qB3[:], OP.add))
        vq = em.op("vector", lambda e: nc.vector.tensor_tensor(
            qB1[:].rearrange("p (a b) -> p a b", b=32),
            qB2[:].rearrange("p (a b) -> p a b", b=32),
            qsc[:, :, None].broadcast_to([128, 32, 32]), OP.mult), inc=("v", 1))
        em.wait("scalar", vq)
        return em.op("scalar", lambda e: nc.scalar.copy(out_bf_ap, qB1[:]), inc=("s", 1))

    def emit_ln_f32(x_ap, out_f32_ap):
        sumx = stats[:, 0:1]; sumsq = stats[:, 1:2]; m = stats[:, 2:3]
        var = stats[:, 3:4]; r = stats[:, 4:5]; nmr = stats[:, 5:6]
        u = stats[:, 6:7]; mm = stats[:, 7:8]
        em.op("vector", lambda e: nc.vector.tensor_reduce(sumx, x_ap, AX.X, OP.add),
              inc=("v", 1))
        sv = em.op("scalar", lambda e: nc.scalar.activation(
            dump[:], x_ap, AF.Square, accum_out=sumsq), inc=("s", 1))
        em.wait("vector", sv)
        em.seq("vector", lambda e: nc.vector.tensor_scalar(m, sumx, 1.0 / D, None, OP.mult))
        em.seq("vector", lambda e: nc.vector.tensor_scalar(var, sumsq, 1.0 / D, None, OP.mult))
        em.seq("vector", lambda e: nc.vector.tensor_tensor(mm, m, m, OP.mult))
        vv = em.op("vector", lambda e: nc.vector.tensor_tensor(var, var, mm, OP.subtract),
                   inc=("v", 1))
        em.wait("scalar", vv)
        em.seq("scalar", lambda e: nc.scalar.activation(u, var, AF.Ln, bias=kc[:, 0:1]))
        sv2 = em.op("scalar", lambda e: nc.scalar.activation(r, u, AF.Exp, scale=-0.5),
                    inc=("s", 1))
        em.wait("vector", sv2)
        em.seq("vector", lambda e: nc.vector.scalar_tensor_tensor(
            nmr, m, -1.0, r, OP.mult, OP.mult))
        return em.op("vector", lambda e: nc.vector.tensor_scalar(
            out_f32_ap, x_ap, r, nmr, OP.mult, OP.add), inc=("v", 1))

    lorow = sb(esC, "lorowC", [1, D], BF16)

    def emit_bias_hilo(src_h, dst_bhl):
        d_b = em.dma("sync", [lambda e: e.dma_start(out=tmprow[:], in_=src_h[:])])
        em.wait("scalar", d_b)
        sh = em.op("scalar", lambda e: nc.scalar.copy(dst_bhl[0:1, :], tmprow[:]),
                   inc=("s", 1))
        em.wait("vector", sh)
        em.seq("vector", lambda e: nc.vector.tensor_copy(dump[0:1, :], dst_bhl[0:1, :]))
        vlo = em.op("vector", lambda e: nc.vector.tensor_tensor(
            lorow[:], tmprow[:], dump[0:1, :], OP.subtract), inc=("v", 1))
        em.wait("sync", vlo)
        return em.dma("sync", [lambda e: e.dma_start(out=dst_bhl[1:2, :], in_=lorow[:])])

    v_bC = emit_bias_hilo(ffn_b_h, bhlC)

    s_wq_evac = None
    s_q = None
    for rtile in range(8):
        em.wait("sync", s_q)
        d_w = em.dma("sync", [lambda e, rtile=rtile: e.dma_start(
            out=wrow[:], in_=ffn_w_h[rtile * 128:(rtile + 1) * 128, :])])
        em.wait("vector", d_w)
        s_q = emit_quantize(wrow[:], wqb[:])
        em.wait("tensor", s_q)
        em.wait("tensor", s_wq_evac)
        for dt in range(8):
            em.op("tensor", lambda e, dt=dt: nc.tensor.transpose(
                ptrC[:, dt * 128:(dt + 1) * 128],
                wqb[:, dt * 128:(dt + 1) * 128], ident[:]))
        tt = em.mark("tensor", "t")
        em.wait("scalar", tt)
        s_wq_evac = em.op("scalar", lambda e, rtile=rtile: nc.scalar.copy(
            wdTv[:, :, rtile * 128:(rtile + 1) * 128],
            ptrC[:].rearrange("p (dt t) -> p dt t", dt=8)), inc=("s", 1))

    em.wait("tensor", v_bC)
    em.wait("tensor", s_wq_evac)
    v_ln_doneC = None
    d_x2_out = None
    s_xdT = None
    for i in range(NSC):
        if i >= 1:
            em.wait("sync", v_ln_doneC)
        d_in = em.dma("sync", [lambda e, i=i: e.dma_start(
            out=x1cv[:], in_=x1_h[i * SC:(i + 1) * SC, :].rearrange(
                "(j p) d -> p j d", p=128))])
        em.wait("vector", d_in)
        em.wait("scalar", d_in)
        for j in range(4):
            emit_ln_f32(x1cv[:, j], xlnf[:])
            s_q = emit_quantize(xlnf[:], xqb[:])
            if "dbgXD" in dbg:
                em.wait("sync", s_q)
                em.dma("sync", [lambda e, i=i, j=j: e.dma_start(
                    out=dbgXD_h[i * SC + j * 128: i * SC + (j + 1) * 128, :],
                    in_=xqb[:])])
            em.wait("tensor", s_q)
            em.wait("tensor", s_xdT)
            for dt in range(8):
                em.op("tensor", lambda e, dt=dt: nc.tensor.transpose(
                    ptrC[:, dt * 128:(dt + 1) * 128],
                    xqb[:, dt * 128:(dt + 1) * 128], ident[:]))
            tt = em.mark("tensor", "t")
            em.wait("scalar", tt)
            s_xdT = em.op("scalar", lambda e: nc.scalar.copy(
                xdTv[:], ptrC[:].rearrange("p (dt t) -> p dt t", dt=8)), inc=("s", 1))
            em.wait("tensor", s_xdT)
            if i > 0 or j > 0:
                em.wait("tensor", ("v", em.cnt["v"]))
            for ch in range(2):
                bank = poutC[2 * (j % 2) + ch]
                for dt in range(8):
                    em.op("tensor", lambda e, dt=dt, ch=ch, bank=bank: nc.tensor.matmul(
                        bank[:], xdTv[:, dt],
                        wdTv[:, dt, ch * 512:(ch + 1) * 512],
                        start=(dt == 0), stop=False))
                em.op("tensor", lambda e, ch=ch, bank=bank: nc.tensor.matmul(
                    bank[:], ones2[:], bhlC[:, ch * 512:(ch + 1) * 512],
                    start=False, stop=True))
            tt2 = em.mark("tensor", "t")
            em.wait("vector", tt2)
            if i > 0 and j == 0:
                em.wait("vector", d_x2_out)
            for ch in range(2):
                em.op("vector", lambda e, j=j, ch=ch: nc.vector.tensor_tensor(
                    x2bv[:, j, ch * 512:(ch + 1) * 512], poutC[2 * (j % 2) + ch][:],
                    x1cv[:, j, ch * 512:(ch + 1) * 512], OP.add),
                    inc=("v", 1) if ch == 1 else None)
        v_ln_doneC = ("v", em.cnt["v"])
        em.wait("sync", v_ln_doneC)
        d_x2_out = em.dma("sync", [lambda e, i=i: e.dma_start(
            out=x2_h[i * SC:(i + 1) * SC, :].rearrange("(j p) d -> p j d", p=128),
            in_=x2bv[:])])

    em.barrier()
    em.drain_dma()
    esCp.close()
    esC.close()

    if stop_after == "C":
        em.check()
        return nc, em, es

    em.phase("D")
    CD = 512
    NCD = T // CD
    esD = ExitStack()
    wqkv = sb(esD, "wqkvD", [128, 8 * 3072], BF16)
    wqkvv = wqkv[:].rearrange("p (dt f) -> p dt f", dt=8)
    wpro = sb(esD, "wproD", [128, 8 * D], BF16)
    wprov = wpro[:].rearrange("p (dt d) -> p dt d", dt=8)
    bhlD = sb(esD, "bhlD", [2, D], BF16)
    tmprowD = sb(esD, "tmprowD", [1, D])
    x2r = sb(esD, "x2rD", [128, 2 * 4 * D])
    x2rv = x2r[:].rearrange("p (c j d) -> p c j d", c=2, j=4)
    xlnbD = sb(esD, "xlnbD", [128, D], BF16)
    xlnTD = sb(esD, "xlnTD", [128, 8 * CD], BF16)
    xlnTDv = xlnTD[:].rearrange("p (dt t) -> p dt t", dt=8)
    qF = sb(esD, "qFD", [128, 8 * 1024], BF16)
    qFv = qF[:].rearrange("p (f t) -> p f t", f=8)
    kF = sb(esD, "kFD", [128, 8 * 1024], BF16)
    kFv = kF[:].rearrange("p (f t) -> p f t", f=8)
    vT = sb(esD, "vTD", [128, 8 * 16 * 65], BF16)
    vTv = vT[:].rearrange("p (r h c) -> p r h c", r=8, h=16)
    expb = sb(esD, "expbD", [128, 2 * 384], BF16)
    aob = sb(esD, "aobD", [128, D])
    aobb = sb(esD, "aobbD", [128, D], BF16)
    aoT = sb(esD, "aoTD", [128, 8 * 128], BF16)
    aoTv = aoT[:].rearrange("p (dt t) -> p dt t", dt=8)
    x3b = sb(esD, "x3bD", [128, D])
    rcol = sb(esD, "rcolD", [128, 1])

    esDp = ExitStack()
    ptrD = pst(esDp, "ptrD", [128, 1024], BF16)
    pqk = [pst(esDp, f"pqkD{i}", [128, 512]) for i in range(2)]
    psc = [pst(esDp, f"pscD{i}", [128, 384]) for i in range(2)]
    pav = [pst(esDp, f"pavD{i}", [128, 65]) for i in range(2)]
    ppr = pst(esDp, "pprD", [128, 512])

    fns = []
    for dt in range(8):
        fns.append(lambda e, dt=dt: e.dma_start(
            out=wqkvv[:, dt], in_=w_qkv_h[dt * 128:(dt + 1) * 128, :]))
        fns.append(lambda e, dt=dt: e.dma_start(
            out=wprov[:, dt], in_=w_proj_h[dt * 128:(dt + 1) * 128, :]))
    dGw = em.dma("gpsimd", fns)
    lorowD = sb(esD, "lorowD", [1, D], BF16)
    d_b = em.dma("sync", [lambda e: e.dma_start(out=tmprowD[:], in_=b_proj_h[:])])
    em.wait("scalar", d_b)
    sh = em.op("scalar", lambda e: nc.scalar.copy(bhlD[0:1, :], tmprowD[:]), inc=("s", 1))
    em.wait("vector", sh)
    em.seq("vector", lambda e: nc.vector.tensor_copy(dump[0:1, :], bhlD[0:1, :]))
    vloD = em.op("vector", lambda e: nc.vector.tensor_tensor(
        lorowD[:], tmprowD[:], dump[0:1, :], OP.subtract), inc=("v", 1))
    em.wait("sync", vloD)
    v_bD = em.dma("sync", [lambda e: e.dma_start(out=bhlD[1:2, :], in_=lorowD[:])])
    v_ones = em.op("vector", lambda e: nc.vector.memset(vTv[:, :, :, 64:65], 1.0),
                   inc=("v", 1))
    em.wait("tensor", dGw)
    em.wait("tensor", v_bD)
    em.wait("tensor", v_ones)

    s_qk_evac = [None] * 2
    s_exp = [None] * 2
    v_attn_done = None
    s_attn_done = None
    d_x3_out = None
    v_ln_doneD = None
    s_ao_evac = None
    s_aocast = None

    def emit_attn_block(n):
        nonlocal v_attn_done, s_attn_done, d_x3_out, s_ao_evac, s_aocast
        rn = n % 8
        em.wait("vector", s_aocast)
        em.wait("vector", d_x3_out)
        for h in range(16):
            pb = psc[h % 2]
            eb = expb[:, (h % 2) * 384:(h % 2 + 1) * 384]
            ebv = expb[:].rearrange("p (b k q) -> p b k q", b=2, k=3)[:, h % 2]
            em.wait("tensor", s_exp[h % 2])
            for kb in range(3):
                nk = n - 1 + kb
                if nk < 0 or nk > 31:
                    continue
                rk = nk % 8
                hp = (h % 2) * 64
                ft = h // 2
                em.op("tensor", lambda e, kb=kb, rk=rk, hp=hp, ft=ft, pb=pb:
                      nc.tensor.matmul(
                          pb[:, kb * 128:(kb + 1) * 128],
                          kFv[hp:hp + 64, ft, rk * 128:(rk + 1) * 128],
                          qFv[hp:hp + 64, ft, rn * 128:(rn + 1) * 128],
                          start=True, stop=True))
            tt = em.mark("tensor", "t")
            em.wait("scalar", tt)
            lo = 128 if n == 0 else 0
            hi = 256 if n == 31 else 384
            s_e = em.op("scalar", lambda e, pb=pb, eb=eb, lo=lo, hi=hi:
                        nc.scalar.activation(
                eb[:, lo:hi], pb[:, lo:hi], AF.Exp, scale=0.125), inc=("s", 1))
            em.wait("vector", s_e)
            triv = trimask[:].rearrange("p (b q) -> p b q", b=2)
            if n == 0:
                em.seq("vector", lambda e, ebv=ebv: nc.vector.memset(ebv[:, 0], 0.0))
            else:
                em.seq("vector", lambda e, ebv=ebv, triv=triv: nc.vector.tensor_tensor(
                    ebv[:, 0], ebv[:, 0], triv[:, 0], OP.mult))
            if n == 31:
                vm = em.op("vector", lambda e, ebv=ebv: nc.vector.memset(ebv[:, 2], 0.0),
                           inc=("v", 1))
            else:
                vm = em.op("vector", lambda e, ebv=ebv, triv=triv: nc.vector.tensor_tensor(
                    ebv[:, 2], ebv[:, 2], triv[:, 1], OP.mult), inc=("v", 1))
            s_exp[h % 2] = vm
            em.wait("tensor", vm)
            pa = pav[h % 2]
            em.wait("tensor", v_attn_done)
            kbs = [kb for kb in range(3) if 0 <= n - 1 + kb <= 31]
            for kb in kbs:
                rk = (n - 1 + kb) % 8
                em.op("tensor", lambda e, kb=kb, rk=rk, h=h, pa=pa, ebv=ebv:
                      nc.tensor.matmul(
                          pa[:], ebv[:, kb],
                          vTv[:, rk, h, :],
                          start=(kb == kbs[0]), stop=(kb == kbs[-1])))
            tta = em.mark("tensor", "t")
            em.wait("vector", tta)
            em.seq("vector", lambda e, pa=pa: nc.vector.reciprocal(rcol[:], pa[:, 64:65]))
            v_attn_done = em.op("vector", lambda e, pa=pa, h=h: nc.vector.tensor_scalar(
                aob[:, h * 64:(h + 1) * 64], pa[:, 0:64], rcol[:], None, OP.mult),
                inc=("v", 1))
        em.wait("scalar", v_attn_done)
        s_c = em.op("scalar", lambda e: nc.scalar.copy(aobb[:], aob[:]), inc=("s", 1))
        s_aocast = s_c
        em.wait("tensor", s_c)
        em.wait("tensor", s_ao_evac)
        for dt in range(8):
            em.op("tensor", lambda e, dt=dt: nc.tensor.transpose(
                ptrD[:, dt * 128:(dt + 1) * 128],
                aobb[:, dt * 128:(dt + 1) * 128], ident[:]))
        tt = em.mark("tensor", "t")
        em.wait("scalar", tt)
        s_ao_evac = em.op("scalar", lambda e: nc.scalar.copy(
            aoTv[:], ptrD[:].rearrange("p (dt t) -> p dt t", dt=8)), inc=("s", 1))
        em.wait("tensor", s_ao_evac)
        em.wait("tensor", s_attn_done)
        for ch in range(2):
            for dt in range(8):
                em.op("tensor", lambda e, dt=dt, ch=ch: nc.tensor.matmul(
                    ppr[:], aoTv[:, dt],
                    wprov[:, dt, ch * 512:(ch + 1) * 512],
                    start=(dt == 0), stop=False))
            em.op("tensor", lambda e, ch=ch: nc.tensor.matmul(
                ppr[:], ones2[:], bhlD[:, ch * 512:(ch + 1) * 512],
                start=False, stop=True))
            tt = em.mark("tensor", "t")
            em.wait("vector", tt)
            cslot = (n // 4) % 2
            v_r = em.op("vector", lambda e, n=n, ch=ch, cslot=cslot:
                        nc.vector.tensor_tensor(
                            x3b[:, ch * 512:(ch + 1) * 512], ppr[:],
                            x2rv[:, cslot, n % 4, ch * 512:(ch + 1) * 512], OP.add),
                        inc=("v", 1))
            s_attn_done = v_r
            em.wait("tensor", v_r)
        em.wait("sync", s_attn_done)
        d_x3_out = em.dma("sync", [lambda e, n=n: e.dma_start(
            out=x3_h[n * 128:(n + 1) * 128, :], in_=x3b[:])])

    for c in range(NCD):
        cslot = c % 2
        if c >= 2:
            em.wait("sync", s_attn_done)
        d_in = em.dma("sync", [lambda e, c=c, cslot=cslot: e.dma_start(
            out=x2rv[:, cslot], in_=x2_h[c * CD:(c + 1) * CD, :].rearrange(
                "(j p) d -> p j d", p=128))])
        em.wait("vector", d_in)
        em.wait("scalar", d_in)
        s_evac = None
        for j in range(4):
            if not (c == 0 and j == 0):
                em.wait("scalar", ("t", em.cnt["t"]))
            s_apply = emit_ln(x2rv[:, cslot, j], xlnbD[:])
            em.wait("tensor", s_apply)
            em.wait("tensor", s_evac)
            for dt in range(8):
                em.op("tensor", lambda e, dt=dt: nc.tensor.transpose(
                    ptrD[:, dt * 128:(dt + 1) * 128],
                    xlnbD[:, dt * 128:(dt + 1) * 128], ident[:]))
            tt = em.mark("tensor", "t")
            em.wait("scalar", tt)
            s_evac = em.op("scalar", lambda e, j=j: nc.scalar.copy(
                xlnTDv[:, :, j * 128:(j + 1) * 128],
                ptrD[:].rearrange("p (dt t) -> p dt t", dt=8)), inc=("s", 1))
        v_ln_doneD = ("v", em.cnt["v"])
        em.wait("tensor", s_evac)

        rbase = (4 * c) % 8
        sqk = [None, None]
        for f in range(16):
            bank = pqk[f % 2]
            em.wait("tensor", s_qk_evac[f % 2])
            for dt in range(8):
                em.op("tensor", lambda e, f=f, dt=dt, bank=bank: nc.tensor.matmul(
                    bank[:], wqkvv[:, dt, f * 128:(f + 1) * 128],
                    xlnTDv[:, dt], start=(dt == 0), stop=(dt == 7)))
            tt = em.mark("tensor", "t")
            em.wait("scalar", tt)
            dstv = qFv if f < 8 else kFv
            fi = f % 8
            s_qk_evac[f % 2] = em.op("scalar", lambda e, dstv=dstv, fi=fi, bank=bank,
                                     rbase=rbase: nc.scalar.copy(
                dstv[:, fi, rbase * 128:(rbase + 4) * 128], bank[:]), inc=("s", 1))
        for j in range(4):
            for ch in range(2):
                bank = pqk[ch]
                em.wait("tensor", s_qk_evac[ch])
                for dt in range(8):
                    em.op("tensor", lambda e, j=j, ch=ch, dt=dt, bank=bank:
                          nc.tensor.matmul(
                              bank[:], xlnTDv[:, dt, j * 128:(j + 1) * 128],
                              wqkvv[:, dt, 2048 + ch * 512:2048 + (ch + 1) * 512],
                              start=(dt == 0), stop=(dt == 7)))
                tt = em.mark("tensor", "t")
                em.wait("scalar", tt)
                s_qk_evac[ch] = em.op("scalar", lambda e, j=j, ch=ch, bank=bank,
                                      rbase=rbase: nc.scalar.copy(
                    vTv[:, rbase + j, ch * 8:(ch + 1) * 8, 0:64],
                    bank[:].rearrange("p (h dv) -> p h dv", h=8)), inc=("s", 1))
        em.wait("tensor", s_qk_evac[0])
        em.wait("tensor", s_qk_evac[1])

        if c == 0:
            blocks = [0, 1, 2]
        elif c == NCD - 1:
            blocks = [4 * c - 1, 4 * c, 4 * c + 1, 4 * c + 2, 4 * c + 3]
        else:
            blocks = [4 * c - 1, 4 * c, 4 * c + 1, 4 * c + 2]
        for n in blocks:
            emit_attn_block(n)

    em.barrier()
    em.drain_dma()
    esDp.close()
    esD.close()

    if stop_after == "D":
        em.check()
        return nc, em, es

    em.phase("E")
    SE = 1024
    NSE = T // SE
    esE = ExitStack()
    x3r = sb(esE, "x3rE", [128, 2 * D])
    x3rv = x3r[:].rearrange("p (b d) -> p b d", b=2)
    xlnbE = sb(esE, "xlnbE", [128, D], BF16)
    xlnTE = sb(esE, "xlnTE", [128, 8 * SE], BF16)
    xlnTEv = xlnTE[:].rearrange("p (dt t) -> p dt t", dt=8)
    rwb = sb(esE, "rwbE", [128, 8 * 8], BF16)
    rwbv = rwb[:].rearrange("p (dt e) -> p dt e", dt=8)
    rbb = sb(esE, "rbbE", [128, 8])
    scb = sb(esE, "scbE", [128, 8])
    scpb = sb(esE, "scpbE", [128, 8])
    gsb = sb(esE, "gsbE", [128, 4])
    rtmp = sb(esE, "rtmpE", [128, 24])
    cwb = sb(esE, "cwbE", [128, 8 * 8])
    cwbv = cwb[:].rearrange("p (j e) -> p j e", j=8)
    wgE = [sb(esE, f"wgE{i}", [128, 8 * 512], BF16) for i in range(4)]
    wuE = [sb(esE, f"wuE{i}", [128, 8 * 512], BF16) for i in range(4)]
    wdE = [sb(esE, f"wdE{i}", [128, 4 * D], BF16) for i in range(4)]
    gb = [sb(esE, f"gbE{i}", [128, 512]) for i in range(2)]
    ub = [sb(esE, f"ubE{i}", [128, 512]) for i in range(2)]
    gub = [sb(esE, f"gubE{i}", [128, 512], BF16) for i in range(2)]
    guT = [sb(esE, f"guTE{i}", [128, 4 * 128], BF16) for i in range(2)]
    acc = sb(esE, "accE", [128, 8 * D])
    accv = acc[:].rearrange("p (j d) -> p j d", j=8)

    esEp = ExitStack()
    ptrE = pst(esEp, "ptrE", [128, 512], BF16)
    pgE = [pst(esEp, f"pgE{i}", [128, 512]) for i in range(2)]
    puE = [pst(esEp, f"puE{i}", [128, 512]) for i in range(2)]
    pwd = [pst(esEp, f"pwdE{i}", [128, 512]) for i in range(2)]
    prt = pwd[0]

    d_rb = em.dma("sync", [lambda e: e.dma_start(out=rbb[:], in_=router_bb_h[:])])
    fns = []
    for dt in range(8):
        fns.append(lambda e, dt=dt: e.dma_start(
            out=rwbv[:, dt], in_=router_wT_h[dt * 128:(dt + 1) * 128, :]))
    d_rw = em.dma("gpsimd", fns)
    em.wait("tensor", d_rb)
    em.wait("vector", d_rb)
    em.wait("tensor", d_rw)

    def wg_src(e_, slot):
        if e_ < 8:
            return [(lambda dt: lambda e: e.dma_start(
                out=wgE[slot][:, dt * 512:(dt + 1) * 512],
                in_=wgT_h[e_, dt * 128:(dt + 1) * 128, :]))(dt) for dt in range(8)]
        half = e_ - 8
        return [(lambda dt: lambda e: e.dma_start(
            out=wgE[slot][:, dt * 512:(dt + 1) * 512],
            in_=sgT_h[dt * 128:(dt + 1) * 128, half * 512:(half + 1) * 512]))(dt)
            for dt in range(8)]

    def wu_src(e_, slot):
        if e_ < 8:
            return [(lambda dt: lambda e: e.dma_start(
                out=wuE[slot][:, dt * 512:(dt + 1) * 512],
                in_=wuT_h[e_, dt * 128:(dt + 1) * 128, :]))(dt) for dt in range(8)]
        half = e_ - 8
        return [(lambda dt: lambda e: e.dma_start(
            out=wuE[slot][:, dt * 512:(dt + 1) * 512],
            in_=suT_h[dt * 128:(dt + 1) * 128, half * 512:(half + 1) * 512]))(dt)
            for dt in range(8)]

    def wd_src(e_, slot):
        if e_ < 8:
            return [(lambda it: lambda e: e.dma_start(
                out=wdE[slot][:, it * D:(it + 1) * D],
                in_=wdT_h[e_, it * 128:(it + 1) * 128, :]))(it) for it in range(4)]
        half = e_ - 8
        return [(lambda it: lambda e: e.dma_start(
            out=wdE[slot][:, it * D:(it + 1) * D],
            in_=sdT_h[half * 512 + it * 128:half * 512 + (it + 1) * 128, :]))(it)
            for it in range(4)]

    def pair_dma(gp):
        p_ = gp % 5
        base = (gp % 2) * 2
        fns = (wg_src(2 * p_, base) + wu_src(2 * p_, base) + wd_src(2 * p_, base)
               + wg_src(2 * p_ + 1, base + 1) + wu_src(2 * p_ + 1, base + 1)
               + wd_src(2 * p_ + 1, base + 1))
        return em.dma("gpsimd", fns)

    NPAIR_TOT = 5 * NSE
    pair_w_tgt = [None] * (NPAIR_TOT + 1)
    t_pair_done = [None] * NPAIR_TOT
    pair_w_tgt[0] = pair_dma(0)

    v_stripe_doneE = None
    d_y_out = None
    s_guT = [None, None]
    v_acc = None
    s_g_prev = [None, None]
    v_u_prev = [None, None]
    d_x3t = [None, None]
    v_ln_read = [None, None]
    for i in range(NSE):
        if i >= 1:
            em.wait("sync", v_stripe_doneE)
        s_evac = s_guT[1]
        s_sc_prev = None
        for j in range(8):
            em.wait("sync", v_ln_read[j % 2])
            d_x3t[j % 2] = em.dma("sync", [lambda e, i=i, j=j: e.dma_start(
                out=x3rv[:, j % 2], in_=x3_h[i * SE + j * 128:i * SE + (j + 1) * 128, :])])
            em.wait("vector", d_x3t[j % 2])
            em.wait("scalar", d_x3t[j % 2])
            if not (i == 0 and j == 0):
                em.wait("scalar", ("t", em.cnt["t"]))
            s_apply = emit_ln(x3rv[:, j % 2], xlnbE[:])
            v_ln_read[j % 2] = ("v", em.cnt["v"])
            em.wait("tensor", s_apply)
            em.wait("tensor", s_evac)
            for half in range(2):
                for q4 in range(4):
                    dt = half * 4 + q4
                    em.op("tensor", lambda e, dt=dt, q4=q4: nc.tensor.transpose(
                        ptrE[:, q4 * 128:(q4 + 1) * 128],
                        xlnbE[:, dt * 128:(dt + 1) * 128], ident[:]))
                tt = em.mark("tensor", "t")
                em.wait("scalar", tt)
                s_evac = em.op("scalar", lambda e, j=j, half=half: nc.scalar.copy(
                    xlnTEv[:, half * 4:half * 4 + 4, j * 128:(j + 1) * 128],
                    ptrE[:].rearrange("p (dt t) -> p dt t", dt=4)), inc=("s", 1))
                if half == 0:
                    em.wait("tensor", s_evac)
        s_lnT_done = s_evac
        em.wait("tensor", s_lnT_done)
        em.wait("tensor", v_acc)
        v_route_prev = None
        for j in range(8):
            em.wait("tensor", s_sc_prev)
            for dt in range(8):
                em.op("tensor", lambda e, j=j, dt=dt: nc.tensor.matmul(
                    prt[:, 0:8], xlnTEv[:, dt, j * 128:(j + 1) * 128],
                    rwbv[:, dt], start=(dt == 0), stop=(dt == 7)))
            tt = em.mark("tensor", "t")
            em.wait("scalar", tt)
            em.wait("scalar", v_route_prev)
            s_sc = em.op("scalar", lambda e: nc.scalar.activation(
                scb[:], prt[:, 0:8], AF.Sigmoid), inc=("s", 1))
            s_sc_prev = s_sc
            em.wait("vector", s_sc)
            em.seq("vector", lambda e: nc.vector.tensor_tensor(
                scpb[:], scb[:], rbb[:], OP.add))
            em.seq("vector", lambda e: nc.vector.tensor_tensor(
                gsb[:], scpb[:].rearrange("p (g two) -> p g two", two=2)[:, :, 0],
                scpb[:].rearrange("p (g two) -> p g two", two=2)[:, :, 1], OP.add))
            w01 = rtmp[:, 0:1]; w02 = rtmp[:, 1:2]; w03 = rtmp[:, 2:3]
            w12 = rtmp[:, 3:4]; w13 = rtmp[:, 4:5]; w23 = rtmp[:, 5:6]
            pairs = [(0, 1, w01), (0, 2, w02), (0, 3, w03),
                     (1, 2, w12), (1, 3, w13), (2, 3, w23)]
            for a_, b_, wv in pairs:
                em.seq("vector", lambda e, a_=a_, b_=b_, wv=wv: nc.vector.tensor_tensor(
                    wv, gsb[:, a_:a_ + 1], gsb[:, b_:b_ + 1], OP.is_ge))
            wins = rtmp[:, 8:12]
            em.seq("vector", lambda e: nc.vector.tensor_tensor(
                wins[:, 0:1], w01, w02, OP.add))
            em.seq("vector", lambda e: nc.vector.tensor_tensor(
                wins[:, 0:1], wins[:, 0:1], w03, OP.add))
            em.seq("vector", lambda e: nc.vector.scalar_tensor_tensor(
                wins[:, 1:2], w01, -1.0, w12, OP.mult, OP.add))
            em.seq("vector", lambda e: nc.vector.tensor_tensor(
                wins[:, 1:2], wins[:, 1:2], w13, OP.add))
            em.seq("vector", lambda e: nc.vector.tensor_scalar(
                wins[:, 1:2], wins[:, 1:2], 1.0, None, OP.add))
            em.seq("vector", lambda e: nc.vector.scalar_tensor_tensor(
                wins[:, 2:3], w02, -1.0, w23, OP.mult, OP.add))
            em.seq("vector", lambda e: nc.vector.scalar_tensor_tensor(
                wins[:, 2:3], w12, -1.0, wins[:, 2:3], OP.mult, OP.add))
            em.seq("vector", lambda e: nc.vector.tensor_scalar(
                wins[:, 2:3], wins[:, 2:3], 2.0, None, OP.add))
            em.seq("vector", lambda e: nc.vector.tensor_tensor(
                wins[:, 3:4], w03, w13, OP.add))
            em.seq("vector", lambda e: nc.vector.tensor_tensor(
                wins[:, 3:4], wins[:, 3:4], w23, OP.add))
            em.seq("vector", lambda e: nc.vector.tensor_scalar(
                wins[:, 3:4], wins[:, 3:4], -1.0, 3.0, OP.mult, OP.add))
            gmask = rtmp[:, 12:16]
            em.seq("vector", lambda e: nc.vector.tensor_scalar(
                gmask[:], wins[:], 2.0, None, OP.is_ge))
            smask = rtmp[:, 16:24]
            em.seq("vector", lambda e: nc.vector.tensor_copy(
                smask[:].rearrange("p (g two) -> p g two", two=2)[:, :, 0], gmask[:]))
            em.seq("vector", lambda e: nc.vector.tensor_copy(
                smask[:].rearrange("p (g two) -> p g two", two=2)[:, :, 1], gmask[:]))
            masked = rtmp[:, 0:8]
            em.seq("vector", lambda e: nc.vector.tensor_tensor(
                masked[:], scpb[:], smask[:], OP.mult))
            m1 = rtmp[:, 8:9]; m2 = rtmp[:, 9:10]; den = rtmp[:, 10:11]
            sel = cwbv[:, j]
            em.seq("vector", lambda e: nc.vector.tensor_reduce(m1, masked[:], AX.X, OP.max))
            em.seq("vector", lambda e, sel=sel: nc.vector.tensor_scalar(
                sel, masked[:], m1, None, OP.is_ge))
            em.seq("vector", lambda e, sel=sel: nc.vector.tensor_scalar(
                rtmp[:, 16:24], sel, m1, None, OP.mult))
            em.seq("vector", lambda e: nc.vector.tensor_tensor(
                masked[:], masked[:], rtmp[:, 16:24], OP.subtract))
            em.seq("vector", lambda e: nc.vector.tensor_reduce(m2, masked[:], AX.X, OP.max))
            em.seq("vector", lambda e: nc.vector.tensor_scalar(
                rtmp[:, 16:24], masked[:], m2, None, OP.is_ge))
            em.seq("vector", lambda e, sel=sel: nc.vector.tensor_tensor(
                sel, sel, rtmp[:, 16:24], OP.add))
            em.seq("vector", lambda e, sel=sel: nc.vector.tensor_tensor(
                sel, sel, scb[:], OP.mult))
            em.seq("vector", lambda e, sel=sel: nc.vector.tensor_reduce(
                den, sel, AX.X, OP.add))
            em.seq("vector", lambda e: nc.vector.tensor_scalar(
                den, den, 1e-20, None, OP.add))
            em.seq("vector", lambda e: nc.vector.reciprocal(den, den))
            em.op("vector", lambda e, sel=sel: nc.vector.tensor_scalar(
                sel, sel, den, None, OP.mult), inc=("v", 1))
            v_route_prev = ("v", em.cnt["v"])
        v_route_done = ("v", em.cnt["v"])
        if i > 0:
            em.wait("vector", d_y_out)

        for p_ in range(5):
            gp = i * 5 + p_
            if gp + 1 < NPAIR_TOT:
                if gp >= 1:
                    em.wait("gpsimd", t_pair_done[gp - 1])
                pair_w_tgt[gp + 1] = pair_dma(gp + 1)
            em.wait("tensor", pair_w_tgt[gp])
            em.wait("tensor", v_route_done)
            em.wait("tensor", s_lnT_done)
            base = (gp % 2) * 2
            for j in range(8):
                s_gub_t = [None, None]
                for ei in range(2):
                    ex = 2 * p_ + ei
                    slot = base + ei
                    em.wait("tensor", s_g_prev[ei])
                    for dt in range(8):
                        em.op("tensor", lambda e, j=j, dt=dt, slot=slot, ei=ei:
                              nc.tensor.matmul(
                                  pgE[ei][:], xlnTEv[:, dt, j * 128:(j + 1) * 128],
                                  wgE[slot][:, dt * 512:(dt + 1) * 512],
                                  start=(dt == 0), stop=(dt == 7)))
                    ttg = em.mark("tensor", "t")
                    em.wait("tensor", v_u_prev[ei])
                    for dt in range(8):
                        em.op("tensor", lambda e, j=j, dt=dt, slot=slot, ei=ei:
                              nc.tensor.matmul(
                                  puE[ei][:], xlnTEv[:, dt, j * 128:(j + 1) * 128],
                                  wuE[slot][:, dt * 512:(dt + 1) * 512],
                                  start=(dt == 0), stop=(dt == 7)))
                    ttu = em.mark("tensor", "t")
                    em.wait("scalar", ttg)
                    s_g = em.op("scalar", lambda e, ei=ei: nc.scalar.activation(
                        gb[ei][:], pgE[ei][:], AF.Silu), inc=("s", 1))
                    s_g_prev[ei] = s_g
                    em.wait("vector", ttu)
                    em.wait("vector", s_g)
                    v_u_prev[ei] = em.seq("vector", lambda e, ei=ei: nc.vector.tensor_copy(
                        ub[ei][:], puE[ei][:]))
                    vgu = em.op("vector", lambda e, ei=ei: nc.vector.tensor_tensor(
                        ub[ei][:], ub[ei][:], gb[ei][:], OP.mult), inc=("v", 1))
                    if ex < 8:
                        vgu = em.op("vector", lambda e, j=j, ex=ex, ei=ei:
                                    nc.vector.tensor_scalar(
                            ub[ei][:], ub[ei][:], cwbv[:, j, ex:ex + 1], None, OP.mult),
                            inc=("v", 1))
                    em.wait("scalar", vgu)
                    s_gub_t[ei] = em.op("scalar", lambda e, ei=ei: nc.scalar.copy(
                        gub[ei][:], ub[ei][:]), inc=("s", 1))
                em.wait("tensor", v_acc)
                for ei in range(2):
                    slot = base + ei
                    em.wait("tensor", s_gub_t[ei])
                    em.wait("tensor", s_guT[1 - ei])
                    for it in range(4):
                        em.op("tensor", lambda e, it=it, ei=ei: nc.tensor.transpose(
                            ptrE[:, it * 128:(it + 1) * 128],
                            gub[ei][:, it * 128:(it + 1) * 128], ident[:]))
                    ttt = em.mark("tensor", "t")
                    em.wait("scalar", ttt)
                    s_guT[ei] = em.op("scalar", lambda e, ei=ei: nc.scalar.copy(
                        guT[ei][:], ptrE[:, 0:512]), inc=("s", 1))
                    em.wait("tensor", s_guT[ei])
                    for ch in range(2):
                        for it in range(4):
                            em.op("tensor", lambda e, ei=ei, it=it, ch=ch, slot=slot:
                                  nc.tensor.matmul(
                                pwd[ch][:], guT[ei][:, it * 128:(it + 1) * 128],
                                wdE[slot][:, it * D + ch * 512: it * D + (ch + 1) * 512],
                                start=(ei == 0 and it == 0), stop=(ei == 1 and it == 3)))
                ttw = em.mark("tensor", "t")
                em.wait("vector", ttw)
                for ch in range(2):
                    if p_ == 0:
                        em.op("vector", lambda e, j=j, ch=ch: nc.vector.tensor_copy(
                            accv[:, j, ch * 512:(ch + 1) * 512], pwd[ch][:]),
                            inc=("v", 1) if ch == 1 else None)
                    else:
                        em.op("vector", lambda e, j=j, ch=ch: nc.vector.tensor_tensor(
                            accv[:, j, ch * 512:(ch + 1) * 512], pwd[ch][:],
                            accv[:, j, ch * 512:(ch + 1) * 512], OP.add),
                            inc=("v", 1) if ch == 1 else None)
                v_acc = ("v", em.cnt["v"])
            t_pair_done[gp] = ("t", em.cnt["t"])
        v_res = [None, None]
        for j in range(8):
            em.wait("sync", v_res[j % 2] if j >= 2 else v_ln_read[j % 2])
            d_x3t[j % 2] = em.dma("sync", [lambda e, i=i, j=j: e.dma_start(
                out=x3rv[:, j % 2], in_=x3_h[i * SE + j * 128:i * SE + (j + 1) * 128, :])])
            em.wait("vector", d_x3t[j % 2])
            v_res[j % 2] = em.seq("vector", lambda e, j=j: nc.vector.tensor_tensor(
                accv[:, j], accv[:, j], x3rv[:, j % 2], OP.add))
        v_ln_read = [None, None]
        v_stripe_doneE = em.mark("vector", "v")
        em.wait("sync", v_stripe_doneE)
        d_y_out = em.dma("sync", [lambda e, i=i: e.dma_start(
            out=y_h[i * SE:(i + 1) * SE, :].rearrange("(j p) d -> p j d", p=128),
            in_=accv[:])])

    em.barrier()
    em.drain_dma()
    esEp.close()
    esE.close()

    em.check()
    return nc, em, es


def _run_thunks(nc, em, eng, e):
    marks = getattr(em, "phase_marks", [])
    bounds = [(nm, idx[eng]) for nm, idx in marks] + [("end", len(em.prog[eng]))]
    pos = 0
    for k in range(len(bounds) - 1):
        nm, start = bounds[k]
        stop = bounds[k + 1][1]
        for fn in em.prog[eng][pos:start]:
            fn(e)
        pos = start
        with nc.named_scope(f"ph{nm}"):
            for fn in em.prog[eng][start:stop]:
                fn(e)
        pos = stop
    for fn in em.prog[eng][pos:]:
        fn(e)


def _finish(nc, em, es):
    for h in em.sem.values():
        nc.gpsimd.sem_clear(h)
    nc.all_engine_barrier()
    with nc.Block() as block:
        @block.sync
        def _(sync):
            _run_thunks(nc, em, "sync", sync)

        @block.gpsimd
        def _(g):
            _run_thunks(nc, em, "gpsimd", g)

        @block.tensor
        def _(te):
            _run_thunks(nc, em, "tensor", te)

        @block.scalar
        def _(sc):
            _run_thunks(nc, em, "scalar", sc)

        @block.vector
        def _(v):
            _run_thunks(nc, em, "vector", v)
    es.close()
    return nc


def _run(inputs, stop_after="E", dbg=(), cores=None):
    g = _host_prep(inputs)
    nc, em, es = build_program(stop_after=stop_after, dbg=dbg)
    nc = _finish(nc, em, es)
    cores = cores if cores is not None else list(range(NCORES))
    in_maps = []
    for c in cores:
        m = {"x0": g["x0"][c]}
        for k in INPUT_NAMES:
            m[k] = g[k]
        in_maps.append(m)
    res = run_bass_kernel_spmd(nc, in_maps, core_ids=list(range(len(cores))))
    return res


def kernel(**inputs):
    res = _run(inputs, stop_after="E")
    return np.stack([np.asarray(r["y"], dtype=np.float32) for r in res.results])


# version 06
# speedup vs baseline: 1.1033x; 1.0021x over previous
import numpy as np
import ml_dtypes
from contextlib import ExitStack

import concourse.bass as bass
import concourse.mybir as mybir
from concourse.bass_utils import run_bass_kernel_spmd

F32 = mybir.dt.float32
BF16 = mybir.dt.bfloat16
I32 = mybir.dt.int32
AF = mybir.ActivationFunctionType
OP = mybir.AluOpType
AX = mybir.AxisListType
BF = ml_dtypes.bfloat16

T = 4096
D = 1024
HID = 1536
NCORES = 8
EPS = 1e-5
C_CONST = 8.0

SA = 512
NSA = T // SA
SB = 256
NSB = T // SB


class EM:
    ENGS = ("sync", "gpsimd", "tensor", "scalar", "vector")
    POOLS = {"sync": 4, "gpsimd": 4, "scalar": 2}

    def __init__(self, nc, es):
        self.nc = nc
        self.prog = {e: [] for e in self.ENGS}
        self.sem = {}
        self.cnt = {}
        self.max_wait = {}
        for name in ("t", "v", "s"):
            self._mk(es, name)
        self.pool = {}
        self.pool_idx = {}
        for q, n in self.POOLS.items():
            self.pool[q] = []
            self.pool_idx[q] = 0
            for i in range(n):
                nm = f"d_{q}_{i}"
                self._mk(es, nm)
                self.pool[q].append(nm)

    def _mk(self, es, name):
        self.sem[name] = es.enter_context(self.nc.semaphore(name))
        self.cnt[name] = 0
        self.max_wait[name] = 0

    def op(self, eng, fn, inc=None):
        if inc is not None:
            name, n = inc
            self.cnt[name] += n
            sem = self.sem[name]
            if eng in ("tensor", "scalar", "vector"):
                def thunk(e, fn=fn, sem=sem, n=n):
                    fn(e)
                    e.drain().then_inc(sem, n)
            else:
                def thunk(e, fn=fn, sem=sem, n=n):
                    fn(e).then_inc(sem, n)
        else:
            def thunk(e, fn=fn):
                fn(e)
        self.prog[eng].append(thunk)
        return (inc[0], self.cnt[inc[0]]) if inc else None

    def wait(self, eng, tgt):
        if tgt is None:
            return
        name, val = tgt
        if val <= 0:
            return
        self.max_wait[name] = max(self.max_wait[name], val)
        sem = self.sem[name]
        self.prog[eng].append(lambda e, sem=sem, v=val: e.wait_ge(sem, v))

    def dma(self, queue, fns):
        nm = self.pool[queue][self.pool_idx[queue]]
        self.pool_idx[queue] = (self.pool_idx[queue] + 1) % len(self.pool[queue])
        self.wait(queue, (nm, self.cnt[nm]))
        tgt = None
        for fn in fns:
            tgt = self.op(queue, fn, inc=(nm, 16))
        return tgt

    def seq(self, eng, fn):
        name = {"tensor": "t", "scalar": "s", "vector": "v"}[eng]
        tgt = self.op(eng, fn, inc=(name, 1))
        self.wait(eng, tgt)
        return tgt

    def mark(self, eng, name):
        self.cnt[name] += 1
        sem = self.sem[name]
        self.prog[eng].append(lambda e, sem=sem: e.drain().then_inc(sem, 1))
        return (name, self.cnt[name])

    def phase(self, name):
        if not hasattr(self, "phase_marks"):
            self.phase_marks = []
        self.phase_marks.append((name, {e: len(self.prog[e]) for e in self.ENGS}))

    def barrier(self):
        targets = [self.mark(eng, n) for eng, n in
                   (("tensor", "t"), ("scalar", "s"), ("vector", "v"))]
        for eng in self.ENGS:
            for tgt in targets:
                self.wait(eng, tgt)

    def drain_dma(self, eng="sync"):
        for q, names in self.pool.items():
            for nm in names:
                self.wait(eng, (nm, self.cnt[nm]))

    def check(self):
        for name, v in self.max_wait.items():
            assert v <= self.cnt[name], \
                f"wait on {name} for {v} but only {self.cnt[name]} emitted"


def _host_prep(inputs):
    g = {}
    x = np.ascontiguousarray(np.asarray(inputs["L0_attn_x"], dtype=np.float32))
    g["x0"] = [np.ascontiguousarray(x[i]) for i in range(NCORES)]

    tr = lambda a: np.ascontiguousarray(np.asarray(a, dtype=np.float32).T)
    fm = lambda a, n: np.ascontiguousarray(
        np.asarray(a, dtype=np.float32).reshape(n, 128).T)

    g["proj_wT"] = tr(inputs["L0_attn_proj_w"])
    cw = np.asarray(inputs["L0_attn_conv_w"], dtype=np.float32).reshape(HID, 4)
    g["conv_w"] = np.ascontiguousarray(
        cw.reshape(12, 128, 4).transpose(1, 0, 2).reshape(128, 48))
    g["conv_b"] = fm(inputs["L0_attn_conv_b"], 12)
    g["ip_wT"] = tr(inputs["L0_attn_ip_w"])
    g["gp_wT"] = tr(inputs["L0_attn_gp_w"])
    g["ip_b"] = fm(inputs["L0_attn_ip_b"], 12)
    g["gp_b"] = fm(inputs["L0_attn_gp_b"], 12)
    g["lam"] = fm(inputs["L0_attn_forget_lambda"], 12)
    g["out_wT"] = tr(inputs["L0_attn_out_w"])

    g["ffn_w"] = np.ascontiguousarray(np.asarray(inputs["L0_ffn_weight"], dtype=np.float32))
    g["ffn_b"] = np.ascontiguousarray(
        np.asarray(inputs["L0_ffn_bias"], dtype=np.float32).reshape(1, D))

    g["w_qkv"] = np.ascontiguousarray(np.asarray(inputs["L1_attn_w_qkv"], dtype=np.float32))
    g["w_proj"] = np.ascontiguousarray(np.asarray(inputs["L1_attn_w_proj"], dtype=np.float32))
    g["b_proj"] = np.ascontiguousarray(
        np.asarray(inputs["L1_attn_b_proj"], dtype=np.float32).reshape(1, D))

    g["router_wT"] = tr(inputs["L1_ffn_router_w"])
    g["router_bb"] = np.ascontiguousarray(np.broadcast_to(
        np.asarray(inputs["L1_ffn_bias"], dtype=np.float32).reshape(1, 8), (128, 8)))
    g["router_b"] = np.ascontiguousarray(
        np.asarray(inputs["L1_ffn_bias"], dtype=np.float32).reshape(1, 8))
    wg = np.asarray(inputs["L1_ffn_wg"], dtype=np.float32)
    wu = np.asarray(inputs["L1_ffn_wu"], dtype=np.float32)
    wd = np.asarray(inputs["L1_ffn_wd"], dtype=np.float32)
    g["wgT"] = np.ascontiguousarray(wg.transpose(0, 2, 1))
    g["wuT"] = np.ascontiguousarray(wu.transpose(0, 2, 1))
    g["wdT"] = np.ascontiguousarray(wd.transpose(0, 2, 1))
    g["sgT"] = tr(inputs["L1_ffn_sg"])
    g["suT"] = tr(inputs["L1_ffn_su"])
    g["sdT"] = tr(inputs["L1_ffn_sd"])

    g["ident"] = np.eye(128, dtype=BF)
    g["ones2"] = np.ones((2, 128), dtype=BF)
    k_idx = np.arange(128)[:, None]
    q_idx = np.arange(128)[None, :]
    g["trimask"] = np.ascontiguousarray(np.concatenate(
        [(k_idx >= q_idx).astype(BF), (k_idx <= q_idx).astype(BF)], axis=1))
    return g


INPUT_NAMES = ["proj_wT", "conv_w", "conv_b", "ip_wT", "gp_wT", "ip_b", "gp_b",
               "lam", "out_wT", "ffn_w", "ffn_b", "w_qkv", "w_proj", "b_proj",
               "router_wT", "router_b", "router_bb", "wgT", "wuT", "wdT", "sgT", "suT", "sdT",
               "ident", "ones2", "trimask"]


def build_program(stop_after="E", dbg=()):
    es = ExitStack()
    nc = bass.Bass("TRN2", target_bir_lowering=False, detect_race_conditions=False)
    em = EM(nc, es)

    def din(name, shape, dtype=F32):
        return nc.dram_tensor(name, list(shape), dtype, kind="ExternalInput").ap()

    def dscratch(name, shape, dtype=F32):
        kind = "ExternalOutput" if name in dbg else "Internal"
        return nc.dram_tensor(name, list(shape), dtype, kind=kind).ap()

    x0_h = din("x0", [T, D])
    proj_wT_h = din("proj_wT", [D, 3072])
    conv_w_h = din("conv_w", [128, 48])
    conv_b_h = din("conv_b", [128, 12])
    ip_wT_h = din("ip_wT", [HID, HID])
    gp_wT_h = din("gp_wT", [HID, HID])
    ip_b_h = din("ip_b", [128, 12])
    gp_b_h = din("gp_b", [128, 12])
    lam_h = din("lam", [128, 12])
    out_wT_h = din("out_wT", [HID, D])
    ffn_w_h = din("ffn_w", [D, D])
    ffn_b_h = din("ffn_b", [1, D])
    w_qkv_h = din("w_qkv", [D, 3072])
    w_proj_h = din("w_proj", [D, D])
    b_proj_h = din("b_proj", [1, D])
    router_wT_h = din("router_wT", [D, 8])
    router_b_h = din("router_b", [1, 8])
    router_bb_h = din("router_bb", [128, 8])
    wgT_h = din("wgT", [8, D, 512])
    wuT_h = din("wuT", [8, D, 512])
    wdT_h = din("wdT", [8, 512, D])
    sgT_h = din("sgT", [D, D])
    suT_h = din("suT", [D, D])
    sdT_h = din("sdT", [D, D])
    ident_h = din("ident", [128, 128], BF16)
    ones2_h = din("ones2", [2, 128], BF16)
    trimask_h = din("trimask", [128, 256], BF16)

    dbgZ_h = dscratch("dbgZ", [12, 128, T], BF16)
    dbgXD_h = dscratch("dbgXD", [T, D], BF16)
    gate_h = dscratch("gateS", [12, 128, T])
    xc_h = dscratch("xcS", [12, 128, T])
    x1_h = dscratch("x1S", [T, D])
    x2_h = dscratch("x2S", [T, D])
    x3_h = dscratch("x3S", [T, D])
    y_h = nc.dram_tensor("y", [T, D], F32, kind="ExternalOutput").ap()

    def sb(stack, name, shape, dt=F32):
        return stack.enter_context(nc.sbuf_tensor(name, shape, dt))

    def pst(stack, name, shape, dt=F32):
        return stack.enter_context(nc.psum_tensor(name, shape, dt))

    ident = sb(es, "identB", [128, 128], BF16)
    ones2 = sb(es, "ones2B", [2, 128], BF16)
    trimask = sb(es, "trimaskB", [128, 256], BF16)
    cst = sb(es, "cst", [128, 80])
    gpb = sb(es, "gpb", [128, 24])
    stats = sb(es, "stats", [128, 16])
    dump = sb(es, "dump", [128, D])
    kc = sb(es, "kc", [128, 4])

    d_init = em.dma("sync", [
        lambda e: e.dma_start(out=ident[:], in_=ident_h[:]),
        lambda e: e.dma_start(out=ones2[:], in_=ones2_h[:]),
        lambda e: e.dma_start(out=trimask[:], in_=trimask_h[:]),
        lambda e: e.dma_start(out=cst[:, 0:48], in_=conv_w_h[:]),
        lambda e: e.dma_start(out=cst[:, 48:60], in_=conv_b_h[:]),
        lambda e: e.dma_start(out=gpb[:, 0:12], in_=gp_b_h[:]),
        lambda e: e.dma_start(out=gpb[:, 12:24], in_=ip_b_h[:]),
    ])
    lamt = sb(es, "lamt", [128, 12])
    d_lam = em.dma("sync", [lambda e: e.dma_start(out=lamt[:], in_=lam_h[:])])

    em.op("vector", lambda e: nc.vector.memset(kc[:, 0:1], EPS))
    em.op("vector", lambda e: nc.vector.memset(kc[:, 1:2], 1.0 + 1e-6))
    em.op("vector", lambda e: nc.vector.memset(kc[:, 2:3], 1e-6), inc=("v", 1))

    em.wait("scalar", d_lam)
    ctmp = sb(es, "ctmp", [128, 12])
    s_c1 = em.seq("scalar", lambda e: nc.scalar.activation(ctmp[:], lamt[:], AF.Exp))
    em.wait("vector", s_c1)
    v_c1 = em.seq("vector", lambda e: nc.vector.tensor_scalar(
        ctmp[:], ctmp[:], 1.0, None, OP.add))
    em.wait("scalar", v_c1)
    s_c2 = em.op("scalar", lambda e: nc.scalar.activation(lamt[:], ctmp[:], AF.Ln),
                 inc=("s", 1))
    em.wait("vector", s_c2)
    v_c2 = em.op("vector", lambda e: nc.vector.tensor_scalar(
        cst[:, 60:72], lamt[:], -C_CONST, None, OP.mult), inc=("v", 1))
    em.wait("scalar", v_c2)
    em.wait("tensor", v_c2)
    em.wait("scalar", d_init)
    em.wait("vector", d_init)
    em.wait("tensor", d_init)

    def emit_ln(x_ap, out_bf_ap):
        sumx = stats[:, 0:1]; sumsq = stats[:, 1:2]; m = stats[:, 2:3]
        var = stats[:, 3:4]; r = stats[:, 4:5]; nmr = stats[:, 5:6]
        u = stats[:, 6:7]; mm = stats[:, 7:8]
        em.op("vector", lambda e: nc.vector.tensor_reduce(sumx, x_ap, AX.X, OP.add),
              inc=("v", 1))
        sv = em.op("scalar", lambda e: nc.scalar.activation(
            dump[:], x_ap, AF.Square, accum_out=sumsq), inc=("s", 1))
        em.wait("vector", sv)
        em.seq("vector", lambda e: nc.vector.tensor_scalar(m, sumx, 1.0 / D, None, OP.mult))
        em.seq("vector", lambda e: nc.vector.tensor_scalar(var, sumsq, 1.0 / D, None, OP.mult))
        em.seq("vector", lambda e: nc.vector.tensor_tensor(mm, m, m, OP.mult))
        vv = em.op("vector", lambda e: nc.vector.tensor_tensor(var, var, mm, OP.subtract),
                   inc=("v", 1))
        em.wait("scalar", vv)
        em.seq("scalar", lambda e: nc.scalar.activation(u, var, AF.Ln, bias=kc[:, 0:1]))
        sv2 = em.op("scalar", lambda e: nc.scalar.activation(r, u, AF.Exp, scale=-0.5),
                    inc=("s", 1))
        em.wait("vector", sv2)
        em.seq("vector", lambda e: nc.vector.scalar_tensor_tensor(
            nmr, m, -1.0, r, OP.mult, OP.mult))
        vap = em.op("vector", lambda e: nc.vector.tensor_scalar(
            dump[:], x_ap, r, nmr, OP.mult, OP.add), inc=("v", 1))
        em.wait("scalar", vap)
        return em.op("scalar", lambda e: nc.scalar.copy(out_bf_ap, dump[:]),
                     inc=("s", 1))

    em.phase("A")
    esA = ExitStack()
    pw = sb(esA, "pwA", [128, 8 * 3072], BF16)
    dGw = em.dma("gpsimd", [
        (lambda dt: lambda e: e.dma_start(
            out=pw[:, dt * 3072:(dt + 1) * 3072],
            in_=proj_wT_h[dt * 128:(dt + 1) * 128, :]))(dt)
        for dt in range(8)])

    x0b = sb(esA, "x0bA", [128, 2 * 4 * D])
    x0v = x0b[:].rearrange("p (b j d) -> p b j d", b=2, j=4)
    xlnb = sb(esA, "xlnbA", [128, D], BF16)
    xlnT = sb(esA, "xlnTA", [128, 8 * SA], BF16)
    xlnTv = xlnT[:].rearrange("p (dt t) -> p dt t", dt=8)
    gateb = sb(esA, "gatebA", [128, 12 * SA])
    gatev = gateb[:].rearrange("p (h t) -> p h t", h=12)
    xhb = sb(esA, "xhbA", [128, 12 * (SA + 3)])
    xhv = xhb[:].rearrange("p (h t) -> p h t", h=12)
    xcb = sb(esA, "xcbA", [128, 12 * SA])
    xcv = xcb[:].rearrange("p (h t) -> p h t", h=12)


    esAp = ExitStack()
    ptr = pst(esAp, "ptrA", [128, 1024], BF16)
    pgx = [pst(esAp, f"pgxA{i}", [128, SA]) for i in range(4)]

    em.wait("tensor", dGw)

    v_ln_done = [None] * NSA
    d_out_done = [None] * NSA

    for i in range(NSA):
        buf = i % 2
        if i >= 2:
            em.wait("sync", v_ln_done[i - 2])
        d_in = em.dma("sync", [lambda e, i=i, buf=buf: e.dma_start(
            out=x0v[:, buf], in_=x0_h[i * SA:(i + 1) * SA, :].rearrange(
                "(j p) d -> p j d", p=128))])

        em.wait("vector", d_in)
        em.wait("scalar", d_in)
        s_evac = None
        for j in range(4):
            if not (i == 0 and j == 0):
                em.wait("scalar", ("t", em.cnt["t"]))
            s_apply = emit_ln(x0v[:, buf, j], xlnb[:])
            em.wait("tensor", s_apply)
            em.wait("tensor", s_evac)
            for dt in range(8):
                em.op("tensor", lambda e, dt=dt: nc.tensor.transpose(
                    ptr[:, dt * 128:(dt + 1) * 128],
                    xlnb[:, dt * 128:(dt + 1) * 128], ident[:]))
            tt = em.mark("tensor", "t")
            em.wait("scalar", tt)
            s_evac = em.op("scalar", lambda e, j=j: nc.scalar.copy(
                xlnTv[:, :, j * 128:(j + 1) * 128],
                ptr[:].rearrange("p (dt t) -> p dt t", dt=8)), inc=("s", 1))
        v_ln_done[i] = ("v", em.cnt["v"])
        s_xlnT_done = s_evac

        if i == 0:
            em.op("vector", lambda e: nc.vector.memset(xhv[:, :, 0:3], 0.0), inc=("v", 1))
        else:
            em.op("vector", lambda e: nc.vector.tensor_copy(
                xhv[:, :, 0:3], xhv[:, :, SA:SA + 3]), inc=("v", 1))

        em.wait("tensor", ("v", em.cnt["v"]))
        em.wait("tensor", s_xlnT_done)
        if i > 0:
            em.wait("scalar", d_out_done[i - 1])
            em.wait("vector", d_out_done[i - 1])
        s_base = em.cnt["s"]
        v_base = em.cnt["v"]
        for f in range(24):
            bank = pgx[f % 4]
            if f >= 4:
                fp = f - 4
                if fp < 12:
                    em.wait("tensor", ("s", s_base + fp + 1))
                else:
                    em.wait("tensor", ("v", v_base + (fp - 12) + 1))
            for dt in range(8):
                em.op("tensor", lambda e, f=f, dt=dt, bank=bank: nc.tensor.matmul(
                    bank[:], pw[:, dt * 3072 + f * 128: dt * 3072 + (f + 1) * 128],
                    xlnTv[:, dt], start=(dt == 0), stop=(dt == 7)))
            tt = em.mark("tensor", "t")
            if f < 12:
                em.wait("scalar", tt)
                em.op("scalar", lambda e, f=f, bank=bank: nc.scalar.activation(
                    gatev[:, f], bank[:], AF.Gelu), inc=("s", 1))
            else:
                em.wait("vector", tt)
                em.op("vector", lambda e, f=f, bank=bank: nc.vector.tensor_copy(
                    xhv[:, f - 12, 3:3 + SA], bank[:]), inc=("v", 1))
        s_gate_done = ("s", em.cnt["s"])

        for h in range(12):
            em.seq("vector", lambda e, h=h: nc.vector.tensor_scalar(
                xcv[:, h], xhv[:, h, 0:SA], cst[:, 4 * h:4 * h + 1],
                cst[:, 48 + h:49 + h], OP.mult, OP.add))
            for k in range(1, 4):
                em.seq("vector", lambda e, h=h, k=k: nc.vector.scalar_tensor_tensor(
                    xcv[:, h], xhv[:, h, k:k + SA], cst[:, 4 * h + k:4 * h + k + 1],
                    xcv[:, h], OP.mult, OP.add))
        v_conv_done = em.mark("vector", "v")

        em.wait("sync", s_gate_done)
        em.wait("sync", v_conv_done)
        fns = []
        for h in range(12):
            fns.append(lambda e, i=i, h=h: e.dma_start(
                out=gate_h[h, :, i * SA:(i + 1) * SA], in_=gatev[:, h]))
            fns.append(lambda e, i=i, h=h: e.dma_start(
                out=xc_h[h, :, i * SA:(i + 1) * SA], in_=xcv[:, h]))
        d_out_done[i] = em.dma("sync", fns)

    em.barrier()
    em.drain_dma()
    esAp.close()
    esA.close()

    if stop_after == "A":
        em.check()
        return nc, em, es

    em.phase("B")
    esB = ExitStack()
    gw = sb(esB, "gwB", [128, 12 * HID], BF16)
    iw = sb(esB, "iwB", [128, 12 * HID], BF16)
    ow = sb(esB, "owB", [128, 12 * D], BF16)
    fns = []
    for ht in range(12):
        fns.append(lambda e, ht=ht: e.dma_start(
            out=gw[:, ht * HID:(ht + 1) * HID],
            in_=gp_wT_h[ht * 128:(ht + 1) * 128, :]))
        fns.append(lambda e, ht=ht: e.dma_start(
            out=iw[:, ht * HID:(ht + 1) * HID],
            in_=ip_wT_h[ht * 128:(ht + 1) * 128, :]))
        fns.append(lambda e, ht=ht: e.dma_start(
            out=ow[:, ht * D:(ht + 1) * D],
            in_=out_wT_h[ht * 128:(ht + 1) * 128, :]))
    dGw = em.dma("gpsimd", fns)

    xcf = sb(esB, "xcfB", [128, 12 * SB])
    xcfv = xcf[:].rearrange("p (h t) -> p h t", h=12)
    xcB = sb(esB, "xcBB", [128, 12 * SB], BF16)
    xcBv = xcB[:].rearrange("p (h t) -> p h t", h=12)
    gteb = sb(esB, "gtebB", [128, 12 * SB])
    gtev = gteb[:].rearrange("p (h t) -> p h t", h=12)
    x0c = sb(esB, "x0cB", [128, 2 * D])
    x0cv = x0c[:].rearrange("p (j d) -> p j d", j=2)
    recF = sb(esB, "recFB", [128, 12 * SB])
    recFv = recF[:].rearrange("p (g t) -> p g t", g=12)
    inpF = sb(esB, "inpFB", [128, 12 * SB])
    inpFv = inpF[:].rearrange("p (g t) -> p g t", g=12)
    alfb = sb(esB, "alfbB", [128, SB])
    t3b = sb(esB, "t3bB", [128, SB])
    t6b = sb(esB, "t6bB", [128, SB])
    zBb = sb(esB, "zBbB", [128, 12 * SB], BF16)
    zBv = zBb[:].rearrange("p (h t) -> p h t", h=12)
    x1b = sb(esB, "x1bB", [128, 2 * D])
    x1bv = x1b[:].rearrange("p (j d) -> p j d", j=2)
    carry = sb(esB, "carryB", [128, 12])
    em.op("vector", lambda e: nc.vector.memset(carry[:], 0.0), inc=("v", 1))

    esBp = ExitStack()
    prec = [pst(esBp, f"precB{i}", [128, 512]) for i in range(4)]
    pout = [pst(esBp, f"poutB{i}", [128, 512]) for i in range(4)]

    em.wait("tensor", dGw)

    v_stripe_done = None
    d_x1_out = None
    for i in range(NSB):
        if i >= 1:
            em.wait("sync", v_stripe_done)
        fns = []
        for h in range(12):
            fns.append(lambda e, i=i, h=h: e.dma_start(
                out=xcfv[:, h], in_=xc_h[h, :, i * SB:(i + 1) * SB]))
            fns.append(lambda e, i=i, h=h: e.dma_start(
                out=gtev[:, h], in_=gate_h[h, :, i * SB:(i + 1) * SB]))
        fns.append(lambda e, i=i: e.dma_start(
            out=x0cv[:],
            in_=x0_h[i * SB:(i + 1) * SB, :].rearrange("(j p) d -> p j d", p=128)))
        d_in = em.dma("sync", fns)

        em.wait("vector", d_in)
        v_xcB = em.op("vector", lambda e: nc.vector.tensor_copy(xcB[:], xcf[:]),
                      inc=("v", 1))

        em.wait("tensor", v_xcB)
        s_base = em.cnt["s"]
        for gt in range(12):
            for proj in range(2):
                bank = prec[(2 * gt + proj) % 4]
                if 2 * gt + proj >= 4:
                    em.wait("tensor", ("s", s_base + (2 * gt + proj) - 4 + 1))
                w = gw if proj == 0 else iw
                for ht in range(12):
                    em.op("tensor", lambda e, w=w, ht=ht, gt=gt, bank=bank:
                          nc.tensor.matmul(
                              bank[:, 0:SB],
                              w[:, ht * HID + gt * 128: ht * HID + (gt + 1) * 128],
                              xcBv[:, ht], start=(ht == 0), stop=(ht == 11)))
                tt = em.mark("tensor", "t")
                em.wait("scalar", tt)
                dst = recFv[:, gt] if proj == 0 else inpFv[:, gt]
                bias = gpb[:, gt:gt + 1] if proj == 0 else gpb[:, 12 + gt:13 + gt]
                em.op("scalar", lambda e, dst=dst, bank=bank, bias=bias:
                      nc.scalar.activation(dst, bank[:, 0:SB], AF.Sigmoid, bias=bias),
                      inc=("s", 1))
        s_sig_done = ("s", em.cnt["s"])

        em.wait("vector", s_sig_done)
        em.wait("scalar", s_sig_done)
        for gt in range(12):
            em.wait("scalar", ("v", em.cnt["v"]))
            sv = em.op("scalar", lambda e, gt=gt: nc.scalar.activation(
                alfb[:], recFv[:, gt], AF.Exp, scale=cst[:, 60 + gt:61 + gt]),
                inc=("s", 1))
            em.wait("vector", sv)
            vv = em.op("vector", lambda e: nc.vector.tensor_tensor(
                t3b[:], alfb[:], alfb[:], OP.mult), inc=("v", 1))
            em.seq("vector", lambda e: nc.vector.tensor_scalar(
                t3b[:], t3b[:], -1.0, 1.0, OP.mult, OP.add))
            em.wait("scalar", vv)
            em.wait("scalar", ("v", em.cnt["v"]))
            em.seq("scalar", lambda e: nc.scalar.activation(
                t6b[:], t3b[:], AF.Ln, scale=1.0, bias=kc[:, 2:3]))
            sv2 = em.op("scalar", lambda e: nc.scalar.activation(
                t3b[:], t6b[:], AF.Exp, scale=0.5), inc=("s", 1))
            em.wait("vector", sv2)
            em.seq("vector", lambda e, gt=gt: nc.vector.tensor_tensor(
                t6b[:], t3b[:], inpFv[:, gt], OP.mult))
            em.seq("vector", lambda e, gt=gt: nc.vector.tensor_tensor(
                t6b[:], t6b[:], xcfv[:, gt], OP.mult))
            em.seq("vector", lambda e, gt=gt: nc.vector.tensor_tensor_scan(
                t3b[:], alfb[:], t6b[:], carry[:, gt:gt + 1], OP.mult, OP.add))
            em.seq("vector", lambda e, gt=gt: nc.vector.tensor_copy(
                carry[:, gt:gt + 1], t3b[:, SB - 1:SB]))
            em.seq("vector", lambda e, gt=gt: nc.vector.tensor_tensor(
                t3b[:], t3b[:], gtev[:, gt], OP.mult))
            em.op("vector", lambda e, gt=gt: nc.vector.tensor_copy(
                zBv[:, gt], t3b[:]), inc=("v", 1))
        v_zB = ("v", em.cnt["v"])

        if "dbgZ" in dbg:
            em.wait("sync", ("v", em.cnt["v"]))
            em.dma("sync", [(lambda h: lambda e, i=i: e.dma_start(
                out=dbgZ_h[h, :, i * SB:(i + 1) * SB], in_=zBv[:, h]))(h)
                for h in range(12)])
        em.wait("tensor", v_zB)
        if i > 0:
            em.wait("tensor", ("v", em.cnt["v"]))
        for j in range(2):
            for ch in range(2):
                bank = pout[2 * j + ch]
                for ht in range(12):
                    em.op("tensor", lambda e, j=j, ch=ch, ht=ht, bank=bank:
                          nc.tensor.matmul(
                              bank[:],
                              zBv[:, ht, j * 128:(j + 1) * 128],
                              ow[:, ht * D + ch * 512: ht * D + (ch + 1) * 512],
                              start=(ht == 0), stop=(ht == 11)))
        tt = em.mark("tensor", "t")
        em.wait("vector", tt)
        if i > 0:
            em.wait("vector", d_x1_out)
        for j in range(2):
            for ch in range(2):
                em.op("vector", lambda e, j=j, ch=ch: nc.vector.tensor_tensor(
                    x1bv[:, j, ch * 512:(ch + 1) * 512], pout[2 * j + ch][:],
                    x0cv[:, j, ch * 512:(ch + 1) * 512], OP.add))
        v_x1 = em.mark("vector", "v")
        v_stripe_done = v_x1

        em.wait("sync", v_x1)
        d_x1_out = em.dma("sync", [lambda e, i=i: e.dma_start(
            out=x1_h[i * SB:(i + 1) * SB, :].rearrange("(j p) d -> p j d", p=128),
            in_=x1bv[:])])

    em.barrier()
    em.drain_dma()
    esBp.close()
    esB.close()

    if stop_after == "B":
        em.check()
        return nc, em, es

    em.phase("C")
    SC = 512
    NSC = T // SC
    esC = ExitStack()
    wrow = sb(esC, "wrowC", [128, D])
    wqb = sb(esC, "wqbC", [128, D], BF16)
    wdT = sb(esC, "wdTC", [128, 8 * D], BF16)
    wdTv = wdT[:].rearrange("p (dt d) -> p dt d", dt=8)
    bhlC = sb(esC, "bhlC", [2, D], BF16)
    tmprow = sb(esC, "tmprowC", [1, D])
    x1c = sb(esC, "x1cC", [128, 4 * D])
    x1cv = x1c[:].rearrange("p (j d) -> p j d", j=4)
    xlnf = sb(esC, "xlnfC", [128, D])
    xqb = sb(esC, "xqbC", [128, D], BF16)
    xdT = sb(esC, "xdTC", [128, 8 * 128], BF16)
    xdTv = xdT[:].rearrange("p (dt t) -> p dt t", dt=8)
    x2b = sb(esC, "x2bC", [128, 4 * D])
    x2bv = x2b[:].rearrange("p (j d) -> p j d", j=4)
    qB1 = sb(esC, "qB1C", [128, D])
    qB2 = sb(esC, "qB2C", [128, D])
    qB3 = sb(esC, "qB3C", [128, D])
    qB4 = sb(esC, "qB4C", [128, D])
    qI = sb(esC, "qIC", [128, D], I32)
    qS = sb(esC, "qSC", [128, 96])

    esCp = ExitStack()
    ptrC = pst(esCp, "ptrC", [128, 1024], BF16)
    poutC = [pst(esCp, f"poutC{i}", [128, 512]) for i in range(4)]

    def emit_quantize(x_f32_ap, out_bf_ap):
        amax = qS[:, 0:32]; qsc = qS[:, 32:64]; qrs = qS[:, 64:96]
        xv = x_f32_ap.rearrange("p (a b) -> p a b", b=32)
        em.seq("vector", lambda e: nc.vector.tensor_reduce(
            amax, xv, AX.X, OP.max, apply_absolute_value=True))
        em.seq("vector", lambda e: nc.vector.tensor_scalar(
            qsc, amax, 1.0 / 7.5, 1e-30, OP.mult, OP.max))
        em.seq("vector", lambda e: nc.vector.reciprocal(qrs, qsc))
        em.seq("vector", lambda e: nc.vector.tensor_tensor(
            qB1[:].rearrange("p (a b) -> p a b", b=32), xv,
            qrs[:, :, None].broadcast_to([128, 32, 32]), OP.mult))
        def rhd(mult, dst):
            em.seq("vector", lambda e: nc.vector.tensor_scalar(
                qB4[:], qB1[:], float(mult), None, OP.mult))
            em.seq("vector", lambda e: nc.vector.tensor_copy(qI[:], qB4[:]))
            em.seq("vector", lambda e: nc.vector.tensor_copy(dst, qI[:]))
            em.seq("vector", lambda e: nc.vector.tensor_tensor(
                qB4[:], qB4[:], dst, OP.subtract))
            em.seq("vector", lambda e: nc.vector.scalar_tensor_tensor(
                dst, qB4[:], 0.5, dst, OP.is_gt, OP.add))
            em.seq("vector", lambda e: nc.vector.tensor_scalar(
                qB4[:], qB4[:], -0.5, None, OP.is_le))
            em.seq("vector", lambda e: nc.vector.tensor_tensor(
                dst, dst, qB4[:], OP.subtract))
            em.seq("vector", lambda e: nc.vector.tensor_scalar(
                dst, dst, 1.0 / mult, None, OP.mult))
        rhd(8.0, qB2[:])
        rhd(4.0, qB3[:])
        em.seq("vector", lambda e: nc.vector.tensor_tensor(qB4[:], qB1[:], qB1[:], OP.mult))
        em.seq("vector", lambda e: nc.vector.tensor_scalar(
            qB4[:], qB4[:], 4.0, None, OP.is_ge))
        em.seq("vector", lambda e: nc.vector.tensor_tensor(qB3[:], qB3[:], qB2[:], OP.subtract))
        em.seq("vector", lambda e: nc.vector.tensor_tensor(qB3[:], qB3[:], qB4[:], OP.mult))
        em.seq("vector", lambda e: nc.vector.tensor_tensor(qB2[:], qB2[:], qB3[:], OP.add))
        rhd(2.0, qB3[:])
        em.seq("vector", lambda e: nc.vector.tensor_tensor(qB4[:], qB1[:], qB1[:], OP.mult))
        em.seq("vector", lambda e: nc.vector.tensor_scalar(
            qB4[:], qB4[:], 16.0, None, OP.is_ge))
        em.seq("vector", lambda e: nc.vector.tensor_tensor(qB3[:], qB3[:], qB2[:], OP.subtract))
        em.seq("vector", lambda e: nc.vector.tensor_tensor(qB3[:], qB3[:], qB4[:], OP.mult))
        em.seq("vector", lambda e: nc.vector.tensor_tensor(qB2[:], qB2[:], qB3[:], OP.add))
        vq = em.op("vector", lambda e: nc.vector.tensor_tensor(
            qB1[:].rearrange("p (a b) -> p a b", b=32),
            qB2[:].rearrange("p (a b) -> p a b", b=32),
            qsc[:, :, None].broadcast_to([128, 32, 32]), OP.mult), inc=("v", 1))
        em.wait("scalar", vq)
        return em.op("scalar", lambda e: nc.scalar.copy(out_bf_ap, qB1[:]), inc=("s", 1))

    def emit_ln_f32(x_ap, out_f32_ap):
        sumx = stats[:, 0:1]; sumsq = stats[:, 1:2]; m = stats[:, 2:3]
        var = stats[:, 3:4]; r = stats[:, 4:5]; nmr = stats[:, 5:6]
        u = stats[:, 6:7]; mm = stats[:, 7:8]
        em.op("vector", lambda e: nc.vector.tensor_reduce(sumx, x_ap, AX.X, OP.add),
              inc=("v", 1))
        sv = em.op("scalar", lambda e: nc.scalar.activation(
            dump[:], x_ap, AF.Square, accum_out=sumsq), inc=("s", 1))
        em.wait("vector", sv)
        em.seq("vector", lambda e: nc.vector.tensor_scalar(m, sumx, 1.0 / D, None, OP.mult))
        em.seq("vector", lambda e: nc.vector.tensor_scalar(var, sumsq, 1.0 / D, None, OP.mult))
        em.seq("vector", lambda e: nc.vector.tensor_tensor(mm, m, m, OP.mult))
        vv = em.op("vector", lambda e: nc.vector.tensor_tensor(var, var, mm, OP.subtract),
                   inc=("v", 1))
        em.wait("scalar", vv)
        em.seq("scalar", lambda e: nc.scalar.activation(u, var, AF.Ln, bias=kc[:, 0:1]))
        sv2 = em.op("scalar", lambda e: nc.scalar.activation(r, u, AF.Exp, scale=-0.5),
                    inc=("s", 1))
        em.wait("vector", sv2)
        em.seq("vector", lambda e: nc.vector.scalar_tensor_tensor(
            nmr, m, -1.0, r, OP.mult, OP.mult))
        return em.op("vector", lambda e: nc.vector.tensor_scalar(
            out_f32_ap, x_ap, r, nmr, OP.mult, OP.add), inc=("v", 1))

    lorow = sb(esC, "lorowC", [1, D], BF16)

    def emit_bias_hilo(src_h, dst_bhl):
        d_b = em.dma("sync", [lambda e: e.dma_start(out=tmprow[:], in_=src_h[:])])
        em.wait("scalar", d_b)
        sh = em.op("scalar", lambda e: nc.scalar.copy(dst_bhl[0:1, :], tmprow[:]),
                   inc=("s", 1))
        em.wait("vector", sh)
        em.seq("vector", lambda e: nc.vector.tensor_copy(dump[0:1, :], dst_bhl[0:1, :]))
        vlo = em.op("vector", lambda e: nc.vector.tensor_tensor(
            lorow[:], tmprow[:], dump[0:1, :], OP.subtract), inc=("v", 1))
        em.wait("sync", vlo)
        return em.dma("sync", [lambda e: e.dma_start(out=dst_bhl[1:2, :], in_=lorow[:])])

    v_bC = emit_bias_hilo(ffn_b_h, bhlC)

    s_wq_evac = None
    s_q = None
    for rtile in range(8):
        em.wait("sync", s_q)
        d_w = em.dma("sync", [lambda e, rtile=rtile: e.dma_start(
            out=wrow[:], in_=ffn_w_h[rtile * 128:(rtile + 1) * 128, :])])
        em.wait("vector", d_w)
        s_q = emit_quantize(wrow[:], wqb[:])
        em.wait("tensor", s_q)
        em.wait("tensor", s_wq_evac)
        for dt in range(8):
            em.op("tensor", lambda e, dt=dt: nc.tensor.transpose(
                ptrC[:, dt * 128:(dt + 1) * 128],
                wqb[:, dt * 128:(dt + 1) * 128], ident[:]))
        tt = em.mark("tensor", "t")
        em.wait("scalar", tt)
        s_wq_evac = em.op("scalar", lambda e, rtile=rtile: nc.scalar.copy(
            wdTv[:, :, rtile * 128:(rtile + 1) * 128],
            ptrC[:].rearrange("p (dt t) -> p dt t", dt=8)), inc=("s", 1))

    em.wait("tensor", v_bC)
    em.wait("tensor", s_wq_evac)
    v_ln_doneC = None
    d_x2_out = None
    s_xdT = None
    for i in range(NSC):
        if i >= 1:
            em.wait("sync", v_ln_doneC)
        d_in = em.dma("sync", [lambda e, i=i: e.dma_start(
            out=x1cv[:], in_=x1_h[i * SC:(i + 1) * SC, :].rearrange(
                "(j p) d -> p j d", p=128))])
        em.wait("vector", d_in)
        em.wait("scalar", d_in)
        for j in range(4):
            emit_ln_f32(x1cv[:, j], xlnf[:])
            s_q = emit_quantize(xlnf[:], xqb[:])
            if "dbgXD" in dbg:
                em.wait("sync", s_q)
                em.dma("sync", [lambda e, i=i, j=j: e.dma_start(
                    out=dbgXD_h[i * SC + j * 128: i * SC + (j + 1) * 128, :],
                    in_=xqb[:])])
            em.wait("tensor", s_q)
            em.wait("tensor", s_xdT)
            for dt in range(8):
                em.op("tensor", lambda e, dt=dt: nc.tensor.transpose(
                    ptrC[:, dt * 128:(dt + 1) * 128],
                    xqb[:, dt * 128:(dt + 1) * 128], ident[:]))
            tt = em.mark("tensor", "t")
            em.wait("scalar", tt)
            s_xdT = em.op("scalar", lambda e: nc.scalar.copy(
                xdTv[:], ptrC[:].rearrange("p (dt t) -> p dt t", dt=8)), inc=("s", 1))
            em.wait("tensor", s_xdT)
            if i > 0 or j > 0:
                em.wait("tensor", ("v", em.cnt["v"]))
            for ch in range(2):
                bank = poutC[2 * (j % 2) + ch]
                for dt in range(8):
                    em.op("tensor", lambda e, dt=dt, ch=ch, bank=bank: nc.tensor.matmul(
                        bank[:], xdTv[:, dt],
                        wdTv[:, dt, ch * 512:(ch + 1) * 512],
                        start=(dt == 0), stop=False))
                em.op("tensor", lambda e, ch=ch, bank=bank: nc.tensor.matmul(
                    bank[:], ones2[:], bhlC[:, ch * 512:(ch + 1) * 512],
                    start=False, stop=True))
            tt2 = em.mark("tensor", "t")
            em.wait("vector", tt2)
            if i > 0 and j == 0:
                em.wait("vector", d_x2_out)
            for ch in range(2):
                em.op("vector", lambda e, j=j, ch=ch: nc.vector.tensor_tensor(
                    x2bv[:, j, ch * 512:(ch + 1) * 512], poutC[2 * (j % 2) + ch][:],
                    x1cv[:, j, ch * 512:(ch + 1) * 512], OP.add),
                    inc=("v", 1) if ch == 1 else None)
        v_ln_doneC = ("v", em.cnt["v"])
        em.wait("sync", v_ln_doneC)
        d_x2_out = em.dma("sync", [lambda e, i=i: e.dma_start(
            out=x2_h[i * SC:(i + 1) * SC, :].rearrange("(j p) d -> p j d", p=128),
            in_=x2bv[:])])

    em.barrier()
    em.drain_dma()
    esCp.close()
    esC.close()

    if stop_after == "C":
        em.check()
        return nc, em, es

    em.phase("D")
    CD = 512
    NCD = T // CD
    esD = ExitStack()
    wqkv = sb(esD, "wqkvD", [128, 8 * 3072], BF16)
    wqkvv = wqkv[:].rearrange("p (dt f) -> p dt f", dt=8)
    wpro = sb(esD, "wproD", [128, 8 * D], BF16)
    wprov = wpro[:].rearrange("p (dt d) -> p dt d", dt=8)
    bhlD = sb(esD, "bhlD", [2, D], BF16)
    tmprowD = sb(esD, "tmprowD", [1, D])
    x2r = sb(esD, "x2rD", [128, 2 * 4 * D])
    x2rv = x2r[:].rearrange("p (c j d) -> p c j d", c=2, j=4)
    xlnbD = sb(esD, "xlnbD", [128, D], BF16)
    xlnTD = sb(esD, "xlnTD", [128, 8 * CD], BF16)
    xlnTDv = xlnTD[:].rearrange("p (dt t) -> p dt t", dt=8)
    qF = sb(esD, "qFD", [128, 8 * 1024], BF16)
    qFv = qF[:].rearrange("p (f t) -> p f t", f=8)
    kF = sb(esD, "kFD", [128, 8 * 1024], BF16)
    kFv = kF[:].rearrange("p (f t) -> p f t", f=8)
    vT = sb(esD, "vTD", [128, 8 * 16 * 65], BF16)
    vTv = vT[:].rearrange("p (r h c) -> p r h c", r=8, h=16)
    expb = sb(esD, "expbD", [128, 2 * 384], BF16)
    aob = sb(esD, "aobD", [128, D])
    aobb = sb(esD, "aobbD", [128, D], BF16)
    aoT = sb(esD, "aoTD", [128, 8 * 128], BF16)
    aoTv = aoT[:].rearrange("p (dt t) -> p dt t", dt=8)
    x3b = sb(esD, "x3bD", [128, D])
    rcol = sb(esD, "rcolD", [128, 1])

    esDp = ExitStack()
    ptrD = pst(esDp, "ptrD", [128, 1024], BF16)
    pqk = [pst(esDp, f"pqkD{i}", [128, 512]) for i in range(2)]
    psc = [pst(esDp, f"pscD{i}", [128, 384]) for i in range(2)]
    pav = [pst(esDp, f"pavD{i}", [128, 65]) for i in range(2)]
    ppr = pst(esDp, "pprD", [128, 512])

    fns = []
    for dt in range(8):
        fns.append(lambda e, dt=dt: e.dma_start(
            out=wqkvv[:, dt], in_=w_qkv_h[dt * 128:(dt + 1) * 128, :]))
        fns.append(lambda e, dt=dt: e.dma_start(
            out=wprov[:, dt], in_=w_proj_h[dt * 128:(dt + 1) * 128, :]))
    dGw = em.dma("gpsimd", fns)
    lorowD = sb(esD, "lorowD", [1, D], BF16)
    d_b = em.dma("sync", [lambda e: e.dma_start(out=tmprowD[:], in_=b_proj_h[:])])
    em.wait("scalar", d_b)
    sh = em.op("scalar", lambda e: nc.scalar.copy(bhlD[0:1, :], tmprowD[:]), inc=("s", 1))
    em.wait("vector", sh)
    em.seq("vector", lambda e: nc.vector.tensor_copy(dump[0:1, :], bhlD[0:1, :]))
    vloD = em.op("vector", lambda e: nc.vector.tensor_tensor(
        lorowD[:], tmprowD[:], dump[0:1, :], OP.subtract), inc=("v", 1))
    em.wait("sync", vloD)
    v_bD = em.dma("sync", [lambda e: e.dma_start(out=bhlD[1:2, :], in_=lorowD[:])])
    v_ones = em.op("vector", lambda e: nc.vector.memset(vTv[:, :, :, 64:65], 1.0),
                   inc=("v", 1))
    em.wait("tensor", dGw)
    em.wait("tensor", v_bD)
    em.wait("tensor", v_ones)

    s_qk_evac = [None] * 2
    s_exp = [None] * 2
    v_attn_done = None
    s_attn_done = None
    d_x3_out = None
    v_ln_doneD = None
    s_ao_evac = None
    s_aocast = None

    def emit_attn_block(n):
        nonlocal v_attn_done, s_attn_done, d_x3_out, s_ao_evac, s_aocast
        rn = n % 8
        em.wait("vector", s_aocast)
        em.wait("vector", d_x3_out)
        for h in range(16):
            pb = psc[h % 2]
            eb = expb[:, (h % 2) * 384:(h % 2 + 1) * 384]
            ebv = expb[:].rearrange("p (b k q) -> p b k q", b=2, k=3)[:, h % 2]
            em.wait("tensor", s_exp[h % 2])
            for kb in range(3):
                nk = n - 1 + kb
                if nk < 0 or nk > 31:
                    continue
                rk = nk % 8
                hp = (h % 2) * 64
                ft = h // 2
                em.op("tensor", lambda e, kb=kb, rk=rk, hp=hp, ft=ft, pb=pb:
                      nc.tensor.matmul(
                          pb[:, kb * 128:(kb + 1) * 128],
                          kFv[hp:hp + 64, ft, rk * 128:(rk + 1) * 128],
                          qFv[hp:hp + 64, ft, rn * 128:(rn + 1) * 128],
                          start=True, stop=True))
            tt = em.mark("tensor", "t")
            em.wait("scalar", tt)
            lo = 128 if n == 0 else 0
            hi = 256 if n == 31 else 384
            s_e = em.op("scalar", lambda e, pb=pb, eb=eb, lo=lo, hi=hi:
                        nc.scalar.activation(
                eb[:, lo:hi], pb[:, lo:hi], AF.Exp, scale=0.125), inc=("s", 1))
            em.wait("vector", s_e)
            triv = trimask[:].rearrange("p (b q) -> p b q", b=2)
            if n == 0:
                em.seq("vector", lambda e, ebv=ebv: nc.vector.memset(ebv[:, 0], 0.0))
            else:
                em.seq("vector", lambda e, ebv=ebv, triv=triv: nc.vector.tensor_tensor(
                    ebv[:, 0], ebv[:, 0], triv[:, 0], OP.mult))
            if n == 31:
                vm = em.op("vector", lambda e, ebv=ebv: nc.vector.memset(ebv[:, 2], 0.0),
                           inc=("v", 1))
            else:
                vm = em.op("vector", lambda e, ebv=ebv, triv=triv: nc.vector.tensor_tensor(
                    ebv[:, 2], ebv[:, 2], triv[:, 1], OP.mult), inc=("v", 1))
            s_exp[h % 2] = vm
            em.wait("tensor", vm)
            pa = pav[h % 2]
            em.wait("tensor", v_attn_done)
            kbs = [kb for kb in range(3) if 0 <= n - 1 + kb <= 31]
            for kb in kbs:
                rk = (n - 1 + kb) % 8
                em.op("tensor", lambda e, kb=kb, rk=rk, h=h, pa=pa, ebv=ebv:
                      nc.tensor.matmul(
                          pa[:], ebv[:, kb],
                          vTv[:, rk, h, :],
                          start=(kb == kbs[0]), stop=(kb == kbs[-1])))
            tta = em.mark("tensor", "t")
            em.wait("vector", tta)
            em.seq("vector", lambda e, pa=pa: nc.vector.reciprocal(rcol[:], pa[:, 64:65]))
            v_attn_done = em.op("vector", lambda e, pa=pa, h=h: nc.vector.tensor_scalar(
                aob[:, h * 64:(h + 1) * 64], pa[:, 0:64], rcol[:], None, OP.mult),
                inc=("v", 1))
        em.wait("scalar", v_attn_done)
        s_c = em.op("scalar", lambda e: nc.scalar.copy(aobb[:], aob[:]), inc=("s", 1))
        s_aocast = s_c
        em.wait("tensor", s_c)
        em.wait("tensor", s_ao_evac)
        for dt in range(8):
            em.op("tensor", lambda e, dt=dt: nc.tensor.transpose(
                ptrD[:, dt * 128:(dt + 1) * 128],
                aobb[:, dt * 128:(dt + 1) * 128], ident[:]))
        tt = em.mark("tensor", "t")
        em.wait("scalar", tt)
        s_ao_evac = em.op("scalar", lambda e: nc.scalar.copy(
            aoTv[:], ptrD[:].rearrange("p (dt t) -> p dt t", dt=8)), inc=("s", 1))
        em.wait("tensor", s_ao_evac)
        em.wait("tensor", s_attn_done)
        for ch in range(2):
            for dt in range(8):
                em.op("tensor", lambda e, dt=dt, ch=ch: nc.tensor.matmul(
                    ppr[:], aoTv[:, dt],
                    wprov[:, dt, ch * 512:(ch + 1) * 512],
                    start=(dt == 0), stop=False))
            em.op("tensor", lambda e, ch=ch: nc.tensor.matmul(
                ppr[:], ones2[:], bhlD[:, ch * 512:(ch + 1) * 512],
                start=False, stop=True))
            tt = em.mark("tensor", "t")
            em.wait("vector", tt)
            cslot = (n // 4) % 2
            v_r = em.op("vector", lambda e, n=n, ch=ch, cslot=cslot:
                        nc.vector.tensor_tensor(
                            x3b[:, ch * 512:(ch + 1) * 512], ppr[:],
                            x2rv[:, cslot, n % 4, ch * 512:(ch + 1) * 512], OP.add),
                        inc=("v", 1))
            s_attn_done = v_r
            em.wait("tensor", v_r)
        em.wait("sync", s_attn_done)
        d_x3_out = em.dma("sync", [lambda e, n=n: e.dma_start(
            out=x3_h[n * 128:(n + 1) * 128, :], in_=x3b[:])])

    for c in range(NCD):
        cslot = c % 2
        if c >= 2:
            em.wait("sync", s_attn_done)
        d_in = em.dma("sync", [lambda e, c=c, cslot=cslot: e.dma_start(
            out=x2rv[:, cslot], in_=x2_h[c * CD:(c + 1) * CD, :].rearrange(
                "(j p) d -> p j d", p=128))])
        em.wait("vector", d_in)
        em.wait("scalar", d_in)
        s_evac = None
        for j in range(4):
            if not (c == 0 and j == 0):
                em.wait("scalar", ("t", em.cnt["t"]))
            s_apply = emit_ln(x2rv[:, cslot, j], xlnbD[:])
            em.wait("tensor", s_apply)
            em.wait("tensor", s_evac)
            for dt in range(8):
                em.op("tensor", lambda e, dt=dt: nc.tensor.transpose(
                    ptrD[:, dt * 128:(dt + 1) * 128],
                    xlnbD[:, dt * 128:(dt + 1) * 128], ident[:]))
            tt = em.mark("tensor", "t")
            em.wait("scalar", tt)
            s_evac = em.op("scalar", lambda e, j=j: nc.scalar.copy(
                xlnTDv[:, :, j * 128:(j + 1) * 128],
                ptrD[:].rearrange("p (dt t) -> p dt t", dt=8)), inc=("s", 1))
        v_ln_doneD = ("v", em.cnt["v"])
        em.wait("tensor", s_evac)

        rbase = (4 * c) % 8
        sqk = [None, None]
        for f in range(16):
            bank = pqk[f % 2]
            em.wait("tensor", s_qk_evac[f % 2])
            for dt in range(8):
                em.op("tensor", lambda e, f=f, dt=dt, bank=bank: nc.tensor.matmul(
                    bank[:], wqkvv[:, dt, f * 128:(f + 1) * 128],
                    xlnTDv[:, dt], start=(dt == 0), stop=(dt == 7)))
            tt = em.mark("tensor", "t")
            em.wait("scalar", tt)
            dstv = qFv if f < 8 else kFv
            fi = f % 8
            s_qk_evac[f % 2] = em.op("scalar", lambda e, dstv=dstv, fi=fi, bank=bank,
                                     rbase=rbase: nc.scalar.copy(
                dstv[:, fi, rbase * 128:(rbase + 4) * 128], bank[:]), inc=("s", 1))
        for j in range(4):
            for ch in range(2):
                bank = pqk[ch]
                em.wait("tensor", s_qk_evac[ch])
                for dt in range(8):
                    em.op("tensor", lambda e, j=j, ch=ch, dt=dt, bank=bank:
                          nc.tensor.matmul(
                              bank[:], xlnTDv[:, dt, j * 128:(j + 1) * 128],
                              wqkvv[:, dt, 2048 + ch * 512:2048 + (ch + 1) * 512],
                              start=(dt == 0), stop=(dt == 7)))
                tt = em.mark("tensor", "t")
                em.wait("scalar", tt)
                s_qk_evac[ch] = em.op("scalar", lambda e, j=j, ch=ch, bank=bank,
                                      rbase=rbase: nc.scalar.copy(
                    vTv[:, rbase + j, ch * 8:(ch + 1) * 8, 0:64],
                    bank[:].rearrange("p (h dv) -> p h dv", h=8)), inc=("s", 1))
        em.wait("tensor", s_qk_evac[0])
        em.wait("tensor", s_qk_evac[1])

        if c == 0:
            blocks = [0, 1, 2]
        elif c == NCD - 1:
            blocks = [4 * c - 1, 4 * c, 4 * c + 1, 4 * c + 2, 4 * c + 3]
        else:
            blocks = [4 * c - 1, 4 * c, 4 * c + 1, 4 * c + 2]
        for n in blocks:
            emit_attn_block(n)

    em.barrier()
    em.drain_dma()
    esDp.close()
    esD.close()

    if stop_after == "D":
        em.check()
        return nc, em, es

    em.phase("E")
    SE = 1024
    NSE = T // SE
    esE = ExitStack()
    x3r = sb(esE, "x3rE", [128, 2 * D])
    x3rv = x3r[:].rearrange("p (b d) -> p b d", b=2)
    xlnbE = sb(esE, "xlnbE", [128, D], BF16)
    xlnTE = sb(esE, "xlnTE", [128, 8 * SE], BF16)
    xlnTEv = xlnTE[:].rearrange("p (dt t) -> p dt t", dt=8)
    rwb = sb(esE, "rwbE", [128, 8 * 8], BF16)
    rwbv = rwb[:].rearrange("p (dt e) -> p dt e", dt=8)
    rbb = sb(esE, "rbbE", [128, 8])
    scb = sb(esE, "scbE", [128, 8])
    scpb = sb(esE, "scpbE", [128, 8])
    gsb = sb(esE, "gsbE", [128, 4])
    rtmp = sb(esE, "rtmpE", [128, 24])
    cwb = sb(esE, "cwbE", [128, 8 * 8])
    cwbv = cwb[:].rearrange("p (j e) -> p j e", j=8)
    wgE = [sb(esE, f"wgE{i}", [128, 8 * 512], BF16) for i in range(4)]
    wuE = [sb(esE, f"wuE{i}", [128, 8 * 512], BF16) for i in range(4)]
    wdE = [sb(esE, f"wdE{i}", [128, 4 * D], BF16) for i in range(4)]
    gb = [sb(esE, f"gbE{i}", [128, 512]) for i in range(2)]
    ub = [sb(esE, f"ubE{i}", [128, 512]) for i in range(2)]
    gub = [sb(esE, f"gubE{i}", [128, 512], BF16) for i in range(2)]
    guT = [sb(esE, f"guTE{i}", [128, 4 * 128], BF16) for i in range(2)]
    acc = sb(esE, "accE", [128, 8 * D])
    accv = acc[:].rearrange("p (j d) -> p j d", j=8)

    esEp = ExitStack()
    ptrE = pst(esEp, "ptrE", [128, 512], BF16)
    pgE = [pst(esEp, f"pgE{i}", [128, 512]) for i in range(2)]
    puE = [pst(esEp, f"puE{i}", [128, 512]) for i in range(2)]
    pwd = [pst(esEp, f"pwdE{i}", [128, 512]) for i in range(2)]
    prt = pwd[0]

    d_rb = em.dma("sync", [lambda e: e.dma_start(out=rbb[:], in_=router_bb_h[:])])
    fns = []
    for dt in range(8):
        fns.append(lambda e, dt=dt: e.dma_start(
            out=rwbv[:, dt], in_=router_wT_h[dt * 128:(dt + 1) * 128, :]))
    d_rw = em.dma("gpsimd", fns)
    em.wait("tensor", d_rb)
    em.wait("vector", d_rb)
    em.wait("tensor", d_rw)

    def wg_src(e_, slot):
        if e_ < 8:
            return [lambda e, e_=e_, slot=slot: e.dma_start(
                out=wgE[slot][:].rearrange("p (dt f) -> p dt f", dt=8),
                in_=wgT_h[e_, :, :].rearrange("(dt p) f -> p dt f", p=128))]
        half = e_ - 8
        return [lambda e, half=half, slot=slot: e.dma_start(
            out=wgE[slot][:].rearrange("p (dt f) -> p dt f", dt=8),
            in_=sgT_h[:, half * 512:(half + 1) * 512]
            .rearrange("(dt p) f -> p dt f", p=128))]

    def wu_src(e_, slot):
        if e_ < 8:
            return [lambda e, e_=e_, slot=slot: e.dma_start(
                out=wuE[slot][:].rearrange("p (dt f) -> p dt f", dt=8),
                in_=wuT_h[e_, :, :].rearrange("(dt p) f -> p dt f", p=128))]
        half = e_ - 8
        return [lambda e, half=half, slot=slot: e.dma_start(
            out=wuE[slot][:].rearrange("p (dt f) -> p dt f", dt=8),
            in_=suT_h[:, half * 512:(half + 1) * 512]
            .rearrange("(dt p) f -> p dt f", p=128))]

    def wd_src(e_, slot):
        if e_ < 8:
            return [lambda e, e_=e_, slot=slot: e.dma_start(
                out=wdE[slot][:].rearrange("p (it d) -> p it d", it=4),
                in_=wdT_h[e_, :, :].rearrange("(it p) d -> p it d", p=128))]
        half = e_ - 8
        return [lambda e, half=half, slot=slot: e.dma_start(
            out=wdE[slot][:].rearrange("p (it d) -> p it d", it=4),
            in_=sdT_h[half * 512:(half + 1) * 512, :]
            .rearrange("(it p) d -> p it d", p=128))]

    def pair_dma(gp):
        p_ = gp % 5
        base = (gp % 2) * 2
        fns = (wg_src(2 * p_, base) + wu_src(2 * p_, base) + wd_src(2 * p_, base)
               + wg_src(2 * p_ + 1, base + 1) + wu_src(2 * p_ + 1, base + 1)
               + wd_src(2 * p_ + 1, base + 1))
        return em.dma("gpsimd", fns)

    NPAIR_TOT = 5 * NSE
    pair_w_tgt = [None] * (NPAIR_TOT + 1)
    t_pair_done = [None] * NPAIR_TOT
    pair_w_tgt[0] = pair_dma(0)

    v_stripe_doneE = None
    d_y_out = None
    s_guT = [None, None]
    v_acc = None
    s_g_prev = [None, None]
    v_u_prev = [None, None]
    d_x3t = [None, None]
    v_ln_read = [None, None]
    for i in range(NSE):
        if i >= 1:
            em.wait("sync", v_stripe_doneE)
        s_evac = s_guT[1]
        s_sc_prev = None
        for j in range(8):
            em.wait("sync", v_ln_read[j % 2])
            d_x3t[j % 2] = em.dma("sync", [lambda e, i=i, j=j: e.dma_start(
                out=x3rv[:, j % 2], in_=x3_h[i * SE + j * 128:i * SE + (j + 1) * 128, :])])
            em.wait("vector", d_x3t[j % 2])
            em.wait("scalar", d_x3t[j % 2])
            if not (i == 0 and j == 0):
                em.wait("scalar", ("t", em.cnt["t"]))
            s_apply = emit_ln(x3rv[:, j % 2], xlnbE[:])
            v_ln_read[j % 2] = ("v", em.cnt["v"])
            em.wait("tensor", s_apply)
            em.wait("tensor", s_evac)
            for half in range(2):
                for q4 in range(4):
                    dt = half * 4 + q4
                    em.op("tensor", lambda e, dt=dt, q4=q4: nc.tensor.transpose(
                        ptrE[:, q4 * 128:(q4 + 1) * 128],
                        xlnbE[:, dt * 128:(dt + 1) * 128], ident[:]))
                tt = em.mark("tensor", "t")
                em.wait("scalar", tt)
                s_evac = em.op("scalar", lambda e, j=j, half=half: nc.scalar.copy(
                    xlnTEv[:, half * 4:half * 4 + 4, j * 128:(j + 1) * 128],
                    ptrE[:].rearrange("p (dt t) -> p dt t", dt=4)), inc=("s", 1))
                if half == 0:
                    em.wait("tensor", s_evac)
        s_lnT_done = s_evac
        em.wait("tensor", s_lnT_done)
        em.wait("tensor", v_acc)
        v_route_prev = None
        for j in range(8):
            em.wait("tensor", s_sc_prev)
            for dt in range(8):
                em.op("tensor", lambda e, j=j, dt=dt: nc.tensor.matmul(
                    prt[:, 0:8], xlnTEv[:, dt, j * 128:(j + 1) * 128],
                    rwbv[:, dt], start=(dt == 0), stop=(dt == 7)))
            tt = em.mark("tensor", "t")
            em.wait("scalar", tt)
            em.wait("scalar", v_route_prev)
            s_sc = em.op("scalar", lambda e: nc.scalar.activation(
                scb[:], prt[:, 0:8], AF.Sigmoid), inc=("s", 1))
            s_sc_prev = s_sc
            em.wait("vector", s_sc)
            em.seq("vector", lambda e: nc.vector.tensor_tensor(
                scpb[:], scb[:], rbb[:], OP.add))
            em.seq("vector", lambda e: nc.vector.tensor_tensor(
                gsb[:], scpb[:].rearrange("p (g two) -> p g two", two=2)[:, :, 0],
                scpb[:].rearrange("p (g two) -> p g two", two=2)[:, :, 1], OP.add))
            w01 = rtmp[:, 0:1]; w02 = rtmp[:, 1:2]; w03 = rtmp[:, 2:3]
            w12 = rtmp[:, 3:4]; w13 = rtmp[:, 4:5]; w23 = rtmp[:, 5:6]
            pairs = [(0, 1, w01), (0, 2, w02), (0, 3, w03),
                     (1, 2, w12), (1, 3, w13), (2, 3, w23)]
            for a_, b_, wv in pairs:
                em.seq("vector", lambda e, a_=a_, b_=b_, wv=wv: nc.vector.tensor_tensor(
                    wv, gsb[:, a_:a_ + 1], gsb[:, b_:b_ + 1], OP.is_ge))
            wins = rtmp[:, 8:12]
            em.seq("vector", lambda e: nc.vector.tensor_tensor(
                wins[:, 0:1], w01, w02, OP.add))
            em.seq("vector", lambda e: nc.vector.tensor_tensor(
                wins[:, 0:1], wins[:, 0:1], w03, OP.add))
            em.seq("vector", lambda e: nc.vector.scalar_tensor_tensor(
                wins[:, 1:2], w01, -1.0, w12, OP.mult, OP.add))
            em.seq("vector", lambda e: nc.vector.tensor_tensor(
                wins[:, 1:2], wins[:, 1:2], w13, OP.add))
            em.seq("vector", lambda e: nc.vector.tensor_scalar(
                wins[:, 1:2], wins[:, 1:2], 1.0, None, OP.add))
            em.seq("vector", lambda e: nc.vector.scalar_tensor_tensor(
                wins[:, 2:3], w02, -1.0, w23, OP.mult, OP.add))
            em.seq("vector", lambda e: nc.vector.scalar_tensor_tensor(
                wins[:, 2:3], w12, -1.0, wins[:, 2:3], OP.mult, OP.add))
            em.seq("vector", lambda e: nc.vector.tensor_scalar(
                wins[:, 2:3], wins[:, 2:3], 2.0, None, OP.add))
            em.seq("vector", lambda e: nc.vector.tensor_tensor(
                wins[:, 3:4], w03, w13, OP.add))
            em.seq("vector", lambda e: nc.vector.tensor_tensor(
                wins[:, 3:4], wins[:, 3:4], w23, OP.add))
            em.seq("vector", lambda e: nc.vector.tensor_scalar(
                wins[:, 3:4], wins[:, 3:4], -1.0, 3.0, OP.mult, OP.add))
            gmask = rtmp[:, 12:16]
            em.seq("vector", lambda e: nc.vector.tensor_scalar(
                gmask[:], wins[:], 2.0, None, OP.is_ge))
            smask = rtmp[:, 16:24]
            em.seq("vector", lambda e: nc.vector.tensor_copy(
                smask[:].rearrange("p (g two) -> p g two", two=2)[:, :, 0], gmask[:]))
            em.seq("vector", lambda e: nc.vector.tensor_copy(
                smask[:].rearrange("p (g two) -> p g two", two=2)[:, :, 1], gmask[:]))
            masked = rtmp[:, 0:8]
            em.seq("vector", lambda e: nc.vector.tensor_tensor(
                masked[:], scpb[:], smask[:], OP.mult))
            m1 = rtmp[:, 8:9]; m2 = rtmp[:, 9:10]; den = rtmp[:, 10:11]
            sel = cwbv[:, j]
            em.seq("vector", lambda e: nc.vector.tensor_reduce(m1, masked[:], AX.X, OP.max))
            em.seq("vector", lambda e, sel=sel: nc.vector.tensor_scalar(
                sel, masked[:], m1, None, OP.is_ge))
            em.seq("vector", lambda e, sel=sel: nc.vector.tensor_scalar(
                rtmp[:, 16:24], sel, m1, None, OP.mult))
            em.seq("vector", lambda e: nc.vector.tensor_tensor(
                masked[:], masked[:], rtmp[:, 16:24], OP.subtract))
            em.seq("vector", lambda e: nc.vector.tensor_reduce(m2, masked[:], AX.X, OP.max))
            em.seq("vector", lambda e: nc.vector.tensor_scalar(
                rtmp[:, 16:24], masked[:], m2, None, OP.is_ge))
            em.seq("vector", lambda e, sel=sel: nc.vector.tensor_tensor(
                sel, sel, rtmp[:, 16:24], OP.add))
            em.seq("vector", lambda e, sel=sel: nc.vector.tensor_tensor(
                sel, sel, scb[:], OP.mult))
            em.seq("vector", lambda e, sel=sel: nc.vector.tensor_reduce(
                den, sel, AX.X, OP.add))
            em.seq("vector", lambda e: nc.vector.tensor_scalar(
                den, den, 1e-20, None, OP.add))
            em.seq("vector", lambda e: nc.vector.reciprocal(den, den))
            em.op("vector", lambda e, sel=sel: nc.vector.tensor_scalar(
                sel, sel, den, None, OP.mult), inc=("v", 1))
            v_route_prev = ("v", em.cnt["v"])
        v_route_done = ("v", em.cnt["v"])
        if i > 0:
            em.wait("vector", d_y_out)

        for p_ in range(5):
            gp = i * 5 + p_
            if gp + 1 < NPAIR_TOT:
                if gp >= 1:
                    em.wait("gpsimd", t_pair_done[gp - 1])
                pair_w_tgt[gp + 1] = pair_dma(gp + 1)
            em.wait("tensor", pair_w_tgt[gp])
            em.wait("tensor", v_route_done)
            em.wait("tensor", s_lnT_done)
            base = (gp % 2) * 2
            for j in range(8):
                s_gub_t = [None, None]
                for ei in range(2):
                    ex = 2 * p_ + ei
                    slot = base + ei
                    em.wait("tensor", s_g_prev[ei])
                    for dt in range(8):
                        em.op("tensor", lambda e, j=j, dt=dt, slot=slot, ei=ei:
                              nc.tensor.matmul(
                                  pgE[ei][:], xlnTEv[:, dt, j * 128:(j + 1) * 128],
                                  wgE[slot][:, dt * 512:(dt + 1) * 512],
                                  start=(dt == 0), stop=(dt == 7)))
                    ttg = em.mark("tensor", "t")
                    em.wait("tensor", v_u_prev[ei])
                    for dt in range(8):
                        em.op("tensor", lambda e, j=j, dt=dt, slot=slot, ei=ei:
                              nc.tensor.matmul(
                                  puE[ei][:], xlnTEv[:, dt, j * 128:(j + 1) * 128],
                                  wuE[slot][:, dt * 512:(dt + 1) * 512],
                                  start=(dt == 0), stop=(dt == 7)))
                    ttu = em.mark("tensor", "t")
                    em.wait("scalar", ttg)
                    s_g = em.op("scalar", lambda e, ei=ei: nc.scalar.activation(
                        gb[ei][:], pgE[ei][:], AF.Silu), inc=("s", 1))
                    s_g_prev[ei] = s_g
                    em.wait("vector", ttu)
                    em.wait("vector", s_g)
                    v_u_prev[ei] = em.seq("vector", lambda e, ei=ei: nc.vector.tensor_copy(
                        ub[ei][:], puE[ei][:]))
                    vgu = em.op("vector", lambda e, ei=ei: nc.vector.tensor_tensor(
                        ub[ei][:], ub[ei][:], gb[ei][:], OP.mult), inc=("v", 1))
                    if ex < 8:
                        vgu = em.op("vector", lambda e, j=j, ex=ex, ei=ei:
                                    nc.vector.tensor_scalar(
                            ub[ei][:], ub[ei][:], cwbv[:, j, ex:ex + 1], None, OP.mult),
                            inc=("v", 1))
                    em.wait("scalar", vgu)
                    s_gub_t[ei] = em.op("scalar", lambda e, ei=ei: nc.scalar.copy(
                        gub[ei][:], ub[ei][:]), inc=("s", 1))
                em.wait("tensor", v_acc)
                for ei in range(2):
                    slot = base + ei
                    em.wait("tensor", s_gub_t[ei])
                    em.wait("tensor", s_guT[1 - ei])
                    for it in range(4):
                        em.op("tensor", lambda e, it=it, ei=ei: nc.tensor.transpose(
                            ptrE[:, it * 128:(it + 1) * 128],
                            gub[ei][:, it * 128:(it + 1) * 128], ident[:]))
                    ttt = em.mark("tensor", "t")
                    em.wait("scalar", ttt)
                    s_guT[ei] = em.op("scalar", lambda e, ei=ei: nc.scalar.copy(
                        guT[ei][:], ptrE[:, 0:512]), inc=("s", 1))
                    em.wait("tensor", s_guT[ei])
                    for ch in range(2):
                        for it in range(4):
                            em.op("tensor", lambda e, ei=ei, it=it, ch=ch, slot=slot:
                                  nc.tensor.matmul(
                                pwd[ch][:], guT[ei][:, it * 128:(it + 1) * 128],
                                wdE[slot][:, it * D + ch * 512: it * D + (ch + 1) * 512],
                                start=(ei == 0 and it == 0), stop=(ei == 1 and it == 3)))
                ttw = em.mark("tensor", "t")
                em.wait("vector", ttw)
                for ch in range(2):
                    if p_ == 0:
                        em.op("vector", lambda e, j=j, ch=ch: nc.vector.tensor_copy(
                            accv[:, j, ch * 512:(ch + 1) * 512], pwd[ch][:]),
                            inc=("v", 1) if ch == 1 else None)
                    else:
                        em.op("vector", lambda e, j=j, ch=ch: nc.vector.tensor_tensor(
                            accv[:, j, ch * 512:(ch + 1) * 512], pwd[ch][:],
                            accv[:, j, ch * 512:(ch + 1) * 512], OP.add),
                            inc=("v", 1) if ch == 1 else None)
                v_acc = ("v", em.cnt["v"])
            t_pair_done[gp] = ("t", em.cnt["t"])
        v_res = [None, None]
        for j in range(8):
            em.wait("sync", v_res[j % 2] if j >= 2 else v_ln_read[j % 2])
            d_x3t[j % 2] = em.dma("sync", [lambda e, i=i, j=j: e.dma_start(
                out=x3rv[:, j % 2], in_=x3_h[i * SE + j * 128:i * SE + (j + 1) * 128, :])])
            em.wait("vector", d_x3t[j % 2])
            v_res[j % 2] = em.seq("vector", lambda e, j=j: nc.vector.tensor_tensor(
                accv[:, j], accv[:, j], x3rv[:, j % 2], OP.add))
        v_ln_read = [None, None]
        v_stripe_doneE = em.mark("vector", "v")
        em.wait("sync", v_stripe_doneE)
        d_y_out = em.dma("sync", [lambda e, i=i: e.dma_start(
            out=y_h[i * SE:(i + 1) * SE, :].rearrange("(j p) d -> p j d", p=128),
            in_=accv[:])])

    em.barrier()
    em.drain_dma()
    esEp.close()
    esE.close()

    em.check()
    return nc, em, es


def _run_thunks(nc, em, eng, e):
    marks = getattr(em, "phase_marks", [])
    bounds = [(nm, idx[eng]) for nm, idx in marks] + [("end", len(em.prog[eng]))]
    pos = 0
    for k in range(len(bounds) - 1):
        nm, start = bounds[k]
        stop = bounds[k + 1][1]
        for fn in em.prog[eng][pos:start]:
            fn(e)
        pos = start
        with nc.named_scope(f"ph{nm}"):
            for fn in em.prog[eng][start:stop]:
                fn(e)
        pos = stop
    for fn in em.prog[eng][pos:]:
        fn(e)


def _finish(nc, em, es):
    for h in em.sem.values():
        nc.gpsimd.sem_clear(h)
    nc.all_engine_barrier()
    with nc.Block() as block:
        @block.sync
        def _(sync):
            _run_thunks(nc, em, "sync", sync)

        @block.gpsimd
        def _(g):
            _run_thunks(nc, em, "gpsimd", g)

        @block.tensor
        def _(te):
            _run_thunks(nc, em, "tensor", te)

        @block.scalar
        def _(sc):
            _run_thunks(nc, em, "scalar", sc)

        @block.vector
        def _(v):
            _run_thunks(nc, em, "vector", v)
    es.close()
    return nc


def _run(inputs, stop_after="E", dbg=(), cores=None):
    g = _host_prep(inputs)
    nc, em, es = build_program(stop_after=stop_after, dbg=dbg)
    nc = _finish(nc, em, es)
    cores = cores if cores is not None else list(range(NCORES))
    in_maps = []
    for c in cores:
        m = {"x0": g["x0"][c]}
        for k in INPUT_NAMES:
            m[k] = g[k]
        in_maps.append(m)
    res = run_bass_kernel_spmd(nc, in_maps, core_ids=list(range(len(cores))))
    return res


def kernel(**inputs):
    res = _run(inputs, stop_after="E")
    return np.stack([np.asarray(r["y"], dtype=np.float32) for r in res.results])
